# version 16
# baseline (speedup 1.0000x reference)
import numpy as np
import ml_dtypes
from contextlib import ExitStack

import concourse.bass as bass
import concourse.mybir as mybir
from concourse.bass_utils import run_bass_kernel_spmd

F32 = mybir.dt.float32
BF16 = mybir.dt.bfloat16
AF = mybir.ActivationFunctionType
ALU = mybir.AluOpType
AX = mybir.AxisListType

S = 2048
D = 1024
NT = S // 128
NC_ = D // 128
NEG = -1.0e30

ENGINES = ("pe", "dve", "act", "pool", "sp")
DMA_CH = ("qsp", "qpool", "qact")
DMA_SLOTS = {"qsp": 8, "qpool": 8, "qact": 8}
DMA_SLOT_NAMES = tuple(f"{q}_{i}" for q in DMA_CH for i in range(DMA_SLOTS[q]))
ALL_CH = ENGINES + DMA_SLOT_NAMES + ("cc",)
SBUF_LO = 16512
SBUF_HI = 229312
X_FIXED = 127296


def host_constants():
    c = {}
    c["identf"] = np.eye(128, dtype=np.float32)
    c["identb"] = np.eye(128, dtype=np.float32)
    t = np.arange(S, dtype=np.float64)

    dk = 64
    inv = 1.0 / (10000.0 ** (np.arange(0, dk, 2, dtype=np.float64) / dk))
    ang = np.outer(t, inv)
    cos, sin = np.cos(ang), np.sin(ang)
    cf = np.zeros((64, S)); ssw = np.zeros((64, S))
    cf[0::2] = cos.T; cf[1::2] = cos.T
    ssw[0::2] = sin.T; ssw[1::2] = -sin.T
    c["l0_cos"] = np.tile(cf, (2, 1)).astype(np.float32)
    c["l0_sinsw"] = np.tile(ssw, (2, 1)).astype(np.float32)

    ghd = 32
    ginv = 1.0 / (10000.0 ** (np.arange(0, ghd, 2, dtype=np.float64) / ghd))
    gang = np.outer(t, ginv)
    gcos, gsin = np.cos(gang), np.sin(gang)
    hc = np.zeros((64, S)); hs = np.zeros((64, S))
    hc[0:32] = 1.0
    hc[32:48] = gcos.T; hc[48:64] = gcos.T
    hs[32:48] = gsin.T; hs[48:64] = -gsin.T
    c["l1a_cos"] = np.tile(hc, (2, 1)).astype(np.float32)
    c["l1a_sinsw"] = np.tile(hs, (2, 1)).astype(np.float32)

    dh = 64
    half = np.arange(0, dh, 2, dtype=np.float64)
    xfreq = np.outer(t, 1.0 / (10000.0 ** (half / dh)))
    xf2 = np.concatenate([xfreq, xfreq], axis=1)
    base = (half + 0.4 * dh) / (1.4 * dh)
    power = (t - S // 2) / 256.0
    scl = base[None, :] ** power[:, None]
    scl2 = np.concatenate([scl, scl], axis=1)
    xc, xs_ = np.cos(xf2), np.sin(xf2)
    qc = xc * scl2 * 8.0
    qs_ = np.concatenate([-(xs_[:, :32] * scl), (xs_[:, 32:] * scl)], axis=1) * 8.0
    kc = xc / scl2
    ks_ = np.concatenate([-(xs_[:, :32] / scl), (xs_[:, 32:] / scl)], axis=1)
    for nm, tab in (("xq_c", qc), ("xq_s", qs_), ("xk_c", kc), ("xk_s", ks_)):
        m = np.zeros((128, NT * dh), dtype=np.float32)
        for i in range(NT):
            m[:, i * dh:(i + 1) * dh] = tab[i * 128:(i + 1) * 128]
        c[nm] = m

    j = np.arange(-512, 512)[None, :]
    k = np.arange(128)[:, None]
    c["band"] = np.where(k <= j, 0.0, NEG).astype(np.float32)
    c["band1"] = np.where(k <= j + 1, 0.0, NEG).astype(np.float32)
    c["band_prev"] = np.where(k >= j, 0.0, NEG).astype(np.float32)
    c["padcol"] = np.concatenate(
        [np.zeros((1, 1)), np.full((127, 1), NEG)]).astype(np.float32)
    c["padcol64"] = np.tile(c["padcol"], (1, 64))
    c["blockdiag2"] = np.zeros((128, 128), dtype=np.float32)
    c["blockdiag2"][0:64, 0] = 1.0
    c["blockdiag2"][64:128, 1] = 1.0
    c["ones_1x128"] = np.ones((1, 128), dtype=np.float32)
    c["ones128"] = np.ones((128, 128), dtype=np.float32)
    c["neg_ones_1x128"] = np.full((1, 128), -1.0, dtype=np.float32)
    return c


class Buf:
    __slots__ = ("name", "t", "off", "nbytes", "ghosts")
    def __init__(self, name, t, off, nbytes, ghosts):
        self.name, self.t, self.off, self.nbytes, self.ghosts = name, t, off, nbytes, ghosts
    def __getitem__(self, idx):
        return self.t[idx]
    @property
    def ap(self):
        return self.t[:]


class Prog:
    def __init__(self, nc):
        self.nc = nc
        self.es = ExitStack()
        self.ops = {e: [] for e in ENGINES}
        self.sem = {n: self.es.enter_context(nc.semaphore("sem_" + n)) for n in ALL_CH}
        self.cnt = {n: 0 for n in ALL_CH}
        self.dma_rr = {q: 0 for q in DMA_CH}
        self.clockat = {}
        self.eng_clock = {e: {} for e in ENGINES}
        self.last_w = {}
        self.readers = {}
        self.free = [(SBUF_LO, SBUF_HI)]
        self.ghost = []
        self.high_water = 0
        self.bufs = {}
        self.PS = self.es.enter_context(nc.psum_tensor("PS", [128, 4096], F32))
        self.n_ops = 0
        self.n_waits = 0

    def alloc(self, name, shape, dtype, align=64, side="low", fixed_off=None):
        assert name not in self.bufs, name
        itemsize = 2 if dtype == BF16 else 4
        per_part = int(np.prod(shape[1:])) * itemsize
        per_part = -(-per_part // align) * align
        cands = []
        for i, (a, b) in enumerate(self.free):
            if fixed_off is not None:
                if a <= fixed_off and fixed_off + per_part <= b:
                    cands.append((0, i, fixed_off))
            elif side == "high":
                aa = ((b - per_part) // align) * align
                if aa >= a:
                    cands.append((-b, i, aa))
            else:
                aa = -(-a // align) * align
                if b - aa >= per_part:
                    key = -(b - aa) if per_part >= 24576 else (b - aa)
                    cands.append((key, i, aa))
        cands.sort()
        for _, i, aa in cands[:1]:
            a, b = self.free[i]
            if True:
                del self.free[i]
                if aa > a:
                    self.free.insert(i, (a, aa)); i += 1
                if aa + per_part < b:
                    self.free.insert(i, (aa + per_part, b))
                ghosts = []
                new_ghost = []
                for (s0, e0, kl) in self.ghost:
                    if s0 < aa + per_part and e0 > aa:
                        ghosts.extend(kl)
                        if s0 < aa:
                            new_ghost.append((s0, aa, kl))
                        if e0 > aa + per_part:
                            new_ghost.append((aa + per_part, e0, kl))
                    else:
                        new_ghost.append((s0, e0, kl))
                self.ghost = new_ghost
                t = self.nc.alloc_sbuf_tensor_at(name, list(shape), dtype, offset=aa)
                b_ = Buf(name, t, aa, per_part, ghosts)
                self.bufs[name] = b_
                self.high_water = max(self.high_water, self.used())
                return b_
        raise MemoryError(f"SBUF alloc fail {name} ({per_part}B); free={self.free}")

    def used(self):
        return (SBUF_HI - SBUF_LO) - sum(b - a for a, b in self.free)

    def freebuf(self, buf):
        if isinstance(buf, str):
            buf = self.bufs[buf]
        del self.bufs[buf.name]
        pref = buf.name + "#"
        keys = [k for k in (set(self.last_w) | set(self.readers))
                if k == buf.name or k.startswith(pref)]
        self.ghost.append((buf.off, buf.off + buf.nbytes, keys))
        self.free.append((buf.off, buf.off + buf.nbytes))
        self.free.sort()
        merged = []
        for a, b in self.free:
            if merged and a <= merged[-1][1]:
                merged[-1] = (merged[-1][0], max(merged[-1][1], b))
            else:
                merged.append((a, b))
        self.free = merged

    def _wait(self, eng, pts):
        need = {}
        ec = self.eng_clock[eng]
        for ch, v in pts:
            if ec.get(ch, 0) < v:
                need[ch] = max(need.get(ch, 0), v)
        for ch, v in sorted(need.items()):
            self.ops[eng].append(("wait", ch, v))
            self.n_waits += 1
            ec[ch] = max(ec.get(ch, 0), v)
            for c2, v2 in self.clockat.get((ch, v), {}).items():
                ec[c2] = max(ec.get(c2, 0), v2)

    def emit(self, eng, fn, reads=(), writes=(), inc=True, ch=None, wover=()):
        ch = ch or eng
        slot_gate = None
        if ch in DMA_CH:
            q = ch
            ch = f"{q}_{self.dma_rr[q] % DMA_SLOTS[q]}"
            self.dma_rr[q] += 1
            if self.cnt[ch] > 0:
                slot_gate = (ch, self.cnt[ch])
        step = 16 if ch.startswith("q") else 1
        reads = [r for r in reads if r]
        writes = [w for w in writes if w]
        wover = [w for w in wover if w]
        ghost_extra = []
        for w in writes + wover:
            b = self.bufs.get(w.split("#")[0])
            if b and b.ghosts:
                ghost_extra.extend(g for g in b.ghosts
                                   if g in self.last_w or g in self.readers)
        pts = []
        for kk in reads:
            w = self.last_w.get(kk)
            if w:
                pts.append(w)
        for kk in writes + ghost_extra:
            w = self.last_w.get(kk)
            if w:
                pts.append(w)
            pts.extend(self.readers.get(kk, ()))
        if slot_gate is not None:
            pts.append(slot_gate)
        self._wait(eng, pts)
        pub = self.cnt[ch] + step
        if inc:
            self.cnt[ch] = pub
            self.clockat[(ch, pub)] = dict(self.eng_clock[eng])
        self.ops[eng].append(("op", fn, self.sem[ch] if inc else None, step))
        for kk in writes + wover:
            self.last_w[kk] = (ch, pub)
            self.readers[kk] = []
        for kk in reads:
            self.readers.setdefault(kk, []).append((ch, pub))
        self.n_ops += 1

    def hch(self):
        self.hflip = getattr(self, "hflip", 0) ^ 1
        return ("sp", "qsp") if self.hflip else ("act", "qact")

    def wait_all(self, eng):
        self._wait(eng, [(ch, self.cnt[ch]) for ch in ALL_CH if self.cnt[ch] > 0])

    def finalize(self):
        with self.nc.Block() as block:
            attr = {"pe": "tensor", "dve": "vector", "act": "scalar",
                    "pool": "gpsimd", "sp": "sync"}
            for eng in ENGINES:
                ops = self.ops[eng]
                sems = self.sem
                def body(e, ops=ops, sems=sems):
                    for item in ops:
                        if item[0] == "wait":
                            e.wait_ge(sems[item[1]], item[2])
                        else:
                            _, fn, sem_, step = item
                            ins = fn(e)
                            if sem_ is not None:
                                ins.then_inc(sem_, step)
                getattr(block, attr[eng])(body)


class MK:
    def __init__(self, nc, io):
        self.nc = nc
        self.io = io
        self.p = Prog(nc)
        self.consts_np = host_constants()
        self.cbuf = {}
        self._uid = 0
        self.evac_flip = 0
        p = self.p
        self.eps = {}
        self.X = p.alloc("X", [128, NT * D], F32, fixed_off=X_FIXED)
        self.spill = {}
        self.stat = p.alloc("stat", [128, 32], F32, side="high")
        self.bb = p.alloc("bb", [128, 24], F32, side="high")
        self.brow = p.alloc("brow", [1, 80], F32, side="high")
        self.biasb = p.alloc("biasb", [128, 4], F32, side="high")
        self.biasball = p.alloc("biasball", [128, 128], F32, side="high")
        self.rA = p.alloc("ropeA", [128, S], F32, side="high")
        self.rB = p.alloc("ropeB", [128, S], F32, side="high")
        self.rC = p.alloc("ropeC", [128, S], F32, side="high")
        for nm in ("identf", "ones_1x128", "neg_ones_1x128", "padcol", "padcol64"):
            self.const(nm)
        self.epsbuf(1e-3)
        self.epsbuf(1e-5)
        for nm in ("identb", "blockdiag2", "band", "band_prev"):
            self.const(nm, BF16)
        self.rzb = p.alloc("rzb", [128, 1024], F32, side="high")

    def uid(self):
        self._uid += 1
        return self._uid

    def const(self, name, dtype=F32, data=None):
        key = (name, str(dtype))
        if key in self.cbuf:
            return self.cbuf[key]
        arr = self.consts_np[name] if data is None else data
        arr = arr.astype(ml_dtypes.bfloat16 if dtype == BF16 else np.float32)
        h = self.nc.inline_tensor(np.ascontiguousarray(arr), name=f"c{self.uid()}_{name}")
        b = self.p.alloc(f"C{self.uid()}_{name}", list(arr.shape), dtype, side="high")
        self.p.emit("sp", lambda e, b=b, h=h: e.dma_start(out=b.ap, in_=h.ap()),
                    writes=[b.name], ch="qsp")
        self.cbuf[key] = b
        return b

    def free_const(self, name, dtype=F32):
        key = (name, str(dtype))
        if key in self.cbuf:
            self.p.freebuf(self.cbuf.pop(key))

    def epsbuf(self, val):
        if val in self.eps:
            return self.eps[val]
        b = self.p.alloc(f"eps{self.uid()}", [128, 1], F32, side="high")
        self.p.emit("dve", lambda e, b=b, v=val: e.memset(b.ap, v), writes=[b.name])
        self.eps[val] = b
        return b

    def evac(self, dst_ap, src_ap, reads, writes, scale=None, eng=None):
        if eng is None:
            eng = ("dve", "act")[self.evac_flip % 2]
            self.evac_flip += 1
        if eng == "dve":
            if scale is None:
                self.p.emit("dve", lambda e: e.tensor_copy(dst_ap, src_ap),
                            reads=reads, writes=writes)
            else:
                self.p.emit("dve", lambda e: e.tensor_scalar_mul(dst_ap, src_ap, scale),
                            reads=reads, writes=writes)
        else:
            if scale is None:
                self.p.emit("act", lambda e: e.copy(dst_ap, src_ap),
                            reads=reads, writes=writes)
            else:
                self.p.emit("act", lambda e: e.mul(dst_ap, src_ap, scale),
                            reads=reads, writes=writes)

    def mm_group(self, bank_key, members):
        n = len(members)
        for i, (fn, reads) in enumerate(members):
            first, last = i == 0, i == n - 1
            self.p.emit("pe", fn, reads=reads,
                        writes=[bank_key] if first else (),
                        wover=[bank_key] if (last and not first) else (),
                        inc=last)

    def transpose128(self, bank_key, out_ap, in_ap, reads, ident, wover=False):
        self.p.emit("pe", lambda e: e.transpose(out_ap, in_ap, ident.ap),
                    reads=list(reads) + [ident.name],
                    writes=() if wover else [bank_key],
                    wover=[bank_key] if wover else (), inc=True)

    def full_transpose_cast(self, src_ap_fn, src_key_fn, dst, dst_key_fn, scale=None):
        identf = self.const("identf")
        p = self.p
        g = 0
        for t_ in range(NT):
            for ic0 in range(0, NC_, 4):
                bank = g % 4
                g += 1
                bk = f"ps{bank}"
                for j in range(4):
                    ic = ic0 + j
                    ps = p.PS[:, bank * 512 + j * 128: bank * 512 + (j + 1) * 128]
                    self.transpose128(bk, ps, src_ap_fn(ic, t_),
                                      [src_key_fn(ic, t_)], identf,
                                      wover=(j > 0))
                srcap = bass.AP(p.PS, bank * 512, [[4096, 128], [1, 512]])
                dstap = bass.AP(dst.t, ic0 * S + t_ * 128,
                                [[NC_ * S, 128], [S, 4], [1, 128]])
                self.evac(dstap, srcap, reads=[bk],
                          writes=[dst_key_fn(ic0 + j, t_) for j in range(4)],
                          scale=scale)

    def load_weight_T(self, wname, dram, scale=None):
        p = self.p
        O, I = dram.shape[0], dram.shape[1]
        noc, nic = O // 128, I // 128
        wt = p.alloc(wname, [128, nic * O], BF16)
        stage = p.alloc(wname + "_st", [128, 2 * I], F32)
        identf = self.const("identf")
        for oc in range(noc):
            par = oc % 2
            skey = f"{wname}_st#{par}"
            sta = stage.t[:, par * I:(par + 1) * I]
            eng, ch = p.hch()
            p.emit(eng, lambda e, oc=oc, sta=sta, dram=dram:
                   e.dma_start(out=sta, in_=dram[oc * 128:(oc + 1) * 128, :]),
                   reads=[skey], writes=[skey], ch=ch)
            for ic in range(nic):
                bank = (oc * nic + ic) % 4
                bk = f"ps{bank}"
                ps = p.PS[:, bank * 512: bank * 512 + 128]
                self.transpose128(bk, ps, sta[:, ic * 128:(ic + 1) * 128],
                                  [skey], identf)
                dcol = ic * O + oc * 128
                self.evac(wt.t[:, dcol:dcol + 128], ps,
                          reads=[bk], writes=[f"{wname}#{ic}"], scale=scale)
        p.freebuf(stage)
        return wt

    def load_weight_N(self, wname, dram, scale=None, O=None):
        p = self.p
        if O is None:
            I, O = dram.shape[0], dram.shape[1]
        else:
            I = dram.shape[0]
        nic = I // 128
        wt = p.alloc(wname, [128, nic * O], BF16)
        stage = p.alloc(wname + "_st", [128, 2 * O], F32)
        for ic in range(nic):
            par = ic % 2
            skey = f"{wname}_st#{par}"
            sta = stage.t[:, par * O:(par + 1) * O]
            eng, ch = p.hch()
            p.emit(eng, lambda e, ic=ic, sta=sta, dram=dram:
                   e.dma_start(out=sta, in_=dram[ic * 128:(ic + 1) * 128, :]),
                   reads=[skey], writes=[skey], ch=ch)
            self.evac(wt.t[:, ic * O:(ic + 1) * O], sta,
                      reads=[skey], writes=[f"{wname}#{ic}"], scale=scale)
        p.freebuf(stage)
        return wt

    def spill_x(self, tag, free=True):
        sp = self.nc.dram_tensor(f"spill_{tag}", [S, D], F32, kind="Internal").ap()
        self.spill[tag] = sp
        X = self.X
        for i in range(NT):
            self.p.emit("pool", lambda e, i=i, sp=sp, X=X:
                        e.dma_start(out=sp[i * 128:(i + 1) * 128, :],
                                    in_=X[:, i * D:(i + 1) * D]),
                        reads=[f"X#{i}"], writes=[f"sp_{tag}#{i}"], ch="qpool")
        if free:
            self.p.freebuf(self.X)
            self.X = None

    def realloc_x(self):
        assert self.X is None
        try:
            self.X = self.p.alloc("X", [128, NT * D], F32, fixed_off=X_FIXED)
        except MemoryError:
            self.X = self.p.alloc("X", [128, NT * D], F32)

    def unspill_x_tile(self, tag, i):
        sp = self.spill[tag]
        X = self.X
        eng, ch = self.p.hch()
        self.p.emit(eng, lambda e, i=i, sp=sp, X=X:
                    e.dma_start(out=X[:, i * D:(i + 1) * D],
                                in_=sp[i * 128:(i + 1) * 128, :]),
                    reads=[f"sp_{tag}#{i}"], writes=[f"X#{i}"], ch=ch)

    def ln_tile_stages(self, i, eps):
        p = self.p
        st = self.stat
        b0 = (i % 2) * 16
        sk = f"stat#{i % 2}"
        epsb = self.epsbuf(eps)
        xap = self.X[:, i * D:(i + 1) * D]
        xk = f"X#{i}"

        def fa():
            for ch_ in range(2):
                a = self.X[:, i * D + ch_ * 512: i * D + (ch_ + 1) * 512]
                p.emit("dve", lambda e, a=a, ch_=ch_, b0=b0: e.bn_stats(
                    st[:, b0 + ch_ * 6: b0 + (ch_ + 1) * 6], a),
                    reads=[xk], writes=[sk])
            p.emit("dve", lambda e, b0=b0: e.bn_aggr(st[:, b0 + 12:b0 + 14],
                                                     st[:, b0:b0 + 12]),
                   reads=[sk], writes=[sk])
            p.emit("act", lambda e, b0=b0: e.activation(
                st[:, b0 + 14:b0 + 15], st[:, b0 + 13:b0 + 14], AF.Sqrt,
                bias=epsb.ap), reads=[sk, epsb.name], writes=[sk])

        def fb():
            p.emit("dve", lambda e, b0=b0: e.reciprocal(st[:, b0 + 15:b0 + 16],
                                                        st[:, b0 + 14:b0 + 15]),
                   reads=[sk], writes=[sk])
            p.emit("dve", lambda e, b0=b0: e.scalar_tensor_tensor(
                st[:, b0 + 12:b0 + 13], st[:, b0 + 12:b0 + 13], -1.0,
                st[:, b0 + 15:b0 + 16], ALU.mult, ALU.mult),
                reads=[sk], writes=[sk])
            p.emit("act", lambda e, b0=b0: e.activation(
                xap, xap, AF.Identity, bias=st[:, b0 + 12:b0 + 13],
                scale=st[:, b0 + 15:b0 + 16]),
                reads=[xk, sk], writes=[xk])
        return fa, fb

    def ln_tile(self, i, eps, gamma=None, beta=None):
        p = self.p
        st = self.stat
        b0 = (i % 2) * 16
        sk = f"stat#{i % 2}"
        epsb = self.epsbuf(eps)
        xap = self.X[:, i * D:(i + 1) * D]
        xk = f"X#{i}"
        for ch_ in range(2):
            a = self.X[:, i * D + ch_ * 512: i * D + (ch_ + 1) * 512]
            p.emit("dve", lambda e, a=a, ch_=ch_, b0=b0: e.bn_stats(
                st[:, b0 + ch_ * 6: b0 + (ch_ + 1) * 6], a),
                reads=[xk], writes=[sk])
        p.emit("dve", lambda e, b0=b0: e.bn_aggr(st[:, b0 + 12:b0 + 14],
                                                 st[:, b0:b0 + 12]),
               reads=[sk], writes=[sk])
        p.emit("act", lambda e, b0=b0: e.activation(
            st[:, b0 + 14:b0 + 15], st[:, b0 + 13:b0 + 14], AF.Sqrt,
            bias=epsb.ap), reads=[sk, epsb.name], writes=[sk])
        p.emit("dve", lambda e, b0=b0: e.reciprocal(st[:, b0 + 15:b0 + 16],
                                                    st[:, b0 + 14:b0 + 15]),
               reads=[sk], writes=[sk])
        p.emit("dve", lambda e, b0=b0: e.scalar_tensor_tensor(
            st[:, b0 + 12:b0 + 13], st[:, b0 + 12:b0 + 13], -1.0,
            st[:, b0 + 15:b0 + 16], ALU.mult, ALU.mult),
            reads=[sk], writes=[sk])
        p.emit("act", lambda e, b0=b0: e.activation(
            xap, xap, AF.Identity, bias=st[:, b0 + 12:b0 + 13],
            scale=st[:, b0 + 15:b0 + 16]),
            reads=[xk, sk], writes=[xk])
        if gamma is not None:
            p.emit("dve", lambda e: e.tensor_mul(xap, xap, gamma.ap),
                   reads=[xk, gamma.name], writes=[xk])
            p.emit("dve", lambda e: e.tensor_add(xap, xap, beta.ap),
                   reads=[xk, beta.name], writes=[xk])

    def bcast_scalar(self, src_ap, src_key, dst_ap, dst_key, neg=False):
        p = self.p
        ones = self.const("neg_ones_1x128" if neg else "ones_1x128", F32)
        self.mm_group("ps7", [(lambda e: e.matmul(
            p.PS[:, 7 * 512:7 * 512 + 1], ones.ap, src_ap,
            start=True, stop=True), [ones.name, src_key])])
        p.emit("dve", lambda e: e.tensor_copy(dst_ap, p.PS[:, 7 * 512:7 * 512 + 1]),
               reads=["ps7"], writes=[dst_key])

    def part_reduce(self, src_ap, src_key, dst_ap, dst_key, op, n=128):
        p = self.p
        identf = self.const("identf")
        self.transpose128("ps7", p.PS[0:1, 7 * 512:7 * 512 + n], src_ap,
                          [src_key], identf)
        p.emit("dve", lambda e: e.tensor_reduce(
            dst_ap, p.PS[0:1, 7 * 512:7 * 512 + n], axis=AX.X, op=op),
            reads=["ps7"], writes=[dst_key])

    def write_out(self):
        out = self.io["out"]
        X = self.X
        for i in range(NT):
            self.p.emit("pool", lambda e, i=i, X=X: e.dma_start(
                out=out[i * 128:(i + 1) * 128, :],
                in_=X[:, i * D:(i + 1) * D]),
                reads=[f"X#{i}"], ch="qpool")
        self.p.wait_all("pool")

    def finalize(self):
        self.p.finalize()


def head_stats(mk, TN, tname, col0, nchunks=NC_, span=S):
    p = mk.p
    bdb = mk.const("blockdiag2", BF16)
    sq = p.alloc("sqst", [128, span], BF16)
    tail = span - min(span, 2048)
    for oc in range(nchunks):
        p.emit("act", lambda e, oc=oc: e.activation(
            sq.ap, TN.t[:, oc * span:(oc + 1) * span], AF.Square),
            reads=[f"{tname}#{oc}"], writes=["sqst"])
        nb = min(span, 2048) // 512
        for tb in range(nb):
            bank = 4 + tb
            mk.mm_group(f"ps{bank}", [(lambda e, tb=tb, bank=bank: e.matmul(
                p.PS[:, bank * 512:(bank + 1) * 512], bdb.ap,
                sq[:, tb * 512:(tb + 1) * 512], start=True, stop=True),
                [bdb.name, "sqst"])])
        p.emit("dve", lambda e, nb=nb, oc=oc: e.tensor_reduce(
            mk.bb[0:2, col0 + oc:col0 + oc + 1], p.PS[0:2, 4 * 512:(4 + nb) * 512],
            axis=AX.X, op=ALU.max),
            reads=[f"ps{4 + tb}" for tb in range(nb)], writes=["bb"])
        if tail > 0:
            mk.mm_group("ps4", [(lambda e: e.matmul(
                p.PS[:, 4 * 512:4 * 512 + tail], bdb.ap,
                sq[:, 2048:2048 + tail], start=True, stop=True),
                [bdb.name, "sqst"])])
            p.emit("dve", lambda e: e.tensor_reduce(
                mk.bb[0:2, 22:23], p.PS[0:2, 4 * 512:4 * 512 + tail],
                axis=AX.X, op=ALU.max), reads=["ps4"], writes=["bb"])
            p.emit("dve", lambda e, oc=oc: e.tensor_max(
                mk.bb[0:2, col0 + oc:col0 + oc + 1],
                mk.bb[0:2, col0 + oc:col0 + oc + 1], mk.bb[0:2, 22:23]),
                reads=["bb"], writes=["bb"])
    p.freebuf(sq)


def finish_bounds(mk, nheads):
    p = mk.p
    p.emit("sp", lambda e: e.dma_start(out=mk.brow[0:1, 0:32], in_=mk.bb[0:2, 0:16]),
           reads=["bb"], writes=["brow"], ch="qsp")
    qap = bass.AP(mk.brow.t, 0, [[80, 1], [16, 2], [1, 8]])
    kap = bass.AP(mk.brow.t, 8, [[80, 1], [16, 2], [1, 8]])
    oap = bass.AP(mk.brow.t, 32, [[80, 1], [8, 2], [1, 8]])
    p.emit("dve", lambda e: e.tensor_tensor(oap, qap, kap, ALU.mult),
           reads=["brow"], writes=["brow"])
    p.emit("act", lambda e: e.activation(mk.brow[0:1, 48:64], mk.brow[0:1, 32:48],
                                         AF.Sqrt), reads=["brow"], writes=["brow"])


def attn_loop(mk, QT, KT, vp_fn, OnT, H, span, kt_lists, null_pad_tile=None,
              kspan=None, names=("QT", "KT", "Vp", "OnT"), bias_col=None):
    p = mk.p
    PTb = mk.PTb
    qtn, ktn, vpn, ontn = names
    identb = mk.const("identb", BF16)
    onesb = mk.const("ones128", BF16)
    negonesf = mk.const("neg_ones_1x128", F32)
    nqb = len(kt_lists)
    if bias_col is None:
        bias_col = lambda h, qb: 48 + (h % 2) * 8 + h // 2
    mk.mm_group("ps0", [(lambda e: e.matmul(
        p.PS[:, 0:64], negonesf.ap, mk.brow[0:1, 0:64],
        start=True, stop=True), [negonesf.name, "brow"])])
    p.emit("dve", lambda e: e.tensor_copy(mk.biasball[:, 0:64], p.PS[:, 0:64]),
           reads=["ps0"], writes=["biasball"])
    if null_pad_tile is not None:
        padc = mk.const("padcol64")
        p.emit("dve", lambda e: e.tensor_tensor(
            mk.biasball[:, 64:128], mk.biasball[:, 0:64], padc.ap, ALU.add),
            reads=["biasball", padc.name], writes=["biasball"])

    bc = [0]
    gpt = [0]
    gz = [0]
    pending_z = []
    ksp = kspan or span

    def emit_S_pair(h0, h1, qb, kt, bandname, off, b0_, b1_):
        has_mask = bandname is not None
        for h, b in ((h0, b0_), (h1, b1_)):
            half, oc = h % 2, h // 2
            lhs = KT.t[half * 64:half * 64 + 64,
                       oc * ksp + kt * 128: oc * ksp + kt * 128 + 128]
            rhs = QT.t[half * 64:half * 64 + 64,
                       oc * S + qb * 512: oc * S + qb * 512 + 512]
            p.emit("pe", lambda e, lhs=lhs, rhs=rhs, b=b, hm=has_mask:
                   e.matmul(p.PS[:, b * 512:(b + 1) * 512], lhs, rhs,
                            start=True, stop=not hm),
                   reads=[f"{ktn}#{oc}", f"{qtn}#{oc}"],
                   writes=[f"ps{b}"], inc=not has_mask)
        if has_mask:
            bnd = mk.const(bandname, BF16)
            if bandname == "band_prev":
                c_lo, c_hi = off, 512
                msl = bnd.t[:, 512: 512 + (512 - off)]
            else:
                c_lo, c_hi = 0, min(512, off + 128)
                msl = bnd.t[:, 512 - off: 512 - off + (c_hi - c_lo)]
            for b in (b0_, b1_):
                p.emit("pe", lambda e, msl=msl, b=b, c_lo=c_lo, c_hi=c_hi:
                       e.matmul(p.PS[:, b * 512 + c_lo: b * 512 + c_hi],
                                identb.ap, msl, start=False, stop=True),
                       reads=[identb.name, bnd.name],
                       wover=[f"ps{b}"], inc=True)

    def emit_exp1(h, kt, qb, b):
        c0 = bias_col(h, qb)
        pslot = gpt[0] % 6
        gpt[0] += 1
        ptk = f"PTb#{pslot}"
        pad = (null_pad_tile is not None and kt == null_pad_tile)
        bsl = (mk.biasball[:, 64 + c0:64 + c0 + 1] if pad
               else mk.biasball[:, c0:c0 + 1])
        p.emit("act", lambda e, bsl=bsl, b=b, pslot=pslot:
               e.activation(PTb.t[:, pslot * 512:(pslot + 1) * 512],
                            p.PS[:, b * 512:(b + 1) * 512],
                            AF.Exp, bias=bsl),
               reads=[f"ps{b}", "biasball"], writes=[ptk])
        return pslot

    def emit_pv1(h, kt, ob, pslot, st_, last):
        ptk = f"PTb#{pslot}"
        vb, vcol0, vkey = vp_fn(kt)
        lhs = vb.t[:, vcol0 + h * 65: vcol0 + h * 65 + 65]
        rhs = PTb.t[:, pslot * 512:(pslot + 1) * 512]
        p.emit("pe", lambda e, lhs=lhs, rhs=rhs, st_=st_, last=last, ob=ob:
               e.matmul(p.PS[0:65, ob * 512:ob * 512 + 512], lhs, rhs,
                        start=st_, stop=last),
               reads=[vkey, ptk],
               writes=[f"ps{ob}"] if st_ else (),
               wover=[f"ps{ob}"] if (last and not st_) else (),
               inc=last)

    zstage = mk.nc.dram_tensor(f"zstage_{mk.uid()}", [2, 512], F32,
                               kind="Internal").ap()

    def make_z2(dsts, ocks, obs):
        for hx in range(2):
            ob = obs[hx]
            stg = mk.rzb.t[64:65, hx * 512:(hx + 1) * 512]
            p.emit("dve", lambda e, ob=ob, stg=stg: e.reciprocal(
                stg, p.PS[64:65, ob * 512:(ob + 1) * 512]),
                reads=[f"ps{ob}"], writes=[f"zrow#{hx}"])
            p.emit("sp", lambda e, hx=hx, stg=stg: e.dma_start(
                out=zstage[hx:hx + 1, :], in_=stg),
                reads=[f"zrow#{hx}"], writes=[f"zst#{hx}"], ch="qsp")
            srcap = bass.AP(zstage.tensor, hx * 512, [[0, 64], [1, 512]])
            dstap = bass.AP(mk.rzb.t, hx * 512, [[1024, 64], [1, 512]])
            p.emit("sp", lambda e, srcap=srcap, dstap=dstap: e.dma_start(
                out=dstap, in_=srcap),
                reads=[f"zst#{hx}"], writes=[f"rzb#{hx}"], ch="qsp")

        def go():
            for hx in range(2):
                ob, dst, ock = obs[hx], dsts[hx], ocks[hx]
                rza = mk.rzb.t[0:64, hx * 512:(hx + 1) * 512]
                p.emit("dve", lambda e, rza=rza, dst=dst, ob=ob: e.tensor_tensor(
                    dst, p.PS[0:64, ob * 512:(ob + 1) * 512], rza, ALU.mult),
                    reads=[f"ps{ob}", f"rzb#{hx}"], writes=[ock])
        return go

    items = [(hp, qb) for hp in range(H // 2) for qb in range(nqb)]
    for item_i, (hp, qb) in enumerate(items):
        h0, h1 = 2 * hp, 2 * hp + 1
        oc = hp
        obs = (4, 5) if item_i % 2 == 0 else (6, 7)
        kt_list = kt_lists[qb]
        n = len(kt_list)
        sbank = []
        pslots = []

        def do_exp(ki):
            pkt = kt_list[ki][0]
            pslots.append((emit_exp1(h0, pkt, qb, sbank[ki][0]),
                           emit_exp1(h1, pkt, qb, sbank[ki][1])))

        def do_pv(ki):
            pkt = kt_list[ki][0]
            for hx, (h, ob) in enumerate(((h0, obs[0]), (h1, obs[1]))):
                emit_pv1(h, pkt, ob, pslots[ki][hx],
                         st_=(ki == 0), last=(ki == n - 1))

        assert n >= 3
        for ki, (kt, bandname, off) in enumerate(kt_list):
            if ki >= 3:
                do_pv(ki - 3)
            if ki >= 1:
                do_exp(ki - 1)
            b0_ = bc[0] % 4
            b1_ = (bc[0] + 1) % 4
            bc[0] += 2
            sbank.append((b0_, b1_))
            emit_S_pair(h0, h1, qb, kt, bandname, off, b0_, b1_)
            if ki == 2 and pending_z:
                for z in pending_z:
                    z()
                pending_z.clear()
        do_exp(n - 1)
        for ki in range(max(0, n - 3), n):
            do_pv(ki)
        dsts, ocks = [], []
        for h in (h0, h1):
            half = h % 2
            dsts.append(OnT.t[half * 64:half * 64 + 64,
                              oc * S + qb * 512: oc * S + qb * 512 + 512])
            ocks.append(f"{ontn}#{oc}")
        pending_z.append(make_z2(dsts, ocks, obs))
    for z in pending_z:
        z()


def out_proj_residual(mk, OnT, WoT, ont_name, wot_name, spill_tag, post_eps,
                      gamma=None, beta=None, extra_spill=None, bias_tab=None,
                      nchunks=NC_, scale_ap=None, scale_key=None, chunk_k=None,
                      tiles=None, ont_span=S, dst_spill=None):
    p = mk.p
    X = mk.X
    ln_pend = [None]
    if dst_spill is not None and dst_spill not in mk.spill:
        mk.spill[dst_spill] = mk.nc.dram_tensor(
            f"spill_{dst_spill}", [S, D], F32, kind="Internal").ap()
    if dst_spill is None and spill_tag is not None:
        for i in (tiles if tiles is not None else range(NT)):
            mk.unspill_x_tile(spill_tag, i)
    for i in (tiles if tiles is not None else range(NT)):
        if dst_spill is not None:
            xb = mk.xtra2 if i % 2 == 0 else mk.xtra2_b
            xbk = "xtra2" if i % 2 == 0 else "xtra2_b"
            sp0 = mk.spill[spill_tag]
            eng, ch = p.hch()
            p.emit(eng, lambda e, i=i, sp0=sp0, xb=xb: e.dma_start(
                out=xb.ap, in_=sp0[i * 128:(i + 1) * 128, :]),
                reads=[f"sp_{spill_tag}#{i}"], writes=[xbk], ch=ch)
        if extra_spill is not None:
            sp = mk.spill[extra_spill]
            p.emit("sp", lambda e, i=i, sp=sp: e.dma_start(
                out=mk.xtra.ap, in_=sp[i * 128:(i + 1) * 128, :]),
                reads=[f"sp_{extra_spill}#{i}"], writes=["xtra"], ch="qsp")
        tpar = (i % 2) * 2
        for oh in range(2):
            bank = 4 + tpar + oh
            members = []
            for ic in range(nchunks):
                kk = 128 if chunk_k is None else chunk_k(ic)
                i_loc = i - (tiles.start if tiles is not None else 0)
                lhs = OnT.t[0:kk, ic * ont_span + i_loc * 128:
                            ic * ont_span + i_loc * 128 + 128]
                rhs = WoT.t[0:kk, ic * D + oh * 512: ic * D + oh * 512 + 512]
                members.append((lambda e, lhs=lhs, rhs=rhs, bank=bank, ic=ic,
                                nchunks=nchunks:
                                e.matmul(p.PS[:, bank * 512:(bank + 1) * 512],
                                         lhs, rhs, start=(ic == 0),
                                         stop=(ic == nchunks - 1)),
                                [f"{ont_name}#{ic}", f"{wot_name}#{ic}"]))
            mk.mm_group(f"ps{bank}", members)
        for oh in range(2):
            bank = 4 + tpar + oh
            if dst_spill is not None:
                xap = xb[:, oh * 512:(oh + 1) * 512]
                xkey = xbk
            else:
                xap = X[:, i * D + oh * 512: i * D + (oh + 1) * 512]
                xkey = f"X#{i}"
            if scale_ap is None:
                p.emit("dve", lambda e, xap=xap, bank=bank: e.scalar_tensor_tensor(
                    xap, p.PS[:, bank * 512:(bank + 1) * 512], 1.0, xap,
                    ALU.mult, ALU.add),
                    reads=[f"ps{bank}", xkey], writes=[xkey])
            else:
                p.emit("dve", lambda e, xap=xap, bank=bank: e.scalar_tensor_tensor(
                    xap, p.PS[:, bank * 512:(bank + 1) * 512], scale_ap, xap,
                    ALU.mult, ALU.add),
                    reads=[f"ps{bank}", xkey, scale_key], writes=[xkey])
            if bias_tab is not None:
                p.emit("dve", lambda e, xap=xap, oh=oh: e.tensor_add(
                    xap, xap, bias_tab.t[:, oh * 512:(oh + 1) * 512]),
                    reads=[xkey, bias_tab.name], writes=[xkey])
            if extra_spill is not None:
                p.emit("dve", lambda e, xap=xap, oh=oh: e.tensor_add(
                    xap, xap, mk.xtra[:, oh * 512:(oh + 1) * 512]),
                    reads=[xkey, "xtra"], writes=[xkey])
        if dst_spill is not None:
            spd = mk.spill[dst_spill]
            p.emit("pool", lambda e, i=i, spd=spd, xb=xb: e.dma_start(
                out=spd[i * 128:(i + 1) * 128, :], in_=xb.ap),
                reads=[xbk], writes=[f"sp_{dst_spill}#{i}"], ch="qpool")
        if post_eps is not None:
            assert gamma is None
            fa, fb = mk.ln_tile_stages(i, post_eps)
            fa()
            if ln_pend[0] is not None:
                ln_pend[0]()
            ln_pend[0] = fb
    if ln_pend[0] is not None:
        ln_pend[0]()


def block_l0_attn(mk):
    p = mk.p
    io = mk.io
    H = 16
    mk.const("l0_cos"); mk.const("l0_sinsw")
    WqT = mk.load_weight_T("WqT", io["L0_attn_Wq"], scale=0.125)
    XT = p.alloc("XT", [128, NC_ * S], BF16)
    mk.full_transpose_cast(
        lambda ic, t: mk.X[:, t * D + ic * 128: t * D + ic * 128 + 128],
        lambda ic, t: f"X#{t}", XT, lambda ic, t: f"XT#{ic}")
    mk.spill_x("l0a")

    cosb = mk.const("l0_cos")
    sinswb = mk.const("l0_sinsw")
    QT = p.alloc("QT", [128, NC_ * S], BF16)
    KT = p.alloc("KT", [128, NC_ * S], BF16)
    rA, rB, rC = mk.rA, mk.rB, mk.rC

    def project_chunk(wt, wname, oc, b0=0):
        for tb in range(4):
            members = []
            for ic in range(NC_):
                lhs = wt.t[:, ic * D + oc * 128: ic * D + oc * 128 + 128]
                rhs = XT.t[:, ic * S + tb * 512: ic * S + tb * 512 + 512]
                members.append((lambda e, lhs=lhs, rhs=rhs, b=b0 + tb, ic=ic:
                                e.matmul(p.PS[:, b * 512:(b + 1) * 512], lhs, rhs,
                                         start=(ic == 0), stop=(ic == NC_ - 1)),
                                [f"{wname}#{ic}", f"XT#{ic}"]))
            mk.mm_group(f"ps{b0 + tb}", members)

    def rope_evac(dst, dname, oc, cos, sinsw, swap, b0=0):
        pall = p.PS[:, b0 * 512: b0 * 512 + 2048]
        pk = [f"ps{b0 + j}" for j in range(4)]
        p.emit("dve", lambda e, pall=pall: e.tensor_mul(rA.ap, pall, cos.ap),
               reads=pk + [cos.name], writes=["ropeA"])
        p.emit("dve", lambda e, pall=pall: e.tensor_mul(rB.ap, pall, sinsw.ap),
               reads=pk + [sinsw.name], writes=["ropeB"])
        if swap == "pair":
            mask = [(i + 1 if i % 2 == 0 else i - 1) for i in range(32)]
        else:
            mask = [(i + 16) % 32 for i in range(32)]
        p.emit("dve", lambda e, mask=mask: e.stream_shuffle(rC.ap, rB.ap, mask),
               reads=["ropeB"], writes=["ropeC"])
        dcol = oc * S
        p.emit("dve", lambda e: e.tensor_add(dst.t[:, dcol:dcol + S], rA.ap, rC.ap),
               reads=["ropeA", "ropeC"], writes=[f"{dname}#{oc}"])

    for oc in range(NC_):
        project_chunk(WqT, "WqT", oc, (oc % 2) * 4)
        if oc > 0:
            rope_evac(QT, "QT", oc - 1, cosb, sinswb, "pair", ((oc - 1) % 2) * 4)
    rope_evac(QT, "QT", NC_ - 1, cosb, sinswb, "pair", ((NC_ - 1) % 2) * 4)
    p.freebuf(WqT)
    WkT = mk.load_weight_T("WkT", io["L0_attn_Wk"])
    for oc in range(NC_):
        project_chunk(WkT, "WkT", oc, (oc % 2) * 4)
        if oc > 0:
            rope_evac(KT, "KT", oc - 1, cosb, sinswb, "pair", ((oc - 1) % 2) * 4)
    rope_evac(KT, "KT", NC_ - 1, cosb, sinswb, "pair", ((NC_ - 1) % 2) * 4)
    p.freebuf(WkT)
    mk.free_const("l0_cos"); mk.free_const("l0_sinsw")
    head_stats(mk, QT, "QT", 0)
    head_stats(mk, KT, "KT", 8)

    VW = H * 65
    VpA = p.alloc("VpA", [128, 8 * VW], BF16)
    VpB = p.alloc("VpB", [128, 8 * VW], BF16)

    def vp_fn(kt):
        if kt < 8:
            return VpA, kt * VW, f"VpA#{kt}"
        return VpB, (kt - 8) * VW, f"VpB#{kt - 8}"
    for vb, vn in ((VpA, "VpA"), (VpB, "VpB")):
        ones_ap = bass.AP(vb.t, 64, [[8 * VW, 128], [VW, 8], [65, H]])
        p.emit("dve", lambda e, ones_ap=ones_ap: e.memset(ones_ap, 1.0),
               writes=[f"{vn}#{kt}" for kt in range(8)])
    vchunk = p.alloc("vchunk", [128, S], F32)
    identf = mk.const("identf")
    WvT = mk.load_weight_T("WvT", io["L0_attn_Wv"])

    def v_evac(oc, b0):
        pall = p.PS[:, b0 * 512: b0 * 512 + 2048]
        pk = [f"ps{b0 + j}" for j in range(4)]
        p.emit("dve", lambda e, pall=pall: e.tensor_copy(vchunk.ap, pall),
               reads=pk, writes=["vchunk"])
        h0 = 2 * oc
        for g4 in range(4):
            bank = b0 + g4
            bk = f"ps{bank}"
            for j in range(4):
                kt = g4 * 4 + j
                ps = p.PS[:, bank * 512 + j * 128: bank * 512 + (j + 1) * 128]
                mk.transpose128(bk, ps, vchunk[:, kt * 128:(kt + 1) * 128],
                                ["vchunk"], identf, wover=(j > 0))
            vb, vcol0, _ = vp_fn(g4 * 4)
            srcap = bass.AP(p.PS, bank * 512,
                            [[4096, 128], [128, 4], [64, 2], [1, 64]])
            dstap = bass.AP(vb.t, vcol0 + h0 * 65,
                            [[8 * VW, 128], [VW, 4], [65, 2], [1, 64]])
            p.emit("dve", lambda e, srcap=srcap, dstap=dstap:
                   e.tensor_copy(dstap, srcap),
                   reads=[bk], writes=[vp_fn(g4 * 4 + j)[2] for j in range(4)])

    for oc in range(NC_):
        project_chunk(WvT, "WvT", oc, (oc % 2) * 4)
        if oc > 0:
            v_evac(oc - 1, ((oc - 1) % 2) * 4)
    v_evac(NC_ - 1, ((NC_ - 1) % 2) * 4)
    p.freebuf(WvT); p.freebuf(vchunk); p.freebuf(XT)
    finish_bounds(mk, H)

    OnT = p.alloc("OnT", [128, NC_ * S], BF16)
    mk.PTb = p.alloc("PTb", [128, 3072], BF16)
    kt_lists = [[(kt, ("band" if kt * 128 >= qb * 512 else None),
                  (kt * 128 - qb * 512 if kt * 128 >= qb * 512 else None))
                 for kt in range(4 * (qb + 1))] for qb in range(4)]
    attn_loop(mk, QT, KT, vp_fn, OnT, H, S, kt_lists)
    p.freebuf(QT); p.freebuf(KT); p.freebuf(VpA); p.freebuf(VpB)
    p.freebuf(mk.PTb)
    del mk.PTb

    WoT = mk.load_weight_T("WoT", io["L0_attn_Wo"])
    mk.realloc_x()
    out_proj_residual(mk, OnT, WoT, "OnT", "WoT", "l0a", post_eps=1e-5)
    p.freebuf(OnT); p.freebuf(WoT)


def block_l0_ffn(mk, n_cores=8):
    p = mk.p
    io = mk.io
    nc = mk.nc
    gtab = p.alloc("l0f_g", [128, D], F32)
    btab = p.alloc("l0f_b", [128, D], F32)
    for tab, nm in ((gtab, "L0_ffn_ln_gamma"), (btab, "L0_ffn_ln_beta")):
        srcap = bass.AP(io[nm].tensor, 0, [[0, 128], [1, D]])
        p.emit("sp", lambda e, tab=tab, srcap=srcap: e.dma_start(out=tab.ap, in_=srcap),
               writes=[tab.name], ch="qsp")
    xnT = p.alloc("xnT", [128, NC_ * S], BF16)
    xn = p.alloc("xn_scr", [128, 2 * D], F32)
    gm = p.alloc("gmax", [128, 8], F32)
    st = mk.stat
    epsb = mk.epsbuf(1e-3)
    identf = mk.const("identf")
    def xn_A(i):
        par = i % 2
        b0 = par * 16
        sk = f"stat#{par}"
        xap = mk.X[:, i * D:(i + 1) * D]
        xk = f"X#{i}"
        for ch_ in range(2):
            p.emit("dve", lambda e, xap=xap, ch_=ch_, b0=b0: e.bn_stats(
                st[:, b0 + ch_ * 6: b0 + (ch_ + 1) * 6],
                xap[:, ch_ * 512:(ch_ + 1) * 512]),
                reads=[xk], writes=[sk])
        p.emit("dve", lambda e, b0=b0: e.bn_aggr(st[:, b0 + 12:b0 + 14],
                                                 st[:, b0:b0 + 12]),
               reads=[sk], writes=[sk])
        p.emit("act", lambda e, b0=b0: e.activation(
            st[:, b0 + 14:b0 + 15], st[:, b0 + 13:b0 + 14], AF.Sqrt,
            bias=epsb.ap), reads=[sk, epsb.name], writes=[sk])

    def xn_CD(i):
        par = i % 2
        b0 = par * 16
        sk = f"stat#{par}"
        xnk = f"xn_scr#{par}"
        xna = xn.t[:, par * D:(par + 1) * D]
        xap = mk.X[:, i * D:(i + 1) * D]
        xk = f"X#{i}"
        p.emit("dve", lambda e, b0=b0: e.reciprocal(st[:, b0 + 15:b0 + 16],
                                                    st[:, b0 + 14:b0 + 15]),
               reads=[sk], writes=[sk])
        p.emit("dve", lambda e, b0=b0: e.scalar_tensor_tensor(
            st[:, b0 + 12:b0 + 13], st[:, b0 + 12:b0 + 13], -1.0,
            st[:, b0 + 15:b0 + 16], ALU.mult, ALU.mult),
            reads=[sk], writes=[sk])
        p.emit("act", lambda e, xap=xap, xna=xna, b0=b0: e.activation(
            xna, xap, AF.Identity, bias=st[:, b0 + 12:b0 + 13],
            scale=st[:, b0 + 15:b0 + 16]),
            reads=[xk, sk], writes=[xnk])

    def xn_EF(i):
        par = i % 2
        xnk = f"xn_scr#{par}"
        xna = xn.t[:, par * D:(par + 1) * D]
        p.emit("dve", lambda e, xna=xna: e.tensor_mul(xna, xna, gtab.ap),
               reads=[xnk, gtab.name], writes=[xnk])
        p.emit("dve", lambda e, xna=xna: e.tensor_add(xna, xna, btab.ap),
               reads=[xnk, btab.name], writes=[xnk])
        if i < 2:
            p.emit("dve", lambda e, xna=xna, par=par: e.tensor_reduce(
                gm[:, par:par + 1], xna, axis=AX.X, op=ALU.max,
                apply_absolute_value=True), reads=[xnk], writes=[f"gmax#{par}"])
        else:
            p.emit("dve", lambda e, xna=xna, par=par: e.tensor_reduce(
                gm[:, 2 + par:3 + par], xna, axis=AX.X, op=ALU.max,
                apply_absolute_value=True), reads=[xnk], writes=[f"gmax#{par}"])
            p.emit("dve", lambda e, par=par: e.tensor_max(
                gm[:, par:par + 1], gm[:, par:par + 1], gm[:, 2 + par:3 + par]),
                reads=[f"gmax#{par}"], writes=[f"gmax#{par}"])
        for ic0 in range(0, NC_, 4):
            bank = par * 2 + ic0 // 4
            bk = f"ps{bank}"
            for j in range(4):
                ic = ic0 + j
                ps = p.PS[:, bank * 512 + j * 128: bank * 512 + (j + 1) * 128]
                mk.transpose128(bk, ps, xna[:, ic * 128:(ic + 1) * 128],
                                [xnk], identf, wover=(j > 0))
            srcap = bass.AP(p.PS, bank * 512, [[4096, 128], [1, 512]])
            dstap = bass.AP(xnT.t, ic0 * S + i * 128,
                            [[NC_ * S, 128], [S, 4], [1, 128]])
            mk.evac(dstap, srcap, reads=[bk],
                    writes=[f"xnT#{ic0 + j}" for j in range(4)])

    for i in range(NT):
        xn_A(i)
        if i >= 1:
            xn_CD(i - 1)
        if i >= 2:
            xn_EF(i - 2)
    xn_CD(NT - 1)
    xn_EF(NT - 2)
    xn_EF(NT - 1)
    p.emit("dve", lambda e: e.tensor_max(gm[:, 0:1], gm[:, 0:1], gm[:, 1:2]),
           reads=["gmax#0", "gmax#1"], writes=["gmax"])
    p.freebuf(xn)
    p.emit("dve", lambda e: e.memset(gm[:, 4:6], 0.0),
           reads=["gmax"], writes=["gmax"])
    mk.part_reduce(gm[:, 0:1], "gmax", gm[0:1, 4:5], "gmax", ALU.max)
    ccbuf = p.alloc("ccbuf", [1, 16], F32)
    p.emit("dve", lambda e: e.memset(ccbuf.ap, 0.0), writes=["ccbuf"])
    p.emit("dve", lambda e: e.tensor_copy(ccbuf[0:1, 0:1], gm[0:1, 4:5]),
           reads=["gmax"], writes=["ccbuf"])
    if n_cores > 1:
        cc_in = nc.dram_tensor("cc_in_l0f", [1, 16], F32, kind="Internal")
        cc_out = nc.dram_tensor("cc_out_l0f", [1, 16], F32, kind="Internal")
        p.emit("pool", lambda e: e.dma_start(out=cc_in[:, :], in_=ccbuf.ap),
               reads=["ccbuf"], writes=["ccdram"], ch="qpool")
        p.emit("pool", lambda e: e.collective_compute(
            "AllReduce", ALU.max, replica_groups=[list(range(n_cores))],
            ins=[cc_in.ap().opt()], outs=[cc_out.ap().opt()]),
            reads=["ccdram"], writes=["ccdram2"], ch="cc")
        p.emit("sp", lambda e: e.dma_start(out=ccbuf[0:1, 0:16], in_=cc_out[:, :]),
               reads=["ccdram2"], writes=["ccbuf"], ch="qsp")
    p.emit("dve", lambda e: e.reciprocal(ccbuf[0:1, 1:2], ccbuf[0:1, 0:1]),
           reads=["ccbuf"], writes=["ccbuf"])
    p.emit("dve", lambda e: e.tensor_scalar_mul(ccbuf[0:1, 2:3], ccbuf[0:1, 1:2],
                                                128.0), reads=["ccbuf"], writes=["ccbuf"])
    mk.bcast_scalar(ccbuf[0:1, 2:3], "ccbuf", mk.biasb[:, 0:1], "biasb")
    CLIP = 128.0 - 1e-5
    for ic in range(NC_):
        sl = xnT.t[:, ic * S:(ic + 1) * S]
        p.emit("dve", lambda e, sl=sl: e.tensor_scalar(
            sl, sl, mk.biasb[:, 0:1], CLIP, ALU.mult, ALU.min),
            reads=[f"xnT#{ic}", "biasb"], writes=[f"xnT#{ic}"])
        p.emit("dve", lambda e, sl=sl: e.tensor_scalar_max(sl, sl, -CLIP),
            reads=[f"xnT#{ic}"], writes=[f"xnT#{ic}"])
    wq = p.alloc("l0f_wq", [128, NC_ * D], BF16)
    wstage = p.alloc("l0f_wst", [128, D], F32)
    wsum = p.alloc("wsum", [128, 4], F32)
    for ic in range(NC_):
        p.emit("sp", lambda e, ic=ic: e.dma_start(
            out=wstage.ap,
            in_=io["L0_ffn_weight"][ic * 128:(ic + 1) * 128, :]),
            reads=["l0f_wst"], writes=["l0f_wst"], ch="qsp")
        if ic == 0:
            p.emit("dve", lambda e: e.tensor_reduce(
                wsum[:, 0:1], wstage.ap, axis=AX.X,
                op=ALU.add, apply_absolute_value=True),
                reads=["l0f_wst"], writes=["wsum"])
        else:
            p.emit("dve", lambda e: e.tensor_reduce(
                wsum[:, 1:2], wstage.ap, axis=AX.X,
                op=ALU.add, apply_absolute_value=True),
                reads=["l0f_wst"], writes=["wsum"])
            p.emit("dve", lambda e: e.tensor_add(wsum[:, 0:1], wsum[:, 0:1],
                                                 wsum[:, 1:2]),
                   reads=["wsum"], writes=["wsum"])
    mk.part_reduce(wsum[:, 0:1], "wsum", ccbuf[0:1, 4:5], "ccbuf", ALU.add)
    p.emit("dve", lambda e: e.tensor_scalar_mul(ccbuf[0:1, 5:6], ccbuf[0:1, 4:5],
                                                1.0 / (D * D)),
           reads=["ccbuf"], writes=["ccbuf"])
    p.emit("dve", lambda e: e.tensor_scalar(ccbuf[0:1, 6:7], ccbuf[0:1, 5:6],
                                            1e-5, 0.5, ALU.add, ALU.mult),
           reads=["ccbuf"], writes=["ccbuf"])
    p.emit("dve", lambda e: e.tensor_mul(ccbuf[0:1, 7:8], ccbuf[0:1, 5:6],
                                         ccbuf[0:1, 0:1]),
           reads=["ccbuf"], writes=["ccbuf"])
    p.emit("dve", lambda e: e.tensor_scalar_mul(ccbuf[0:1, 8:9], ccbuf[0:1, 7:8],
                                                1.0 / 128.0),
           reads=["ccbuf"], writes=["ccbuf"])
    mk.bcast_scalar(ccbuf[0:1, 6:7], "ccbuf", mk.biasb[:, 1:2], "biasb")
    mk.bcast_scalar(ccbuf[0:1, 8:9], "ccbuf", mk.biasb[:, 2:3], "biasb")
    tmp = p.alloc("l0f_tmp", [128, D], F32)
    for ic in range(NC_):
        qsl = wq.t[:, ic * D:(ic + 1) * D]
        p.emit("sp", lambda e, ic=ic: e.dma_start(
            out=wstage.ap,
            in_=io["L0_ffn_weight"][ic * 128:(ic + 1) * 128, :]),
            reads=["l0f_wst"], writes=["l0f_wst"], ch="qsp")
        p.emit("dve", lambda e, qsl=qsl: e.tensor_scalar(
            qsl, wstage.ap, mk.biasb[:, 1:2], None, ALU.is_gt),
            reads=["l0f_wst", "biasb"], writes=[f"l0f_wq#{ic}"])
        p.emit("dve", lambda e: e.tensor_scalar_mul(tmp.ap, wstage.ap, -1.0),
               reads=["l0f_wst"], writes=["l0f_tmp"])
        p.emit("dve", lambda e: e.tensor_scalar(
            tmp.ap, tmp.ap, mk.biasb[:, 1:2], None, ALU.is_gt),
            reads=["l0f_tmp", "biasb"], writes=["l0f_tmp"])
        p.emit("dve", lambda e, qsl=qsl: e.tensor_sub(qsl, qsl, tmp.ap),
            reads=[f"l0f_wq#{ic}", "l0f_tmp"], writes=[f"l0f_wq#{ic}"])
    p.freebuf(wstage); p.freebuf(tmp)
    out_proj_residual(mk, xnT, wq, "xnT", "l0f_wq", None, 1e-5,
                      scale_ap=mk.biasb[:, 2:3], scale_key="biasb")
    p.freebuf(xnT); p.freebuf(wq); p.freebuf(gtab); p.freebuf(btab)
    p.freebuf(gm); p.freebuf(ccbuf); p.freebuf(wsum)


def block_l1_attn(mk):
    p = mk.p
    io = mk.io
    H = 16
    KSPAN = 17 * 128
    mk.const("l1a_cos"); mk.const("l1a_sinsw")
    XT = p.alloc("XT", [128, NC_ * S], BF16)
    mk.full_transpose_cast(
        lambda ic, t: mk.X[:, t * D + ic * 128: t * D + ic * 128 + 128],
        lambda ic, t: f"X#{t}", XT, lambda ic, t: f"XT#{ic}")
    mk.spill_x("l1a")

    cosb = mk.const("l1a_cos")
    sinswb = mk.const("l1a_sinsw")
    QT = p.alloc("QTa", [128, NC_ * S], BF16)
    KT = p.alloc("KTa", [128, NC_ * KSPAN], BF16)
    VW = H * 65
    NKT = 17
    rA, rB, rC = mk.rA, mk.rB, mk.rC
    p.emit("dve", lambda e: e.memset(KT.ap, 0.0),
           writes=[f"KTa#{ic}" for ic in range(NC_)])

    def load_cat_weight(wname, sem_dram, geo_dram, scale):
        wt = p.alloc(wname, [128, NC_ * D], BF16)
        stg = p.alloc(wname + "_stg", [128, 1024], F32)
        skey = wname + "_stg"
        for ic in range(NC_):
            p.emit("sp", lambda e, ic=ic: e.dma_start(
                out=stg[:, 0:512], in_=sem_dram[ic * 128:(ic + 1) * 128, :]),
                reads=[skey], writes=[skey], ch="qsp")
            p.emit("sp", lambda e, ic=ic: e.dma_start(
                out=stg[:, 512:1024], in_=geo_dram[ic * 128:(ic + 1) * 128, :]),
                reads=[skey], writes=[skey], ch="qsp")
            for part in range(2):
                srcap = bass.AP(stg.t, part * 512, [[1024, 128], [32, 16], [1, 32]])
                dstap = bass.AP(wt.t, ic * D + part * 32,
                                [[NC_ * D, 128], [64, 16], [1, 32]])
                if scale is None:
                    p.emit("dve", lambda e, srcap=srcap, dstap=dstap:
                           e.tensor_copy(dstap, srcap),
                           reads=[skey], writes=[f"{wname}#{ic}"])
                else:
                    p.emit("dve", lambda e, srcap=srcap, dstap=dstap, scale=scale:
                           e.tensor_scalar_mul(dstap, srcap, scale),
                           reads=[skey], writes=[f"{wname}#{ic}"])
        p.freebuf(stg)
        return wt

    def project_chunk(wt, wname, oc, b0=0):
        for tb in range(4):
            members = []
            for ic in range(NC_):
                lhs = wt.t[:, ic * D + oc * 128: ic * D + oc * 128 + 128]
                rhs = XT.t[:, ic * S + tb * 512: ic * S + tb * 512 + 512]
                members.append((lambda e, lhs=lhs, rhs=rhs, b=b0 + tb, ic=ic:
                                e.matmul(p.PS[:, b * 512:(b + 1) * 512], lhs, rhs,
                                         start=(ic == 0), stop=(ic == NC_ - 1)),
                                [f"{wname}#{ic}", f"XT#{ic}"]))
            mk.mm_group(f"ps{b0 + tb}", members)

    def rope_evac(dst, dname, oc, dcol, cos, sinsw, b0=0):
        pall = p.PS[:, b0 * 512: b0 * 512 + 2048]
        pk = [f"ps{b0 + j}" for j in range(4)]
        p.emit("dve", lambda e, pall=pall: e.tensor_mul(rA.ap, pall, cos.ap),
               reads=pk + [cos.name], writes=["ropeA"])
        p.emit("dve", lambda e, pall=pall: e.tensor_mul(rB.ap, pall, sinsw.ap),
               reads=pk + [sinsw.name], writes=["ropeB"])
        mask = [(i + 16) % 32 for i in range(32)]
        p.emit("dve", lambda e, mask=mask: e.stream_shuffle(rC.ap, rB.ap, mask),
               reads=["ropeB"], writes=["ropeC"])
        p.emit("dve", lambda e: e.tensor_add(dst.t[:, dcol:dcol + S], rA.ap, rC.ap),
               reads=["ropeA", "ropeC"], writes=[f"{dname}#{oc}"])

    sc = 1.0 / np.sqrt(32.0)
    Wq = load_cat_weight("WqcT", io["L1_attn_wq_sem"], io["L1_attn_wq_geo"], sc)
    for oc in range(NC_):
        project_chunk(Wq, "WqcT", oc, (oc % 2) * 4)
        if oc > 0:
            rope_evac(QT, "QTa", oc - 1, (oc - 1) * S, cosb, sinswb,
                      ((oc - 1) % 2) * 4)
    rope_evac(QT, "QTa", NC_ - 1, (NC_ - 1) * S, cosb, sinswb, ((NC_ - 1) % 2) * 4)
    p.freebuf(Wq)
    Wk = load_cat_weight("WkcT", io["L1_attn_wk_sem"], io["L1_attn_wk_geo"], None)
    for oc in range(NC_):
        project_chunk(Wk, "WkcT", oc, (oc % 2) * 4)
        if oc > 0:
            rope_evac(KT, "KTa", oc - 1, (oc - 1) * KSPAN, cosb, sinswb,
                      ((oc - 1) % 2) * 4)
    rope_evac(KT, "KTa", NC_ - 1, (NC_ - 1) * KSPAN, cosb, sinswb,
              ((NC_ - 1) % 2) * 4)
    p.freebuf(Wk)
    mk.free_const("l1a_cos"); mk.free_const("l1a_sinsw")
    head_stats(mk, QT, "QTa", 0)
    for ic in range(NC_):
        for part, nm in ((0, "L1_attn_k_sem_null"), (1, "L1_attn_k_geo_null")):
            for hh in range(2):
                h = 2 * ic + hh
                dst = KT.t[:, ic * KSPAN + 2048: ic * KSPAN + 2049]
                dstap = bass.AP(KT.t, (hh * 64 + part * 32) * (NC_ * KSPAN)
                                + ic * KSPAN + 2048,
                                [[NC_ * KSPAN, 32], [1, 1]])
                srcap = io[nm][h * 32:(h + 1) * 32]
                p.emit("pool", lambda e, dstap=dstap, srcap=srcap:
                       e.dma_start(out=dstap, in_=srcap.unsqueeze(1)),
                       reads=[f"KTa#{ic}"], writes=[f"KTa#{ic}"], ch="qpool")

    VpA = p.alloc("VpaA", [128, 8 * VW], BF16)
    VpB = p.alloc("VpaB", [128, 9 * VW], BF16)

    def vp_fn(kt):
        if kt < 8:
            return VpA, kt * VW, f"VpaA#{kt}"
        return VpB, (kt - 8) * VW, f"VpaB#{kt - 8}"
    for vb, vn, ntk in ((VpA, "VpaA", 8), (VpB, "VpaB", 9)):
        p.emit("dve", lambda e, vb=vb: e.memset(vb.ap, 0.0),
               writes=[f"{vn}#{kt}" for kt in range(ntk)])
        ones_ap = bass.AP(vb.t, 64, [[ntk * VW, 128], [VW, ntk], [65, H]])
        p.emit("dve", lambda e, ones_ap=ones_ap: e.memset(ones_ap, 1.0),
               writes=[f"{vn}#{kt}" for kt in range(ntk)])
    vchunk = p.alloc("vchunk", [128, S], F32)
    identf = mk.const("identf")
    WvT = mk.load_weight_N("WvTa", io["L1_attn_wv"])

    def v_evac_a(oc, b0):
        pall = p.PS[:, b0 * 512: b0 * 512 + 2048]
        pk = [f"ps{b0 + j}" for j in range(4)]
        p.emit("dve", lambda e, pall=pall: e.tensor_copy(vchunk.ap, pall),
               reads=pk, writes=["vchunk"])
        h0 = 2 * oc
        for g4 in range(4):
            bank = b0 + g4
            bk = f"ps{bank}"
            for j in range(4):
                kt = g4 * 4 + j
                ps = p.PS[:, bank * 512 + j * 128: bank * 512 + (j + 1) * 128]
                mk.transpose128(bk, ps, vchunk[:, kt * 128:(kt + 1) * 128],
                                ["vchunk"], identf, wover=(j > 0))
            vb, vcol0, _ = vp_fn(g4 * 4)
            nwid = 8 if g4 < 2 else 9
            srcap = bass.AP(p.PS, bank * 512,
                            [[4096, 128], [128, 4], [64, 2], [1, 64]])
            dstap = bass.AP(vb.t, vcol0 + h0 * 65,
                            [[nwid * VW, 128], [VW, 4], [65, 2], [1, 64]])
            p.emit("dve", lambda e, srcap=srcap, dstap=dstap:
                   e.tensor_copy(dstap, srcap),
                   reads=[bk], writes=[vp_fn(g4 * 4 + j)[2] for j in range(4)])

    for oc in range(NC_):
        project_chunk(WvT, "WvTa", oc, (oc % 2) * 4)
        if oc > 0:
            v_evac_a(oc - 1, ((oc - 1) % 2) * 4)
    v_evac_a(NC_ - 1, ((NC_ - 1) % 2) * 4)
    p.freebuf(WvT); p.freebuf(vchunk); p.freebuf(XT)
    vsrc = io["L1_attn_v_null"]
    vdst = bass.AP(VpB.t, 8 * VW, [[9 * VW, 1], [65, 16], [1, 64]])
    p.emit("pool", lambda e, vdst=vdst, vsrc=vsrc: e.dma_start(
        out=vdst, in_=vsrc.rearrange("(h d) -> h d", d=64).unsqueeze(0)),
        reads=["VpaB#8"], writes=["VpaB#8"], ch="qpool")

    head_stats(mk, KT, "KTa", 8, span=KSPAN)
    finish_bounds(mk, H)

    OnT = p.alloc("OnTa", [128, NC_ * S], BF16)
    mk.PTb = p.alloc("PTb", [128, 3072], BF16)
    kt_lists = []
    for qb in range(4):
        lst = [(kt, ("band" if kt * 128 - qb * 512 >= 0 else None),
                (kt * 128 - qb * 512 if kt * 128 - qb * 512 >= 0 else None))
               for kt in range(4 * (qb + 1))]
        lst.append((16, None, None))
        kt_lists.append(lst)
    attn_loop(mk, QT, KT, vp_fn, OnT, H, S, kt_lists, null_pad_tile=16,
              kspan=KSPAN, names=("QTa", "KTa", "Vpa", "OnTa"))
    p.freebuf(QT); p.freebuf(KT); p.freebuf(VpA); p.freebuf(VpB)
    p.freebuf(mk.PTb)
    del mk.PTb

    WoT = mk.load_weight_N("WoTa", io["L1_attn_wo"])
    mk.realloc_x()
    out_proj_residual(mk, OnT, WoT, "OnTa", "WoTa", "l1a", post_eps=1e-5)
    p.freebuf(OnT); p.freebuf(WoT)


def l1f_shiftbuf(mk, name):
    p = mk.p
    X = mk.X
    sb = p.alloc(name, [128, NT * 512], F32)
    p.emit("dve", lambda e, sb=sb: e.memset(sb.t[0:1, 0:512], 0.0),
           writes=[f"{name}#0"])
    for i in range(NT):
        eng, ch = p.hch()
        p.emit(eng, lambda e, i=i, sb=sb, X=X: e.dma_start(
            out=sb.t[1:128, i * 512:(i + 1) * 512],
            in_=X[0:127, i * D + 512:(i + 1) * D]),
            reads=[f"X#{i}"], writes=[f"{name}#{i}"], ch=ch)
        if i > 0:
            p.emit("sp", lambda e, i=i, sb=sb, X=X: e.dma_start(
                out=sb.t[0:1, i * 512:(i + 1) * 512],
                in_=X[127:128, (i - 1) * D + 512:i * D]),
                reads=[f"X#{i-1}"], writes=[f"{name}#{i}"], ch="qsp")
    return sb


def l1f_sub_attn(mk, l, n_sub):
    p = mk.p
    io = mk.io
    H, HD = 8, 64
    NCQ = 4
    wqkv = mk.load_weight_N(f"wqkv{l}", io["L1_ffn_wqkv"][l], O=1536)
    qs8 = p.alloc("qs8", [128, 512], F32)
    ks8 = p.alloc("ks8", [128, 512], F32)
    for tab, nm in ((qs8, "L1_ffn_q_scale"), (ks8, "L1_ffn_k_scale")):
        srcap = bass.AP(io[nm].tensor, l * 64, [[0, 128], [0, 8], [1, 64]])
        p.emit("sp", lambda e, tab=tab, srcap=srcap: e.dma_start(
            out=tab.ap, in_=srcap), writes=[tab.name], ch="qsp")
    sb = l1f_shiftbuf(mk, f"shb{l}")
    hT = p.alloc(f"hT{l}", [128, NC_ * S], BF16)
    def src_ap(ic, t):
        if ic < 4:
            return mk.X[:, t * D + ic * 128: t * D + ic * 128 + 128]
        return sb.t[:, t * 512 + (ic - 4) * 128: t * 512 + (ic - 4) * 128 + 128]
    def src_key(ic, t):
        return f"X#{t}" if ic < 4 else f"shb{l}#{t}"
    mk.full_transpose_cast(src_ap, src_key, hT, lambda ic, t: f"hT{l}#{ic}")
    p.freebuf(sb)
    if l == 0:
        mk.spill_x("l1f_in", free=False)
    mk.spill_x(f"l1f_s{l}a")

    xtabs = {}
    for nm in ("xq_c", "xq_s", "xk_c", "xk_s"):
        key = "dram_" + nm
        if key not in mk.cbuf:
            arr = mk.consts_np[nm].astype(np.float32)
            mk.cbuf[key] = mk.nc.inline_tensor(np.ascontiguousarray(arr),
                                               name=f"c{mk.uid()}_{nm}")
        xtabs[nm] = mk.cbuf[key]
    TBL = p.alloc("xtbl", [128, 2 * 4 * 512], F32)
    QTf = p.alloc(f"QTf{l}", [128, NCQ * S], BF16)
    KTf = p.alloc(f"KTf{l}", [128, NCQ * S], BF16)
    VW = H * 65
    Vp = p.alloc(f"Vpf{l}", [128, NT * VW], BF16)
    ones_ap = bass.AP(Vp.t, 64, [[NT * VW, 128], [VW, NT], [65, H]])
    p.emit("dve", lambda e: e.memset(ones_ap, 1.0),
           writes=[f"Vpf{l}#{kt}" for kt in range(NT)])
    nq = p.alloc("nq", [128, NT * 8], F32)
    nk = p.alloc("nk", [128, NT * 8], F32)
    sc1 = p.alloc("sc1", [128, 1024], F32)
    ss8 = p.alloc("ss8", [128, 48], F32)
    qh2 = p.alloc("qh2", [128, 1024], F32)
    qro = p.alloc("qro", [128, 1024], F32)
    identf = mk.const("identf")

    def emit_proj(i):
        b0 = (i % 2) * 3
        for c3 in range(3):
            members = []
            for ic in range(NC_):
                lhs = hT.t[:, ic * S + i * 128: ic * S + i * 128 + 128]
                rhs = wqkv.t[:, ic * 1536 + c3 * 512: ic * 1536 + (c3 + 1) * 512]
                members.append((lambda e, lhs=lhs, rhs=rhs, b=b0 + c3, ic=ic:
                                e.matmul(p.PS[:, b * 512:(b + 1) * 512], lhs, rhs,
                                         start=(ic == 0), stop=(ic == NC_ - 1)),
                                [f"hT{l}#{ic}", f"wqkv{l}#{ic}"]))
            mk.mm_group(f"ps{b0 + c3}", members)
        ts0 = (i % 2) * 2048
        for ti, nm in enumerate(("xq_c", "xq_s", "xk_c", "xk_s")):
            h64 = xtabs[nm]
            srcap = bass.AP(h64, i * 64, [[NT * 64, 128], [0, 8], [1, 64]])
            eng, ch = p.hch()
            p.emit(eng, lambda e, ti=ti, srcap=srcap, ts0=ts0: e.dma_start(
                out=TBL.t[:, ts0 + ti * 512: ts0 + (ti + 1) * 512], in_=srcap),
                reads=[f"xtbl#{i % 2}"], writes=[f"xtbl#{i % 2}"], ch=ch)

    def emit_process(i):
        b0 = (i % 2) * 3
        ts0 = (i % 2) * 2048
        chains = []
        for qk in range(2):
            ops = []
            bank = b0 + qk
            bk = f"ps{bank}"
            psl = p.PS[:, bank * 512:(bank + 1) * 512]
            s0 = qk * 512
            n0 = qk * 24
            sck, ssk, qhk, qrk = (f"sc1#{qk}", f"ss8#{qk}", f"qh2#{qk}",
                                  f"qro#{qk}")
            sca = sc1.t[:, s0:s0 + 512]
            qha = qh2.t[:, s0:s0 + 512]
            qra = qro.t[:, s0:s0 + 512]
            ops.append(lambda psl=psl, sca=sca, bk=bk, sck=sck: p.emit(
                "act", lambda e, psl=psl, sca=sca: e.activation(
                    sca, psl, AF.Square), reads=[bk], writes=[sck]))
            ops.append(lambda sca=sca, n0=n0, sck=sck, ssk=ssk: p.emit(
                "dve", lambda e, sca=sca, n0=n0: e.tensor_reduce(
                    ss8[:, n0:n0 + 8], sca.rearrange("p (h d) -> p h d", d=64),
                    axis=AX.X, op=ALU.add), reads=[sck], writes=[ssk]))
            ops.append(lambda n0=n0, ssk=ssk: p.emit(
                "act", lambda e, n0=n0: e.activation(
                    ss8[:, n0 + 8:n0 + 16], ss8[:, n0:n0 + 8], AF.Sqrt),
                reads=[ssk], writes=[ssk]))
            ops.append(lambda n0=n0, ssk=ssk: p.emit(
                "dve", lambda e, n0=n0: e.tensor_scalar_max(
                    ss8[:, n0 + 8:n0 + 16], ss8[:, n0 + 8:n0 + 16], 1e-12),
                reads=[ssk], writes=[ssk]))
            ops.append(lambda n0=n0, ssk=ssk: p.emit(
                "dve", lambda e, n0=n0: e.reciprocal(
                    ss8[:, n0 + 16:n0 + 24], ss8[:, n0 + 8:n0 + 16]),
                reads=[ssk], writes=[ssk]))
            for h in range(H):
                ops.append(lambda psl=psl, qha=qha, h=h, n0=n0, bk=bk,
                           ssk=ssk, qhk=qhk: p.emit(
                    "dve", lambda e, psl=psl, qha=qha, h=h, n0=n0: e.tensor_scalar(
                        qha[:, h * 64:(h + 1) * 64], psl[:, h * 64:(h + 1) * 64],
                        ss8[:, n0 + 16 + h:n0 + 17 + h], None, ALU.mult),
                    reads=[bk, ssk], writes=[qhk]))
            stab = qs8 if qk == 0 else ks8
            ops.append(lambda stab=stab, qha=qha, qhk=qhk: p.emit(
                "dve", lambda e, stab=stab, qha=qha: e.tensor_mul(
                    qha, qha, stab.ap), reads=[qhk, stab.name], writes=[qhk]))
            ops.append(lambda sca=sca, qha=qha, sck=sck, qhk=qhk: p.emit(
                "act", lambda e, sca=sca, qha=qha: e.activation(
                    sca, qha, AF.Square), reads=[qhk], writes=[sck]))
            nbuf, nname = (nq, "nq") if qk == 0 else (nk, "nk")
            ops.append(lambda nbuf=nbuf, sca=sca, sck=sck, nname=nname: p.emit(
                "dve", lambda e, nbuf=nbuf, sca=sca, i=i: e.tensor_reduce(
                    nbuf.t[:, i * 8:(i + 1) * 8],
                    sca.rearrange("p (h d) -> p h d", d=64),
                    axis=AX.X, op=ALU.add), reads=[sck], writes=[nname]))
            ct = TBL.t[:, ts0 + (qk * 2) * 512: ts0 + (qk * 2 + 1) * 512]
            st_ = TBL.t[:, ts0 + (qk * 2 + 1) * 512: ts0 + (qk * 2 + 2) * 512]
            swap_ap = bass.AP(qh2.t, s0 + 32,
                              [[1024, 128], [64, 8], [-32, 2], [1, 32]])
            ops.append(lambda ct=ct, qra=qra, qha=qha, qhk=qhk, qrk=qrk: p.emit(
                "dve", lambda e, ct=ct, qra=qra, qha=qha: e.tensor_mul(
                    qra, qha, ct), reads=[qhk, f"xtbl#{i % 2}"], writes=[qrk]))
            ops.append(lambda st_=st_, swap_ap=swap_ap, sca=sca, qhk=qhk,
                       sck=sck: p.emit(
                "dve", lambda e, st_=st_, swap_ap=swap_ap, sca=sca: e.tensor_tensor(
                    sca, swap_ap, st_, ALU.mult),
                reads=[qhk, f"xtbl#{i % 2}"], writes=[sck]))
            ops.append(lambda qra=qra, sca=sca, qrk=qrk, sck=sck: p.emit(
                "dve", lambda e, qra=qra, sca=sca: e.tensor_add(qra, qra, sca),
                reads=[qrk, sck], writes=[qrk]))
            TF = QTf if qk == 0 else KTf
            tfn = f"QTf{l}" if qk == 0 else f"KTf{l}"
            bank2 = 6 + qk
            bk2 = f"ps{bank2}"
            for c in range(4):
                ps2 = p.PS[:, bank2 * 512 + c * 128: bank2 * 512 + (c + 1) * 128]
                ops.append(lambda ps2=ps2, qra=qra, c=c, bk2=bk2, qrk=qrk:
                           mk.transpose128(bk2, ps2, qra[:, c * 128:(c + 1) * 128],
                                           [qrk], identf, wover=(c > 0)))
            srcap = bass.AP(p.PS, bank2 * 512, [[4096, 128], [1, 512]])
            dstap = bass.AP(TF.t, i * 128, [[NCQ * S, 128], [S, 4], [1, 128]])
            ops.append(lambda srcap=srcap, dstap=dstap, bk2=bk2, tfn=tfn:
                       mk.evac(dstap, srcap, reads=[bk2],
                               writes=[f"{tfn}#{c}" for c in range(4)]))
            chains.append(ops)
        for a, b in zip(chains[0], chains[1]):
            a()
            b()
        vb = b0 + 2
        vsrc = p.PS[:, vb * 512:(vb + 1) * 512].rearrange("p (h d) -> p h d", d=64)
        vdst = Vp.t[:, i * VW: i * VW + VW].rearrange("p (h d) -> p h d", d=65)[:, :, 0:64]
        p.emit("dve", lambda e, vsrc=vsrc, vdst=vdst: e.tensor_copy(vdst, vsrc),
               reads=[f"ps{vb}"], writes=[f"Vpf{l}#{i}"])

    for i in range(NT):
        emit_proj(i)
        if i > 0:
            emit_process(i - 1)
    emit_process(NT - 1)
    p.freebuf(hT); p.freebuf(wqkv); p.freebuf(TBL)
    p.freebuf(sc1); p.freebuf(qh2); p.freebuf(qro)
    p.freebuf(qs8); p.freebuf(ks8)

    for w in range(4):
        for qk in range(2):
            nbuf, nname = (nq, "nq") if qk == 0 else (nk, "nk")
            t0_ = 4 * w if qk == 0 else max(0, 4 * w - 4)
            t1_ = 4 * w + 4
            p.emit("dve", lambda e, nbuf=nbuf, t0_=t0_: e.tensor_copy(
                ss8[:, 0:8], nbuf.t[:, t0_ * 8:(t0_ + 1) * 8]),
                reads=[nname], writes=["ss8#0"])
            for t_ in range(t0_ + 1, t1_):
                p.emit("dve", lambda e, nbuf=nbuf, t_=t_: e.tensor_max(
                    ss8[:, 0:8], ss8[:, 0:8], nbuf.t[:, t_ * 8:(t_ + 1) * 8]),
                    reads=["ss8#0", nname], writes=["ss8#0"])
            mk.transpose128("ps7", p.PS[0:8, 7 * 512:7 * 512 + 128],
                            ss8[:, 0:8], ["ss8#0"], identf)
            p.emit("dve", lambda e: e.tensor_reduce(
                mk.bb[0:8, 0:1] if False else mk.bb[0:8, 16 + 0:17],
                p.PS[0:8, 7 * 512:7 * 512 + 128], axis=AX.X, op=ALU.max),
                reads=["ps7"], writes=["bb"])
            p.emit("sp", lambda e, w=w, qk=qk: e.dma_start(
                out=mk.brow[0:1, 64 + qk * 8: 64 + (qk + 1) * 8],
                in_=mk.bb[0:8, 16:17]),
                reads=["bb"], writes=["brow"], ch="qsp")
        p.emit("dve", lambda e, w=w: e.tensor_tensor(
            mk.brow[0:1, w * 16: w * 16 + 8], mk.brow[0:1, 64:72],
            mk.brow[0:1, 72:80], ALU.mult), reads=["brow"], writes=["brow"])
        p.emit("act", lambda e, w=w: e.activation(
            mk.brow[0:1, w * 16: w * 16 + 8], mk.brow[0:1, w * 16: w * 16 + 8],
            AF.Sqrt), reads=["brow"], writes=["brow"])
        p.emit("dve", lambda e, w=w: e.tensor_scalar_mul(
            mk.brow[0:1, w * 16: w * 16 + 8], mk.brow[0:1, w * 16: w * 16 + 8],
            8.0), reads=["brow"], writes=["brow"])
    p.freebuf(nq); p.freebuf(nk); p.freebuf(ss8)

    OnT = p.alloc(f"OnTf{l}", [128, NCQ * S], BF16)
    mk.PTb = p.alloc("PTb", [128, 3072], BF16)
    kt_lists = []
    for w in range(4):
        lst = []
        if w > 0:
            for kt in range(4 * (w - 1), 4 * w):
                lst.append((kt, "band_prev", kt * 128 - 512 * (w - 1)))
        for kt in range(4 * w, 4 * w + 4):
            lst.append((kt, "band", kt * 128 - 512 * w))
        kt_lists.append(lst)
    attn_loop(mk, QTf, KTf,
              lambda kt: (Vp, kt * VW, f"Vpf{l}#{kt}"), OnT, H, S, kt_lists,
              names=(f"QTf{l}", f"KTf{l}", f"Vpf{l}", f"OnTf{l}"),
              bias_col=lambda h, qb: qb * 16 + h)
    p.freebuf(QTf); p.freebuf(KTf); p.freebuf(Vp); p.freebuf(mk.PTb)
    del mk.PTb

    WoT = mk.load_weight_N(f"wof{l}", io["L1_ffn_wo"][l], O=D)
    mk.realloc_x()
    out_proj_residual(mk, OnT, WoT, f"OnTf{l}", f"wof{l}", f"l1f_s{l}a",
                      post_eps=None, nchunks=NCQ)
    p.freebuf(OnT); p.freebuf(WoT)


def l1f_sub_ffn(mk, l, n_sub):
    p = mk.p
    io = mk.io
    HID = 2730
    NU = 22
    sb = l1f_shiftbuf(mk, f"sh2b{l}")
    gtab = p.alloc("l1f_g", [128, D], F32)
    btab = p.alloc("l1f_b", [128, D], F32)
    for tab, nm in ((gtab, "L1_ffn_ln_g"), (btab, "L1_ffn_ln_b")):
        srcap = bass.AP(io[nm].tensor, l * D, [[0, 128], [1, D]])
        p.emit("sp", lambda e, tab=tab, srcap=srcap: e.dma_start(
            out=tab.ap, in_=srcap), writes=[tab.name], ch="qsp")
    lnhT = p.alloc(f"lnhT{l}", [128, NC_ * S], BF16)
    hln = p.alloc("hln", [128, 2 * D], F32)
    st = mk.stat
    epsb = mk.epsbuf(1e-5)
    identf = mk.const("identf")
    for i in range(NT):
        par = i % 2
        b0 = par * 16
        sk = f"stat#{par}"
        hk = f"hln#{par}"
        hla = hln.t[:, par * D:(par + 1) * D]
        x1ap = mk.X[:, i * D: i * D + 512]
        x2ap = sb.t[:, i * 512:(i + 1) * 512]
        xk = f"X#{i}"
        sk2 = f"sh2b{l}#{i}"
        p.emit("dve", lambda e, x1ap=x1ap, b0=b0: e.bn_stats(
            st[:, b0:b0 + 6], x1ap), reads=[xk], writes=[sk])
        p.emit("dve", lambda e, x2ap=x2ap, b0=b0: e.bn_stats(
            st[:, b0 + 6:b0 + 12], x2ap), reads=[sk2], writes=[sk])
        p.emit("dve", lambda e, b0=b0: e.bn_aggr(st[:, b0 + 12:b0 + 14],
                                                 st[:, b0:b0 + 12]),
               reads=[sk], writes=[sk])
        p.emit("act", lambda e, b0=b0: e.activation(
            st[:, b0 + 14:b0 + 15], st[:, b0 + 13:b0 + 14], AF.Sqrt,
            bias=epsb.ap), reads=[sk, epsb.name], writes=[sk])
        p.emit("dve", lambda e, b0=b0: e.reciprocal(st[:, b0 + 15:b0 + 16],
                                                    st[:, b0 + 14:b0 + 15]),
               reads=[sk], writes=[sk])
        p.emit("dve", lambda e, b0=b0: e.scalar_tensor_tensor(
            st[:, b0 + 12:b0 + 13], st[:, b0 + 12:b0 + 13], -1.0,
            st[:, b0 + 15:b0 + 16], ALU.mult, ALU.mult),
            reads=[sk], writes=[sk])
        p.emit("act", lambda e, x1ap=x1ap, hla=hla, b0=b0: e.activation(
            hla[:, 0:512], x1ap, AF.Identity, bias=st[:, b0 + 12:b0 + 13],
            scale=st[:, b0 + 15:b0 + 16]), reads=[xk, sk], writes=[hk])
        p.emit("act", lambda e, x2ap=x2ap, hla=hla, b0=b0: e.activation(
            hla[:, 512:1024], x2ap, AF.Identity, bias=st[:, b0 + 12:b0 + 13],
            scale=st[:, b0 + 15:b0 + 16]), reads=[sk2, sk], writes=[hk])
        p.emit("dve", lambda e, hla=hla: e.tensor_mul(hla, hla, gtab.ap),
               reads=[hk, gtab.name], writes=[hk])
        p.emit("dve", lambda e, hla=hla: e.tensor_add(hla, hla, btab.ap),
               reads=[hk, btab.name], writes=[hk])
        for ic0 in range(0, NC_, 4):
            bank = par * 2 + ic0 // 4
            bk = f"ps{bank}"
            for j in range(4):
                ic = ic0 + j
                ps = p.PS[:, bank * 512 + j * 128: bank * 512 + (j + 1) * 128]
                mk.transpose128(bk, ps, hla[:, ic * 128:(ic + 1) * 128],
                                [hk], identf, wover=(j > 0))
            srcap = bass.AP(p.PS, bank * 512, [[4096, 128], [1, 512]])
            dstap = bass.AP(lnhT.t, ic0 * S + i * 128,
                            [[NC_ * S, 128], [S, 4], [1, 128]])
            mk.evac(dstap, srcap, reads=[bk],
                    writes=[f"lnhT{l}#{ic0 + j}" for j in range(4)])
    p.freebuf(sb); p.freebuf(hln); p.freebuf(gtab); p.freebuf(btab)
    mk.spill_x(f"l1f_s{l}f")

    gatedT = p.alloc(f"gT{l}", [128, NU * 1024], BF16)
    w2b = p.alloc(f"w2b{l}", [128, NU * D], BF16)
    b2tab = p.alloc("l1f_b2", [128, D], F32)
    srcap = bass.AP(io["L1_ffn_b2"].tensor, l * D, [[0, 128], [1, D]])
    p.emit("sp", lambda e, srcap=srcap: e.dma_start(out=b2tab.ap, in_=srcap),
           writes=[b2tab.name], ch="qsp")
    w2st = p.alloc("w2st", [128, 2 * D], F32)
    for uc in range(NU):
        u0 = uc * 128
        usz = min(128, HID - u0)
        par = uc % 2
        skey = f"w2st#{par}"
        sta = w2st.t[0:usz, par * D:(par + 1) * D]
        eng, ch = p.hch()
        p.emit(eng, lambda e, u0=u0, usz=usz, sta=sta: e.dma_start(
            out=sta, in_=io["L1_ffn_w2"][l, u0:u0 + usz, :]),
            reads=[skey], writes=[skey], ch=ch)
        mk.evac(w2b.t[0:usz, uc * D:(uc + 1) * D], sta,
                reads=[skey], writes=[f"w2b{l}#{uc}"])
    p.freebuf(w2st)

    w1st = p.alloc("w1st", [128, 2 * NC_ * 128], F32)
    w1a = p.alloc("w1a", [128, 2 * NC_ * 128], BF16)
    w1g = p.alloc("w1g", [128, 2 * NC_ * 128], BF16)
    b1c = p.alloc("b1c", [128, 4], F32)
    gl = p.alloc("glscr", [128, 2048], F32)
    W1 = io["L1_ffn_w1"]
    B1 = io["L1_ffn_b1"]
    for half in range(2):
        tbase = half * 8

        def emit_h1(uc, tbase=tbase):
            u0 = uc * 128
            usz = min(128, HID - u0)
            par = uc % 2
            ws0 = par * 1024
            for side, coff in ((0, 0), (1, HID)):
                wbuf = (w1a, w1g)[side]
                wname = ("w1a", "w1g")[side] + f"#{par}"
                srcap = bass.AP(W1.tensor, l * (D * 5460) + coff + u0,
                                [[5460, 128], [128 * 5460, NC_], [1, usz]])
                eng, ch = p.hch()
                p.emit(eng, lambda e, srcap=srcap, usz=usz, ws0=ws0: e.dma_start(
                    out=bass.AP(w1st.t, ws0, [[2048, 128], [128, NC_], [1, usz]]),
                    in_=srcap), reads=[f"w1st#{par}"], writes=[f"w1st#{par}"],
                    ch=ch)
                p.emit("dve", lambda e, wbuf=wbuf, ws0=ws0: e.tensor_copy(
                    wbuf.t[:, ws0:ws0 + 1024], w1st.t[:, ws0:ws0 + 1024]),
                    reads=[f"w1st#{par}"], writes=[wname])
            p.emit("sp", lambda e, u0=u0, usz=usz, par=par: e.dma_start(
                out=b1c[0:usz, 2 * par:2 * par + 1],
                in_=B1[l, u0:u0 + usz].unsqueeze(1)),
                reads=[f"b1c#{par}"], writes=[f"b1c#{par}"], ch="qsp")
            p.emit("sp", lambda e, u0=u0, usz=usz, par=par: e.dma_start(
                out=b1c[0:usz, 2 * par + 1:2 * par + 2],
                in_=B1[l, HID + u0:HID + u0 + usz].unsqueeze(1)),
                reads=[f"b1c#{par}"], writes=[f"b1c#{par}"], ch="qsp")
            pb0 = par * 4
            for side in (0, 1):
                wbuf = (w1a, w1g)[side]
                wname = ("w1a", "w1g")[side] + f"#{par}"
                bb0 = pb0 + side * 2
                for tb in range(2):
                    members = []
                    for ic in range(NC_):
                        lhs = wbuf.t[:, ws0 + ic * 128: ws0 + ic * 128 + usz]
                        rhs = lnhT.t[:, ic * S + tbase * 128 + tb * 512:
                                     ic * S + tbase * 128 + (tb + 1) * 512]
                        members.append((lambda e, lhs=lhs, rhs=rhs, b=bb0 + tb,
                                        ic=ic, usz=usz:
                                        e.matmul(p.PS[0:usz, b * 512:(b + 1) * 512],
                                                 lhs, rhs, start=(ic == 0),
                                                 stop=(ic == NC_ - 1)),
                                        [wname, f"lnhT{l}#{ic}"]))
                    mk.mm_group(f"ps{bb0 + tb}", members)

        def emit_gate(uc):
            u0 = uc * 128
            usz = min(128, HID - u0)
            par = uc % 2
            pb0 = par * 4
            p.emit("act", lambda e, usz=usz, pb0=pb0, par=par: e.activation(
                gl[0:usz, par * 1024:(par + 1) * 1024],
                p.PS[0:usz, (pb0 + 2) * 512:(pb0 + 4) * 512], AF.Gelu,
                bias=b1c[0:usz, 2 * par + 1:2 * par + 2]),
                reads=[f"ps{pb0 + 2}", f"ps{pb0 + 3}", f"b1c#{par}"],
                writes=[f"glscr#{par}"])
            dstap = gatedT.t[0:usz, uc * 1024:(uc + 1) * 1024]
            p.emit("dve", lambda e, dstap=dstap, usz=usz, pb0=pb0, par=par:
                   e.scalar_tensor_tensor(
                       dstap, p.PS[0:usz, pb0 * 512:(pb0 + 2) * 512],
                       b1c[0:usz, 2 * par:2 * par + 1],
                       gl[0:usz, par * 1024:(par + 1) * 1024],
                       ALU.add, ALU.mult),
                   reads=[f"ps{pb0}", f"ps{pb0 + 1}", f"b1c#{par}",
                          f"glscr#{par}"], writes=[f"gT{l}#{uc}"])

        for uc in range(NU):
            emit_h1(uc)
            if uc > 0:
                emit_gate(uc - 1)
        emit_gate(NU - 1)
        out_proj_residual(
            mk, gatedT, w2b, f"gT{l}", f"w2b{l}", f"l1f_s{l}f",
            post_eps=None, bias_tab=b2tab,
            nchunks=NU, chunk_k=lambda ic: min(128, HID - ic * 128),
            tiles=range(tbase, tbase + 8), ont_span=1024,
            dst_spill=f"l1f_x{l}fd")
    p.freebuf(gatedT); p.freebuf(w1st); p.freebuf(w1a); p.freebuf(w1g)
    p.freebuf(b1c); p.freebuf(gl); p.freebuf(w2b); p.freebuf(b2tab)
    p.freebuf(lnhT)
    last = (l == n_sub - 1)
    mk.realloc_x()
    ln_pend = None
    for i in range(NT):
        mk.unspill_x_tile(f"l1f_x{l}fd", i)
        if last:
            xb = mk.xtra if i % 2 == 0 else mk.xtra_b
            xbk = "xtra" if i % 2 == 0 else "xtra_b"
            sp = mk.spill["l1f_in"]
            eng, ch = p.hch()
            p.emit(eng, lambda e, i=i, sp=sp, xb=xb: e.dma_start(
                out=xb.ap, in_=sp[i * 128:(i + 1) * 128, :]),
                reads=["sp_l1f_in#%d" % i], writes=[xbk], ch=ch)
            xap = mk.X[:, i * D:(i + 1) * D]
            p.emit("dve", lambda e, xap=xap, xb=xb: e.tensor_add(xap, xap, xb.ap),
                   reads=[f"X#{i}", xbk], writes=[f"X#{i}"])
            fa, fb = mk.ln_tile_stages(i, 1e-5)
            fa()
            if ln_pend is not None:
                ln_pend()
            ln_pend = fb
    if ln_pend is not None:
        ln_pend()


def block_l1_ffn(mk):
    p = mk.p
    if mk.rA is not None:
        p.freebuf(mk.rA); p.freebuf(mk.rB); p.freebuf(mk.rC)
        mk.rA = mk.rB = mk.rC = None
    mk.xtra = p.alloc("xtra", [128, D], F32, side="high")
    mk.xtra2 = p.alloc("xtra2", [128, D], F32, side="high")
    mk.xtra_b = p.alloc("xtra_b", [128, D], F32, side="high")
    mk.xtra2_b = p.alloc("xtra2_b", [128, D], F32, side="high")
    for l in range(2):
        l1f_sub_attn(mk, l, 2)
        l1f_sub_ffn(mk, l, 2)


def build_nc(upto="all", n_cores=8):
    nc = bass.Bass("TRN2", target_bir_lowering=False, debug=False)
    names_2d = {
        "L0_attn_Wq": (D, D), "L0_attn_Wk": (D, D), "L0_attn_Wv": (D, D),
        "L0_attn_Wo": (D, D), "L0_ffn_weight": (D, D),
        "L1_attn_wq_sem": (D, 512), "L1_attn_wk_sem": (D, 512),
        "L1_attn_wq_geo": (D, 512), "L1_attn_wk_geo": (D, 512),
        "L1_attn_wv": (D, D), "L1_attn_wo": (D, D),
    }
    io = {}
    io["x"] = nc.dram_tensor("x", [S, D], F32, kind="ExternalInput").ap()
    for n, shp in names_2d.items():
        io[n] = nc.dram_tensor(n, list(shp), F32, kind="ExternalInput").ap()
    for n, shp in {
        "L0_ffn_ln_gamma": (D,), "L0_ffn_ln_beta": (D,),
        "L1_attn_k_sem_null": (512,), "L1_attn_k_geo_null": (512,),
        "L1_attn_v_null": (D,),
    }.items():
        io[n] = nc.dram_tensor(n, list(shp), F32, kind="ExternalInput").ap()
    for n, shp in {
        "L1_ffn_wqkv": (2, D, 1536), "L1_ffn_q_scale": (2, 64),
        "L1_ffn_k_scale": (2, 64), "L1_ffn_wo": (2, 512, D),
        "L1_ffn_ln_g": (2, D), "L1_ffn_ln_b": (2, D),
        "L1_ffn_w1": (2, D, 5460), "L1_ffn_b1": (2, 5460),
        "L1_ffn_w2": (2, 2730, D), "L1_ffn_b2": (2, D),
    }.items():
        io[n] = nc.dram_tensor(n, list(shp), F32, kind="ExternalInput").ap()
    io["out"] = nc.dram_tensor("out", [S, D], F32, kind="ExternalOutput").ap()

    mk = MK(nc, io)
    X0 = mk.X
    for i in range(NT):
        mk.p.emit("sp", lambda e, i=i, X0=X0: e.dma_start(
            out=X0[:, i * D:(i + 1) * D], in_=io["x"][i * 128:(i + 1) * 128, :]),
            writes=[f"X#{i}"], ch="qsp")
    block_l0_attn(mk)
    if upto == "l0a":
        mk.write_out()
        mk.finalize()
        return nc, mk
    block_l0_ffn(mk, n_cores=n_cores)
    if upto == "l0f":
        mk.write_out()
        mk.finalize()
        return nc, mk
    block_l1_attn(mk)
    if upto == "l1a":
        mk.write_out()
        mk.finalize()
        return nc, mk
    block_l1_ffn(mk)
    mk.write_out()
    mk.finalize()
    return nc, mk


def kernel(**inputs):
    n_cores = 8
    nc, mk = build_nc(upto="all", n_cores=n_cores)
    x_full = np.ascontiguousarray(inputs["L0_attn_x"], dtype=np.float32)
    weight_names = [
        "L0_attn_Wq", "L0_attn_Wk", "L0_attn_Wv", "L0_attn_Wo",
        "L0_ffn_weight", "L0_ffn_ln_gamma", "L0_ffn_ln_beta",
        "L1_attn_wq_sem", "L1_attn_wk_sem", "L1_attn_wq_geo", "L1_attn_wk_geo",
        "L1_attn_wv", "L1_attn_wo", "L1_attn_k_sem_null", "L1_attn_k_geo_null",
        "L1_attn_v_null", "L1_ffn_wqkv", "L1_ffn_q_scale", "L1_ffn_k_scale",
        "L1_ffn_wo", "L1_ffn_ln_g", "L1_ffn_ln_b", "L1_ffn_w1", "L1_ffn_b1",
        "L1_ffn_w2", "L1_ffn_b2",
    ]
    wmap = {n: np.ascontiguousarray(inputs[n], dtype=np.float32)
            for n in weight_names}
    in_maps = []
    for c in range(n_cores):
        m = {"x": np.ascontiguousarray(x_full[c])}
        m.update(wmap)
        in_maps.append(m)
    res = run_bass_kernel_spmd(nc, in_maps, core_ids=list(range(n_cores)))
    out = np.stack([np.asarray(r["out"], dtype=np.float32)
                    for r in res.results], axis=0)
    return out


# version 17
# speedup vs baseline: 1.0351x; 1.0351x over previous
import numpy as np
import ml_dtypes
from contextlib import ExitStack

import concourse.bass as bass
import concourse.mybir as mybir
from concourse.bass_utils import run_bass_kernel_spmd

F32 = mybir.dt.float32
BF16 = mybir.dt.bfloat16
AF = mybir.ActivationFunctionType
ALU = mybir.AluOpType
AX = mybir.AxisListType

S = 2048
D = 1024
NT = S // 128
NC_ = D // 128
NEG = -1.0e30

ENGINES = ("pe", "dve", "act", "pool", "sp")
DMA_CH = ("qsp", "qpool", "qact")
DMA_SLOTS = {"qsp": 8, "qpool": 8, "qact": 8}
DMA_SLOT_NAMES = tuple(f"{q}_{i}" for q in DMA_CH for i in range(DMA_SLOTS[q]))
ALL_CH = ENGINES + DMA_SLOT_NAMES + ("cc",)
SBUF_LO = 16512
SBUF_HI = 229312
X_FIXED = 127296


def host_constants():
    c = {}
    c["identf"] = np.eye(128, dtype=np.float32)
    c["identb"] = np.eye(128, dtype=np.float32)
    t = np.arange(S, dtype=np.float64)

    dk = 64
    inv = 1.0 / (10000.0 ** (np.arange(0, dk, 2, dtype=np.float64) / dk))
    ang = np.outer(t, inv)
    cos, sin = np.cos(ang), np.sin(ang)
    cf = np.zeros((64, S)); ssw = np.zeros((64, S))
    cf[0::2] = cos.T; cf[1::2] = cos.T
    ssw[0::2] = sin.T; ssw[1::2] = -sin.T
    c["l0_cos"] = np.tile(cf, (2, 1)).astype(np.float32)
    c["l0_sinsw"] = np.tile(ssw, (2, 1)).astype(np.float32)

    ghd = 32
    ginv = 1.0 / (10000.0 ** (np.arange(0, ghd, 2, dtype=np.float64) / ghd))
    gang = np.outer(t, ginv)
    gcos, gsin = np.cos(gang), np.sin(gang)
    hc = np.zeros((64, S)); hs = np.zeros((64, S))
    hc[0:32] = 1.0
    hc[32:48] = gcos.T; hc[48:64] = gcos.T
    hs[32:48] = gsin.T; hs[48:64] = -gsin.T
    c["l1a_cos"] = np.tile(hc, (2, 1)).astype(np.float32)
    c["l1a_sinsw"] = np.tile(hs, (2, 1)).astype(np.float32)

    dh = 64
    half = np.arange(0, dh, 2, dtype=np.float64)
    xfreq = np.outer(t, 1.0 / (10000.0 ** (half / dh)))
    xf2 = np.concatenate([xfreq, xfreq], axis=1)
    base = (half + 0.4 * dh) / (1.4 * dh)
    power = (t - S // 2) / 256.0
    scl = base[None, :] ** power[:, None]
    scl2 = np.concatenate([scl, scl], axis=1)
    xc, xs_ = np.cos(xf2), np.sin(xf2)
    qc = xc * scl2 * 8.0
    qs_ = np.concatenate([-(xs_[:, :32] * scl), (xs_[:, 32:] * scl)], axis=1) * 8.0
    kc = xc / scl2
    ks_ = np.concatenate([-(xs_[:, :32] / scl), (xs_[:, 32:] / scl)], axis=1)
    for nm, tab in (("xq_c", qc), ("xq_s", qs_), ("xk_c", kc), ("xk_s", ks_)):
        m = np.zeros((128, NT * dh), dtype=np.float32)
        for i in range(NT):
            m[:, i * dh:(i + 1) * dh] = tab[i * 128:(i + 1) * 128]
        c[nm] = m

    j = np.arange(-512, 512)[None, :]
    k = np.arange(128)[:, None]
    c["band"] = np.where(k <= j, 0.0, NEG).astype(np.float32)
    c["band1"] = np.where(k <= j + 1, 0.0, NEG).astype(np.float32)
    c["band_prev"] = np.where(k >= j, 0.0, NEG).astype(np.float32)
    c["padcol"] = np.concatenate(
        [np.zeros((1, 1)), np.full((127, 1), NEG)]).astype(np.float32)
    c["padcol64"] = np.tile(c["padcol"], (1, 64))
    c["blockdiag2"] = np.zeros((128, 128), dtype=np.float32)
    c["blockdiag2"][0:64, 0] = 1.0
    c["blockdiag2"][64:128, 1] = 1.0
    c["ones_1x128"] = np.ones((1, 128), dtype=np.float32)
    c["ones128"] = np.ones((128, 128), dtype=np.float32)
    c["neg_ones_1x128"] = np.full((1, 128), -1.0, dtype=np.float32)
    return c


class Buf:
    __slots__ = ("name", "t", "off", "nbytes", "ghosts")
    def __init__(self, name, t, off, nbytes, ghosts):
        self.name, self.t, self.off, self.nbytes, self.ghosts = name, t, off, nbytes, ghosts
    def __getitem__(self, idx):
        return self.t[idx]
    @property
    def ap(self):
        return self.t[:]


class Prog:
    def __init__(self, nc):
        self.nc = nc
        self.es = ExitStack()
        self.ops = {e: [] for e in ENGINES}
        self.sem = {n: self.es.enter_context(nc.semaphore("sem_" + n)) for n in ALL_CH}
        self.cnt = {n: 0 for n in ALL_CH}
        self.dma_rr = {q: 0 for q in DMA_CH}
        self.clockat = {}
        self.eng_clock = {e: {} for e in ENGINES}
        self.last_w = {}
        self.readers = {}
        self.free = [(SBUF_LO, SBUF_HI)]
        self.ghost = []
        self.high_water = 0
        self.bufs = {}
        self.PS = self.es.enter_context(nc.psum_tensor("PS", [128, 4096], F32))
        self.n_ops = 0
        self.n_waits = 0

    def alloc(self, name, shape, dtype, align=64, side="low", fixed_off=None):
        assert name not in self.bufs, name
        itemsize = 2 if dtype == BF16 else 4
        per_part = int(np.prod(shape[1:])) * itemsize
        per_part = -(-per_part // align) * align
        cands = []
        for i, (a, b) in enumerate(self.free):
            if fixed_off is not None:
                if a <= fixed_off and fixed_off + per_part <= b:
                    cands.append((0, i, fixed_off))
            elif side == "high":
                aa = ((b - per_part) // align) * align
                if aa >= a:
                    cands.append((-b, i, aa))
            else:
                aa = -(-a // align) * align
                if b - aa >= per_part:
                    key = -(b - aa) if per_part >= 24576 else (b - aa)
                    cands.append((key, i, aa))
        cands.sort()
        for _, i, aa in cands[:1]:
            a, b = self.free[i]
            if True:
                del self.free[i]
                if aa > a:
                    self.free.insert(i, (a, aa)); i += 1
                if aa + per_part < b:
                    self.free.insert(i, (aa + per_part, b))
                ghosts = []
                new_ghost = []
                for (s0, e0, kl) in self.ghost:
                    if s0 < aa + per_part and e0 > aa:
                        ghosts.extend(kl)
                        if s0 < aa:
                            new_ghost.append((s0, aa, kl))
                        if e0 > aa + per_part:
                            new_ghost.append((aa + per_part, e0, kl))
                    else:
                        new_ghost.append((s0, e0, kl))
                self.ghost = new_ghost
                t = self.nc.alloc_sbuf_tensor_at(name, list(shape), dtype, offset=aa)
                b_ = Buf(name, t, aa, per_part, ghosts)
                self.bufs[name] = b_
                self.high_water = max(self.high_water, self.used())
                return b_
        raise MemoryError(f"SBUF alloc fail {name} ({per_part}B); free={self.free}")

    def used(self):
        return (SBUF_HI - SBUF_LO) - sum(b - a for a, b in self.free)

    def freebuf(self, buf):
        if isinstance(buf, str):
            buf = self.bufs[buf]
        del self.bufs[buf.name]
        pref = buf.name + "#"
        keys = [k for k in (set(self.last_w) | set(self.readers))
                if k == buf.name or k.startswith(pref)]
        self.ghost.append((buf.off, buf.off + buf.nbytes, keys))
        self.free.append((buf.off, buf.off + buf.nbytes))
        self.free.sort()
        merged = []
        for a, b in self.free:
            if merged and a <= merged[-1][1]:
                merged[-1] = (merged[-1][0], max(merged[-1][1], b))
            else:
                merged.append((a, b))
        self.free = merged

    def _wait(self, eng, pts):
        need = {}
        ec = self.eng_clock[eng]
        for ch, v in pts:
            if ec.get(ch, 0) < v:
                need[ch] = max(need.get(ch, 0), v)
        for ch, v in sorted(need.items()):
            self.ops[eng].append(("wait", ch, v))
            self.n_waits += 1
            ec[ch] = max(ec.get(ch, 0), v)
            for c2, v2 in self.clockat.get((ch, v), {}).items():
                ec[c2] = max(ec.get(c2, 0), v2)

    def emit(self, eng, fn, reads=(), writes=(), inc=True, ch=None, wover=()):
        ch = ch or eng
        slot_gate = None
        if ch in DMA_CH:
            q = ch
            ch = f"{q}_{self.dma_rr[q] % DMA_SLOTS[q]}"
            self.dma_rr[q] += 1
            if self.cnt[ch] > 0:
                slot_gate = (ch, self.cnt[ch])
        step = 16 if ch.startswith("q") else 1
        reads = [r for r in reads if r]
        writes = [w for w in writes if w]
        wover = [w for w in wover if w]
        ghost_extra = []
        for w in writes + wover:
            b = self.bufs.get(w.split("#")[0])
            if b and b.ghosts:
                ghost_extra.extend(g for g in b.ghosts
                                   if g in self.last_w or g in self.readers)
        pts = []
        for kk in reads:
            w = self.last_w.get(kk)
            if w:
                pts.append(w)
        for kk in writes + ghost_extra:
            w = self.last_w.get(kk)
            if w:
                pts.append(w)
            pts.extend(self.readers.get(kk, ()))
        if slot_gate is not None:
            pts.append(slot_gate)
        self._wait(eng, pts)
        pub = self.cnt[ch] + step
        if inc:
            self.cnt[ch] = pub
            self.clockat[(ch, pub)] = dict(self.eng_clock[eng])
        self.ops[eng].append(("op", fn, self.sem[ch] if inc else None, step))
        for kk in writes + wover:
            self.last_w[kk] = (ch, pub)
            self.readers[kk] = []
        for kk in reads:
            self.readers.setdefault(kk, []).append((ch, pub))
        self.n_ops += 1

    def hch(self):
        self.hflip = (getattr(self, "hflip", 0) + 1) % 3
        return (("sp", "qsp"), ("act", "qact"),
                ("pool", "qpool"))[self.hflip]

    def wait_all(self, eng):
        self._wait(eng, [(ch, self.cnt[ch]) for ch in ALL_CH if self.cnt[ch] > 0])

    def finalize(self):
        with self.nc.Block() as block:
            attr = {"pe": "tensor", "dve": "vector", "act": "scalar",
                    "pool": "gpsimd", "sp": "sync"}
            for eng in ENGINES:
                ops = self.ops[eng]
                sems = self.sem
                def body(e, ops=ops, sems=sems):
                    for item in ops:
                        if item[0] == "wait":
                            e.wait_ge(sems[item[1]], item[2])
                        else:
                            _, fn, sem_, step = item
                            ins = fn(e)
                            if sem_ is not None:
                                ins.then_inc(sem_, step)
                getattr(block, attr[eng])(body)


class MK:
    def __init__(self, nc, io):
        self.nc = nc
        self.io = io
        self.p = Prog(nc)
        self.consts_np = host_constants()
        self.cbuf = {}
        self._uid = 0
        self.evac_flip = 0
        p = self.p
        self.eps = {}
        self.X = p.alloc("X", [128, NT * D], F32, fixed_off=X_FIXED)
        self.spill = {}
        self.stat = p.alloc("stat", [128, 32], F32, side="high")
        self.bb = p.alloc("bb", [128, 24], F32, side="high")
        self.brow = p.alloc("brow", [1, 80], F32, side="high")
        self.biasb = p.alloc("biasb", [128, 4], F32, side="high")
        self.biasball = p.alloc("biasball", [128, 128], F32, side="high")
        self.rA = p.alloc("ropeA", [128, S], F32, side="high")
        self.rB = p.alloc("ropeB", [128, S], F32, side="high")
        self.rC = p.alloc("ropeC", [128, S], F32, side="high")
        for nm in ("identf", "ones_1x128", "neg_ones_1x128", "padcol", "padcol64"):
            self.const(nm)
        self.epsbuf(1e-3)
        self.epsbuf(1e-5)
        for nm in ("identb", "blockdiag2", "band", "band_prev"):
            self.const(nm, BF16)
        self.rzb = p.alloc("rzb", [128, 1024], F32, side="high")

    def uid(self):
        self._uid += 1
        return self._uid

    def const(self, name, dtype=F32, data=None):
        key = (name, str(dtype))
        if key in self.cbuf:
            return self.cbuf[key]
        arr = self.consts_np[name] if data is None else data
        arr = arr.astype(ml_dtypes.bfloat16 if dtype == BF16 else np.float32)
        h = self.nc.inline_tensor(np.ascontiguousarray(arr), name=f"c{self.uid()}_{name}")
        b = self.p.alloc(f"C{self.uid()}_{name}", list(arr.shape), dtype, side="high")
        self.p.emit("sp", lambda e, b=b, h=h: e.dma_start(out=b.ap, in_=h.ap()),
                    writes=[b.name], ch="qsp")
        self.cbuf[key] = b
        return b

    def free_const(self, name, dtype=F32):
        key = (name, str(dtype))
        if key in self.cbuf:
            self.p.freebuf(self.cbuf.pop(key))

    def epsbuf(self, val):
        if val in self.eps:
            return self.eps[val]
        b = self.p.alloc(f"eps{self.uid()}", [128, 1], F32, side="high")
        self.p.emit("dve", lambda e, b=b, v=val: e.memset(b.ap, v), writes=[b.name])
        self.eps[val] = b
        return b

    def evac(self, dst_ap, src_ap, reads, writes, scale=None, eng=None):
        if eng is None:
            eng = ("dve", "act")[self.evac_flip % 2]
            self.evac_flip += 1
        if eng == "dve":
            if scale is None:
                self.p.emit("dve", lambda e: e.tensor_copy(dst_ap, src_ap),
                            reads=reads, writes=writes)
            else:
                self.p.emit("dve", lambda e: e.tensor_scalar_mul(dst_ap, src_ap, scale),
                            reads=reads, writes=writes)
        else:
            if scale is None:
                self.p.emit("act", lambda e: e.copy(dst_ap, src_ap),
                            reads=reads, writes=writes)
            else:
                self.p.emit("act", lambda e: e.mul(dst_ap, src_ap, scale),
                            reads=reads, writes=writes)

    def mm_group(self, bank_key, members):
        n = len(members)
        for i, (fn, reads) in enumerate(members):
            first, last = i == 0, i == n - 1
            self.p.emit("pe", fn, reads=reads,
                        writes=[bank_key] if first else (),
                        wover=[bank_key] if (last and not first) else (),
                        inc=last)

    def transpose128(self, bank_key, out_ap, in_ap, reads, ident, wover=False):
        self.p.emit("pe", lambda e: e.transpose(out_ap, in_ap, ident.ap),
                    reads=list(reads) + [ident.name],
                    writes=() if wover else [bank_key],
                    wover=[bank_key] if wover else (), inc=True)

    def full_transpose_cast(self, src_ap_fn, src_key_fn, dst, dst_key_fn, scale=None):
        identf = self.const("identf")
        p = self.p
        g = 0
        for t_ in range(NT):
            for ic0 in range(0, NC_, 4):
                bank = g % 4
                g += 1
                bk = f"ps{bank}"
                for j in range(4):
                    ic = ic0 + j
                    ps = p.PS[:, bank * 512 + j * 128: bank * 512 + (j + 1) * 128]
                    self.transpose128(bk, ps, src_ap_fn(ic, t_),
                                      [src_key_fn(ic, t_)], identf,
                                      wover=(j > 0))
                srcap = bass.AP(p.PS, bank * 512, [[4096, 128], [1, 512]])
                dstap = bass.AP(dst.t, ic0 * S + t_ * 128,
                                [[NC_ * S, 128], [S, 4], [1, 128]])
                self.evac(dstap, srcap, reads=[bk],
                          writes=[dst_key_fn(ic0 + j, t_) for j in range(4)],
                          scale=scale)

    def load_weight_T(self, wname, dram, scale=None):
        p = self.p
        O, I = dram.shape[0], dram.shape[1]
        noc, nic = O // 128, I // 128
        wt = p.alloc(wname, [128, nic * O], BF16)
        stage = p.alloc(wname + "_st", [128, 2 * I], F32)
        identf = self.const("identf")
        for oc in range(noc):
            par = oc % 2
            skey = f"{wname}_st#{par}"
            sta = stage.t[:, par * I:(par + 1) * I]
            eng, ch = p.hch()
            p.emit(eng, lambda e, oc=oc, sta=sta, dram=dram:
                   e.dma_start(out=sta, in_=dram[oc * 128:(oc + 1) * 128, :]),
                   reads=[skey], writes=[skey], ch=ch)
            for ic in range(nic):
                bank = (oc * nic + ic) % 4
                bk = f"ps{bank}"
                ps = p.PS[:, bank * 512: bank * 512 + 128]
                self.transpose128(bk, ps, sta[:, ic * 128:(ic + 1) * 128],
                                  [skey], identf)
                dcol = ic * O + oc * 128
                self.evac(wt.t[:, dcol:dcol + 128], ps,
                          reads=[bk], writes=[f"{wname}#{ic}"], scale=scale)
        p.freebuf(stage)
        return wt

    def load_weight_N(self, wname, dram, scale=None, O=None):
        p = self.p
        if O is None:
            I, O = dram.shape[0], dram.shape[1]
        else:
            I = dram.shape[0]
        nic = I // 128
        wt = p.alloc(wname, [128, nic * O], BF16)
        stage = p.alloc(wname + "_st", [128, 2 * O], F32)
        for ic in range(nic):
            par = ic % 2
            skey = f"{wname}_st#{par}"
            sta = stage.t[:, par * O:(par + 1) * O]
            eng, ch = p.hch()
            p.emit(eng, lambda e, ic=ic, sta=sta, dram=dram:
                   e.dma_start(out=sta, in_=dram[ic * 128:(ic + 1) * 128, :]),
                   reads=[skey], writes=[skey], ch=ch)
            self.evac(wt.t[:, ic * O:(ic + 1) * O], sta,
                      reads=[skey], writes=[f"{wname}#{ic}"], scale=scale)
        p.freebuf(stage)
        return wt

    def spill_x(self, tag, free=True):
        sp = self.nc.dram_tensor(f"spill_{tag}", [S, D], F32, kind="Internal").ap()
        self.spill[tag] = sp
        X = self.X
        for i in range(NT):
            self.p.emit("pool", lambda e, i=i, sp=sp, X=X:
                        e.dma_start(out=sp[i * 128:(i + 1) * 128, :],
                                    in_=X[:, i * D:(i + 1) * D]),
                        reads=[f"X#{i}"], writes=[f"sp_{tag}#{i}"], ch="qpool")
        if free:
            self.p.freebuf(self.X)
            self.X = None

    def realloc_x(self):
        assert self.X is None
        try:
            self.X = self.p.alloc("X", [128, NT * D], F32, fixed_off=X_FIXED)
        except MemoryError:
            self.X = self.p.alloc("X", [128, NT * D], F32)

    def unspill_x_tile(self, tag, i):
        sp = self.spill[tag]
        X = self.X
        eng, ch = self.p.hch()
        self.p.emit(eng, lambda e, i=i, sp=sp, X=X:
                    e.dma_start(out=X[:, i * D:(i + 1) * D],
                                in_=sp[i * 128:(i + 1) * 128, :]),
                    reads=[f"sp_{tag}#{i}"], writes=[f"X#{i}"], ch=ch)

    def ln_tile_stages(self, i, eps):
        p = self.p
        st = self.stat
        b0 = (i % 2) * 16
        sk = f"stat#{i % 2}"
        epsb = self.epsbuf(eps)
        xap = self.X[:, i * D:(i + 1) * D]
        xk = f"X#{i}"

        def fa():
            for ch_ in range(2):
                a = self.X[:, i * D + ch_ * 512: i * D + (ch_ + 1) * 512]
                p.emit("dve", lambda e, a=a, ch_=ch_, b0=b0: e.bn_stats(
                    st[:, b0 + ch_ * 6: b0 + (ch_ + 1) * 6], a),
                    reads=[xk], writes=[sk])
            p.emit("dve", lambda e, b0=b0: e.bn_aggr(st[:, b0 + 12:b0 + 14],
                                                     st[:, b0:b0 + 12]),
                   reads=[sk], writes=[sk])
            p.emit("act", lambda e, b0=b0: e.activation(
                st[:, b0 + 14:b0 + 15], st[:, b0 + 13:b0 + 14], AF.Sqrt,
                bias=epsb.ap), reads=[sk, epsb.name], writes=[sk])

        def fb():
            p.emit("dve", lambda e, b0=b0: e.reciprocal(st[:, b0 + 15:b0 + 16],
                                                        st[:, b0 + 14:b0 + 15]),
                   reads=[sk], writes=[sk])
            p.emit("dve", lambda e, b0=b0: e.scalar_tensor_tensor(
                st[:, b0 + 12:b0 + 13], st[:, b0 + 12:b0 + 13], -1.0,
                st[:, b0 + 15:b0 + 16], ALU.mult, ALU.mult),
                reads=[sk], writes=[sk])
            p.emit("act", lambda e, b0=b0: e.activation(
                xap, xap, AF.Identity, bias=st[:, b0 + 12:b0 + 13],
                scale=st[:, b0 + 15:b0 + 16]),
                reads=[xk, sk], writes=[xk])
        return fa, fb

    def ln_tile(self, i, eps, gamma=None, beta=None):
        p = self.p
        st = self.stat
        b0 = (i % 2) * 16
        sk = f"stat#{i % 2}"
        epsb = self.epsbuf(eps)
        xap = self.X[:, i * D:(i + 1) * D]
        xk = f"X#{i}"
        for ch_ in range(2):
            a = self.X[:, i * D + ch_ * 512: i * D + (ch_ + 1) * 512]
            p.emit("dve", lambda e, a=a, ch_=ch_, b0=b0: e.bn_stats(
                st[:, b0 + ch_ * 6: b0 + (ch_ + 1) * 6], a),
                reads=[xk], writes=[sk])
        p.emit("dve", lambda e, b0=b0: e.bn_aggr(st[:, b0 + 12:b0 + 14],
                                                 st[:, b0:b0 + 12]),
               reads=[sk], writes=[sk])
        p.emit("act", lambda e, b0=b0: e.activation(
            st[:, b0 + 14:b0 + 15], st[:, b0 + 13:b0 + 14], AF.Sqrt,
            bias=epsb.ap), reads=[sk, epsb.name], writes=[sk])
        p.emit("dve", lambda e, b0=b0: e.reciprocal(st[:, b0 + 15:b0 + 16],
                                                    st[:, b0 + 14:b0 + 15]),
               reads=[sk], writes=[sk])
        p.emit("dve", lambda e, b0=b0: e.scalar_tensor_tensor(
            st[:, b0 + 12:b0 + 13], st[:, b0 + 12:b0 + 13], -1.0,
            st[:, b0 + 15:b0 + 16], ALU.mult, ALU.mult),
            reads=[sk], writes=[sk])
        p.emit("act", lambda e, b0=b0: e.activation(
            xap, xap, AF.Identity, bias=st[:, b0 + 12:b0 + 13],
            scale=st[:, b0 + 15:b0 + 16]),
            reads=[xk, sk], writes=[xk])
        if gamma is not None:
            p.emit("dve", lambda e: e.tensor_mul(xap, xap, gamma.ap),
                   reads=[xk, gamma.name], writes=[xk])
            p.emit("dve", lambda e: e.tensor_add(xap, xap, beta.ap),
                   reads=[xk, beta.name], writes=[xk])

    def bcast_scalar(self, src_ap, src_key, dst_ap, dst_key, neg=False):
        p = self.p
        ones = self.const("neg_ones_1x128" if neg else "ones_1x128", F32)
        self.mm_group("ps7", [(lambda e: e.matmul(
            p.PS[:, 7 * 512:7 * 512 + 1], ones.ap, src_ap,
            start=True, stop=True), [ones.name, src_key])])
        p.emit("dve", lambda e: e.tensor_copy(dst_ap, p.PS[:, 7 * 512:7 * 512 + 1]),
               reads=["ps7"], writes=[dst_key])

    def part_reduce(self, src_ap, src_key, dst_ap, dst_key, op, n=128):
        p = self.p
        identf = self.const("identf")
        self.transpose128("ps7", p.PS[0:1, 7 * 512:7 * 512 + n], src_ap,
                          [src_key], identf)
        p.emit("dve", lambda e: e.tensor_reduce(
            dst_ap, p.PS[0:1, 7 * 512:7 * 512 + n], axis=AX.X, op=op),
            reads=["ps7"], writes=[dst_key])

    def write_out(self):
        out = self.io["out"]
        X = self.X
        for i in range(NT):
            self.p.emit("pool", lambda e, i=i, X=X: e.dma_start(
                out=out[i * 128:(i + 1) * 128, :],
                in_=X[:, i * D:(i + 1) * D]),
                reads=[f"X#{i}"], ch="qpool")
        self.p.wait_all("pool")

    def finalize(self):
        self.p.finalize()


def head_stats(mk, TN, tname, col0, nchunks=NC_, span=S):
    p = mk.p
    bdb = mk.const("blockdiag2", BF16)
    sq = p.alloc("sqst", [128, span], BF16)
    tail = span - min(span, 2048)
    for oc in range(nchunks):
        p.emit("act", lambda e, oc=oc: e.activation(
            sq.ap, TN.t[:, oc * span:(oc + 1) * span], AF.Square),
            reads=[f"{tname}#{oc}"], writes=["sqst"])
        nb = min(span, 2048) // 512
        for tb in range(nb):
            bank = 4 + tb
            mk.mm_group(f"ps{bank}", [(lambda e, tb=tb, bank=bank: e.matmul(
                p.PS[:, bank * 512:(bank + 1) * 512], bdb.ap,
                sq[:, tb * 512:(tb + 1) * 512], start=True, stop=True),
                [bdb.name, "sqst"])])
        p.emit("dve", lambda e, nb=nb, oc=oc: e.tensor_reduce(
            mk.bb[0:2, col0 + oc:col0 + oc + 1], p.PS[0:2, 4 * 512:(4 + nb) * 512],
            axis=AX.X, op=ALU.max),
            reads=[f"ps{4 + tb}" for tb in range(nb)], writes=["bb"])
        if tail > 0:
            mk.mm_group("ps4", [(lambda e: e.matmul(
                p.PS[:, 4 * 512:4 * 512 + tail], bdb.ap,
                sq[:, 2048:2048 + tail], start=True, stop=True),
                [bdb.name, "sqst"])])
            p.emit("dve", lambda e: e.tensor_reduce(
                mk.bb[0:2, 22:23], p.PS[0:2, 4 * 512:4 * 512 + tail],
                axis=AX.X, op=ALU.max), reads=["ps4"], writes=["bb"])
            p.emit("dve", lambda e, oc=oc: e.tensor_max(
                mk.bb[0:2, col0 + oc:col0 + oc + 1],
                mk.bb[0:2, col0 + oc:col0 + oc + 1], mk.bb[0:2, 22:23]),
                reads=["bb"], writes=["bb"])
    p.freebuf(sq)


def finish_bounds(mk, nheads):
    p = mk.p
    p.emit("sp", lambda e: e.dma_start(out=mk.brow[0:1, 0:32], in_=mk.bb[0:2, 0:16]),
           reads=["bb"], writes=["brow"], ch="qsp")
    qap = bass.AP(mk.brow.t, 0, [[80, 1], [16, 2], [1, 8]])
    kap = bass.AP(mk.brow.t, 8, [[80, 1], [16, 2], [1, 8]])
    oap = bass.AP(mk.brow.t, 32, [[80, 1], [8, 2], [1, 8]])
    p.emit("dve", lambda e: e.tensor_tensor(oap, qap, kap, ALU.mult),
           reads=["brow"], writes=["brow"])
    p.emit("act", lambda e: e.activation(mk.brow[0:1, 48:64], mk.brow[0:1, 32:48],
                                         AF.Sqrt), reads=["brow"], writes=["brow"])


def attn_loop(mk, QT, KT, vp_fn, OnT, H, span, kt_lists, null_pad_tile=None,
              kspan=None, names=("QT", "KT", "Vp", "OnT"), bias_col=None):
    p = mk.p
    PTb = mk.PTb
    qtn, ktn, vpn, ontn = names
    identb = mk.const("identb", BF16)
    onesb = mk.const("ones128", BF16)
    negonesf = mk.const("neg_ones_1x128", F32)
    nqb = len(kt_lists)
    if bias_col is None:
        bias_col = lambda h, qb: 48 + (h % 2) * 8 + h // 2
    mk.mm_group("ps0", [(lambda e: e.matmul(
        p.PS[:, 0:64], negonesf.ap, mk.brow[0:1, 0:64],
        start=True, stop=True), [negonesf.name, "brow"])])
    p.emit("dve", lambda e: e.tensor_copy(mk.biasball[:, 0:64], p.PS[:, 0:64]),
           reads=["ps0"], writes=["biasball"])
    if null_pad_tile is not None:
        padc = mk.const("padcol64")
        p.emit("dve", lambda e: e.tensor_tensor(
            mk.biasball[:, 64:128], mk.biasball[:, 0:64], padc.ap, ALU.add),
            reads=["biasball", padc.name], writes=["biasball"])

    bc = [0]
    gpt = [0]
    gz = [0]
    pending_z = []
    ksp = kspan or span

    def emit_S_pair(h0, h1, qb, kt, bandname, off, b0_, b1_):
        has_mask = bandname is not None
        for h, b in ((h0, b0_), (h1, b1_)):
            half, oc = h % 2, h // 2
            lhs = KT.t[half * 64:half * 64 + 64,
                       oc * ksp + kt * 128: oc * ksp + kt * 128 + 128]
            rhs = QT.t[half * 64:half * 64 + 64,
                       oc * S + qb * 512: oc * S + qb * 512 + 512]
            p.emit("pe", lambda e, lhs=lhs, rhs=rhs, b=b, hm=has_mask:
                   e.matmul(p.PS[:, b * 512:(b + 1) * 512], lhs, rhs,
                            start=True, stop=not hm),
                   reads=[f"{ktn}#{oc}", f"{qtn}#{oc}"],
                   writes=[f"ps{b}"], inc=not has_mask)
        if has_mask:
            bnd = mk.const(bandname, BF16)
            if bandname == "band_prev":
                c_lo, c_hi = off, 512
                msl = bnd.t[:, 512: 512 + (512 - off)]
            else:
                c_lo, c_hi = 0, min(512, off + 128)
                msl = bnd.t[:, 512 - off: 512 - off + (c_hi - c_lo)]
            for b in (b0_, b1_):
                p.emit("pe", lambda e, msl=msl, b=b, c_lo=c_lo, c_hi=c_hi:
                       e.matmul(p.PS[:, b * 512 + c_lo: b * 512 + c_hi],
                                identb.ap, msl, start=False, stop=True),
                       reads=[identb.name, bnd.name],
                       wover=[f"ps{b}"], inc=True)

    def emit_exp1(h, kt, qb, b):
        c0 = bias_col(h, qb)
        pslot = gpt[0] % 6
        gpt[0] += 1
        ptk = f"PTb#{pslot}"
        pad = (null_pad_tile is not None and kt == null_pad_tile)
        bsl = (mk.biasball[:, 64 + c0:64 + c0 + 1] if pad
               else mk.biasball[:, c0:c0 + 1])
        p.emit("act", lambda e, bsl=bsl, b=b, pslot=pslot:
               e.activation(PTb.t[:, pslot * 512:(pslot + 1) * 512],
                            p.PS[:, b * 512:(b + 1) * 512],
                            AF.Exp, bias=bsl),
               reads=[f"ps{b}", "biasball"], writes=[ptk])
        return pslot

    def emit_pv1(h, kt, ob, pslot, st_, last):
        ptk = f"PTb#{pslot}"
        vb, vcol0, vkey = vp_fn(kt)
        lhs = vb.t[:, vcol0 + h * 65: vcol0 + h * 65 + 65]
        rhs = PTb.t[:, pslot * 512:(pslot + 1) * 512]
        p.emit("pe", lambda e, lhs=lhs, rhs=rhs, st_=st_, last=last, ob=ob:
               e.matmul(p.PS[0:65, ob * 512:ob * 512 + 512], lhs, rhs,
                        start=st_, stop=last),
               reads=[vkey, ptk],
               writes=[f"ps{ob}"] if st_ else (),
               wover=[f"ps{ob}"] if (last and not st_) else (),
               inc=last)

    zstage = mk.nc.dram_tensor(f"zstage_{mk.uid()}", [2, 512], F32,
                               kind="Internal").ap()

    def make_z2(dsts, ocks, obs):
        for hx in range(2):
            ob = obs[hx]
            stg = mk.rzb.t[64:65, hx * 512:(hx + 1) * 512]
            p.emit("dve", lambda e, ob=ob, stg=stg: e.reciprocal(
                stg, p.PS[64:65, ob * 512:(ob + 1) * 512]),
                reads=[f"ps{ob}"], writes=[f"zrow#{hx}"])
            p.emit("sp", lambda e, hx=hx, stg=stg: e.dma_start(
                out=zstage[hx:hx + 1, :], in_=stg),
                reads=[f"zrow#{hx}"], writes=[f"zst#{hx}"], ch="qsp")
            srcap = bass.AP(zstage.tensor, hx * 512, [[0, 64], [1, 512]])
            dstap = bass.AP(mk.rzb.t, hx * 512, [[1024, 64], [1, 512]])
            p.emit("sp", lambda e, srcap=srcap, dstap=dstap: e.dma_start(
                out=dstap, in_=srcap),
                reads=[f"zst#{hx}"], writes=[f"rzb#{hx}"], ch="qsp")

        def go():
            for hx in range(2):
                ob, dst, ock = obs[hx], dsts[hx], ocks[hx]
                rza = mk.rzb.t[0:64, hx * 512:(hx + 1) * 512]
                p.emit("dve", lambda e, rza=rza, dst=dst, ob=ob: e.tensor_tensor(
                    dst, p.PS[0:64, ob * 512:(ob + 1) * 512], rza, ALU.mult),
                    reads=[f"ps{ob}", f"rzb#{hx}"], writes=[ock])
        return go

    items = [(hp, qb) for hp in range(H // 2) for qb in range(nqb)]
    for item_i, (hp, qb) in enumerate(items):
        h0, h1 = 2 * hp, 2 * hp + 1
        oc = hp
        obs = (4, 5) if item_i % 2 == 0 else (6, 7)
        kt_list = kt_lists[qb]
        n = len(kt_list)
        sbank = []
        pslots = []

        def do_exp(ki):
            pkt = kt_list[ki][0]
            pslots.append((emit_exp1(h0, pkt, qb, sbank[ki][0]),
                           emit_exp1(h1, pkt, qb, sbank[ki][1])))

        def do_pv(ki):
            pkt = kt_list[ki][0]
            for hx, (h, ob) in enumerate(((h0, obs[0]), (h1, obs[1]))):
                emit_pv1(h, pkt, ob, pslots[ki][hx],
                         st_=(ki == 0), last=(ki == n - 1))

        assert n >= 3
        for ki, (kt, bandname, off) in enumerate(kt_list):
            if ki >= 3:
                do_pv(ki - 3)
            if ki >= 1:
                do_exp(ki - 1)
            b0_ = bc[0] % 4
            b1_ = (bc[0] + 1) % 4
            bc[0] += 2
            sbank.append((b0_, b1_))
            emit_S_pair(h0, h1, qb, kt, bandname, off, b0_, b1_)
            if ki == 2 and pending_z:
                for z in pending_z:
                    z()
                pending_z.clear()
        do_exp(n - 1)
        for ki in range(max(0, n - 3), n):
            do_pv(ki)
        dsts, ocks = [], []
        for h in (h0, h1):
            half = h % 2
            dsts.append(OnT.t[half * 64:half * 64 + 64,
                              oc * S + qb * 512: oc * S + qb * 512 + 512])
            ocks.append(f"{ontn}#{oc}")
        pending_z.append(make_z2(dsts, ocks, obs))
    for z in pending_z:
        z()


def out_proj_residual(mk, OnT, WoT, ont_name, wot_name, spill_tag, post_eps,
                      gamma=None, beta=None, extra_spill=None, bias_tab=None,
                      nchunks=NC_, scale_ap=None, scale_key=None, chunk_k=None,
                      tiles=None, ont_span=S, dst_spill=None):
    p = mk.p
    X = mk.X
    ln_pend = [None]
    if dst_spill is not None and dst_spill not in mk.spill:
        mk.spill[dst_spill] = mk.nc.dram_tensor(
            f"spill_{dst_spill}", [S, D], F32, kind="Internal").ap()
    if dst_spill is None and spill_tag is not None:
        for i in (tiles if tiles is not None else range(NT)):
            mk.unspill_x_tile(spill_tag, i)
    for i in (tiles if tiles is not None else range(NT)):
        if dst_spill is not None:
            xb = mk.xtra2 if i % 2 == 0 else mk.xtra2_b
            xbk = "xtra2" if i % 2 == 0 else "xtra2_b"
            sp0 = mk.spill[spill_tag]
            eng, ch = p.hch()
            p.emit(eng, lambda e, i=i, sp0=sp0, xb=xb: e.dma_start(
                out=xb.ap, in_=sp0[i * 128:(i + 1) * 128, :]),
                reads=[f"sp_{spill_tag}#{i}"], writes=[xbk], ch=ch)
        if extra_spill is not None:
            sp = mk.spill[extra_spill]
            p.emit("sp", lambda e, i=i, sp=sp: e.dma_start(
                out=mk.xtra.ap, in_=sp[i * 128:(i + 1) * 128, :]),
                reads=[f"sp_{extra_spill}#{i}"], writes=["xtra"], ch="qsp")
        tpar = (i % 2) * 2
        for oh in range(2):
            bank = 4 + tpar + oh
            members = []
            for ic in range(nchunks):
                kk = 128 if chunk_k is None else chunk_k(ic)
                i_loc = i - (tiles.start if tiles is not None else 0)
                lhs = OnT.t[0:kk, ic * ont_span + i_loc * 128:
                            ic * ont_span + i_loc * 128 + 128]
                rhs = WoT.t[0:kk, ic * D + oh * 512: ic * D + oh * 512 + 512]
                members.append((lambda e, lhs=lhs, rhs=rhs, bank=bank, ic=ic,
                                nchunks=nchunks:
                                e.matmul(p.PS[:, bank * 512:(bank + 1) * 512],
                                         lhs, rhs, start=(ic == 0),
                                         stop=(ic == nchunks - 1)),
                                [f"{ont_name}#{ic}", f"{wot_name}#{ic}"]))
            mk.mm_group(f"ps{bank}", members)
        for oh in range(2):
            bank = 4 + tpar + oh
            if dst_spill is not None:
                xap = xb[:, oh * 512:(oh + 1) * 512]
                xkey = xbk
            else:
                xap = X[:, i * D + oh * 512: i * D + (oh + 1) * 512]
                xkey = f"X#{i}"
            if scale_ap is None:
                p.emit("dve", lambda e, xap=xap, bank=bank: e.scalar_tensor_tensor(
                    xap, p.PS[:, bank * 512:(bank + 1) * 512], 1.0, xap,
                    ALU.mult, ALU.add),
                    reads=[f"ps{bank}", xkey], writes=[xkey])
            else:
                p.emit("dve", lambda e, xap=xap, bank=bank: e.scalar_tensor_tensor(
                    xap, p.PS[:, bank * 512:(bank + 1) * 512], scale_ap, xap,
                    ALU.mult, ALU.add),
                    reads=[f"ps{bank}", xkey, scale_key], writes=[xkey])
            if bias_tab is not None:
                p.emit("dve", lambda e, xap=xap, oh=oh: e.tensor_add(
                    xap, xap, bias_tab.t[:, oh * 512:(oh + 1) * 512]),
                    reads=[xkey, bias_tab.name], writes=[xkey])
            if extra_spill is not None:
                p.emit("dve", lambda e, xap=xap, oh=oh: e.tensor_add(
                    xap, xap, mk.xtra[:, oh * 512:(oh + 1) * 512]),
                    reads=[xkey, "xtra"], writes=[xkey])
        if dst_spill is not None:
            spd = mk.spill[dst_spill]
            p.emit("pool", lambda e, i=i, spd=spd, xb=xb: e.dma_start(
                out=spd[i * 128:(i + 1) * 128, :], in_=xb.ap),
                reads=[xbk], writes=[f"sp_{dst_spill}#{i}"], ch="qpool")
        if post_eps is not None:
            assert gamma is None
            fa, fb = mk.ln_tile_stages(i, post_eps)
            fa()
            if ln_pend[0] is not None:
                ln_pend[0]()
            ln_pend[0] = fb
    if ln_pend[0] is not None:
        ln_pend[0]()


def block_l0_attn(mk):
    p = mk.p
    io = mk.io
    H = 16
    mk.const("l0_cos"); mk.const("l0_sinsw")
    WqT = mk.load_weight_T("WqT", io["L0_attn_Wq"], scale=0.125)
    XT = p.alloc("XT", [128, NC_ * S], BF16)
    mk.full_transpose_cast(
        lambda ic, t: mk.X[:, t * D + ic * 128: t * D + ic * 128 + 128],
        lambda ic, t: f"X#{t}", XT, lambda ic, t: f"XT#{ic}")
    mk.spill_x("l0a")

    cosb = mk.const("l0_cos")
    sinswb = mk.const("l0_sinsw")
    QT = p.alloc("QT", [128, NC_ * S], BF16)
    KT = p.alloc("KT", [128, NC_ * S], BF16)
    rA, rB, rC = mk.rA, mk.rB, mk.rC

    def project_chunk(wt, wname, oc, b0=0):
        for tb in range(4):
            members = []
            for ic in range(NC_):
                lhs = wt.t[:, ic * D + oc * 128: ic * D + oc * 128 + 128]
                rhs = XT.t[:, ic * S + tb * 512: ic * S + tb * 512 + 512]
                members.append((lambda e, lhs=lhs, rhs=rhs, b=b0 + tb, ic=ic:
                                e.matmul(p.PS[:, b * 512:(b + 1) * 512], lhs, rhs,
                                         start=(ic == 0), stop=(ic == NC_ - 1)),
                                [f"{wname}#{ic}", f"XT#{ic}"]))
            mk.mm_group(f"ps{b0 + tb}", members)

    def rope_evac(dst, dname, oc, cos, sinsw, swap, b0=0):
        pall = p.PS[:, b0 * 512: b0 * 512 + 2048]
        pk = [f"ps{b0 + j}" for j in range(4)]
        p.emit("dve", lambda e, pall=pall: e.tensor_mul(rA.ap, pall, cos.ap),
               reads=pk + [cos.name], writes=["ropeA"])
        p.emit("dve", lambda e, pall=pall: e.tensor_mul(rB.ap, pall, sinsw.ap),
               reads=pk + [sinsw.name], writes=["ropeB"])
        if swap == "pair":
            mask = [(i + 1 if i % 2 == 0 else i - 1) for i in range(32)]
        else:
            mask = [(i + 16) % 32 for i in range(32)]
        p.emit("dve", lambda e, mask=mask: e.stream_shuffle(rC.ap, rB.ap, mask),
               reads=["ropeB"], writes=["ropeC"])
        dcol = oc * S
        p.emit("dve", lambda e: e.tensor_add(dst.t[:, dcol:dcol + S], rA.ap, rC.ap),
               reads=["ropeA", "ropeC"], writes=[f"{dname}#{oc}"])

    for oc in range(NC_):
        project_chunk(WqT, "WqT", oc, (oc % 2) * 4)
        if oc > 0:
            rope_evac(QT, "QT", oc - 1, cosb, sinswb, "pair", ((oc - 1) % 2) * 4)
    rope_evac(QT, "QT", NC_ - 1, cosb, sinswb, "pair", ((NC_ - 1) % 2) * 4)
    p.freebuf(WqT)
    WkT = mk.load_weight_T("WkT", io["L0_attn_Wk"])
    for oc in range(NC_):
        project_chunk(WkT, "WkT", oc, (oc % 2) * 4)
        if oc > 0:
            rope_evac(KT, "KT", oc - 1, cosb, sinswb, "pair", ((oc - 1) % 2) * 4)
    rope_evac(KT, "KT", NC_ - 1, cosb, sinswb, "pair", ((NC_ - 1) % 2) * 4)
    p.freebuf(WkT)
    mk.free_const("l0_cos"); mk.free_const("l0_sinsw")
    head_stats(mk, QT, "QT", 0)
    head_stats(mk, KT, "KT", 8)

    VW = H * 65
    VpA = p.alloc("VpA", [128, 8 * VW], BF16)
    VpB = p.alloc("VpB", [128, 8 * VW], BF16)

    def vp_fn(kt):
        if kt < 8:
            return VpA, kt * VW, f"VpA#{kt}"
        return VpB, (kt - 8) * VW, f"VpB#{kt - 8}"
    for vb, vn in ((VpA, "VpA"), (VpB, "VpB")):
        ones_ap = bass.AP(vb.t, 64, [[8 * VW, 128], [VW, 8], [65, H]])
        p.emit("dve", lambda e, ones_ap=ones_ap: e.memset(ones_ap, 1.0),
               writes=[f"{vn}#{kt}" for kt in range(8)])
    vchunk = p.alloc("vchunk", [128, S], F32)
    identf = mk.const("identf")
    WvT = mk.load_weight_T("WvT", io["L0_attn_Wv"])

    def v_evac(oc, b0):
        pall = p.PS[:, b0 * 512: b0 * 512 + 2048]
        pk = [f"ps{b0 + j}" for j in range(4)]
        p.emit("dve", lambda e, pall=pall: e.tensor_copy(vchunk.ap, pall),
               reads=pk, writes=["vchunk"])
        h0 = 2 * oc
        for g4 in range(4):
            bank = b0 + g4
            bk = f"ps{bank}"
            for j in range(4):
                kt = g4 * 4 + j
                ps = p.PS[:, bank * 512 + j * 128: bank * 512 + (j + 1) * 128]
                mk.transpose128(bk, ps, vchunk[:, kt * 128:(kt + 1) * 128],
                                ["vchunk"], identf, wover=(j > 0))
            vb, vcol0, _ = vp_fn(g4 * 4)
            srcap = bass.AP(p.PS, bank * 512,
                            [[4096, 128], [128, 4], [64, 2], [1, 64]])
            dstap = bass.AP(vb.t, vcol0 + h0 * 65,
                            [[8 * VW, 128], [VW, 4], [65, 2], [1, 64]])
            p.emit("dve", lambda e, srcap=srcap, dstap=dstap:
                   e.tensor_copy(dstap, srcap),
                   reads=[bk], writes=[vp_fn(g4 * 4 + j)[2] for j in range(4)])

    for oc in range(NC_):
        project_chunk(WvT, "WvT", oc, (oc % 2) * 4)
        if oc > 0:
            v_evac(oc - 1, ((oc - 1) % 2) * 4)
    v_evac(NC_ - 1, ((NC_ - 1) % 2) * 4)
    p.freebuf(WvT); p.freebuf(vchunk); p.freebuf(XT)
    finish_bounds(mk, H)

    OnT = p.alloc("OnT", [128, NC_ * S], BF16)
    mk.PTb = p.alloc("PTb", [128, 3072], BF16)
    kt_lists = [[(kt, ("band" if kt * 128 >= qb * 512 else None),
                  (kt * 128 - qb * 512 if kt * 128 >= qb * 512 else None))
                 for kt in range(4 * (qb + 1))] for qb in range(4)]
    attn_loop(mk, QT, KT, vp_fn, OnT, H, S, kt_lists)
    p.freebuf(QT); p.freebuf(KT); p.freebuf(VpA); p.freebuf(VpB)
    p.freebuf(mk.PTb)
    del mk.PTb

    WoT = mk.load_weight_T("WoT", io["L0_attn_Wo"])
    mk.realloc_x()
    out_proj_residual(mk, OnT, WoT, "OnT", "WoT", "l0a", post_eps=1e-5)
    p.freebuf(OnT); p.freebuf(WoT)


def block_l0_ffn(mk, n_cores=8):
    p = mk.p
    io = mk.io
    nc = mk.nc
    gtab = p.alloc("l0f_g", [128, D], F32)
    btab = p.alloc("l0f_b", [128, D], F32)
    for tab, nm in ((gtab, "L0_ffn_ln_gamma"), (btab, "L0_ffn_ln_beta")):
        srcap = bass.AP(io[nm].tensor, 0, [[0, 128], [1, D]])
        p.emit("sp", lambda e, tab=tab, srcap=srcap: e.dma_start(out=tab.ap, in_=srcap),
               writes=[tab.name], ch="qsp")
    xnT = p.alloc("xnT", [128, NC_ * S], BF16)
    xn = p.alloc("xn_scr", [128, 2 * D], F32)
    gm = p.alloc("gmax", [128, 8], F32)
    st = mk.stat
    epsb = mk.epsbuf(1e-3)
    identf = mk.const("identf")
    def xn_A(i):
        par = i % 2
        b0 = par * 16
        sk = f"stat#{par}"
        xap = mk.X[:, i * D:(i + 1) * D]
        xk = f"X#{i}"
        for ch_ in range(2):
            p.emit("dve", lambda e, xap=xap, ch_=ch_, b0=b0: e.bn_stats(
                st[:, b0 + ch_ * 6: b0 + (ch_ + 1) * 6],
                xap[:, ch_ * 512:(ch_ + 1) * 512]),
                reads=[xk], writes=[sk])
        p.emit("dve", lambda e, b0=b0: e.bn_aggr(st[:, b0 + 12:b0 + 14],
                                                 st[:, b0:b0 + 12]),
               reads=[sk], writes=[sk])
        p.emit("act", lambda e, b0=b0: e.activation(
            st[:, b0 + 14:b0 + 15], st[:, b0 + 13:b0 + 14], AF.Sqrt,
            bias=epsb.ap), reads=[sk, epsb.name], writes=[sk])

    def xn_CD(i):
        par = i % 2
        b0 = par * 16
        sk = f"stat#{par}"
        xnk = f"xn_scr#{par}"
        xna = xn.t[:, par * D:(par + 1) * D]
        xap = mk.X[:, i * D:(i + 1) * D]
        xk = f"X#{i}"
        p.emit("dve", lambda e, b0=b0: e.reciprocal(st[:, b0 + 15:b0 + 16],
                                                    st[:, b0 + 14:b0 + 15]),
               reads=[sk], writes=[sk])
        p.emit("dve", lambda e, b0=b0: e.scalar_tensor_tensor(
            st[:, b0 + 12:b0 + 13], st[:, b0 + 12:b0 + 13], -1.0,
            st[:, b0 + 15:b0 + 16], ALU.mult, ALU.mult),
            reads=[sk], writes=[sk])
        p.emit("act", lambda e, xap=xap, xna=xna, b0=b0: e.activation(
            xna, xap, AF.Identity, bias=st[:, b0 + 12:b0 + 13],
            scale=st[:, b0 + 15:b0 + 16]),
            reads=[xk, sk], writes=[xnk])

    def xn_EF(i):
        par = i % 2
        xnk = f"xn_scr#{par}"
        xna = xn.t[:, par * D:(par + 1) * D]
        p.emit("dve", lambda e, xna=xna: e.tensor_mul(xna, xna, gtab.ap),
               reads=[xnk, gtab.name], writes=[xnk])
        p.emit("dve", lambda e, xna=xna: e.tensor_add(xna, xna, btab.ap),
               reads=[xnk, btab.name], writes=[xnk])
        if i < 2:
            p.emit("dve", lambda e, xna=xna, par=par: e.tensor_reduce(
                gm[:, par:par + 1], xna, axis=AX.X, op=ALU.max,
                apply_absolute_value=True), reads=[xnk], writes=[f"gmax#{par}"])
        else:
            p.emit("dve", lambda e, xna=xna, par=par: e.tensor_reduce(
                gm[:, 2 + par:3 + par], xna, axis=AX.X, op=ALU.max,
                apply_absolute_value=True), reads=[xnk], writes=[f"gmax#{par}"])
            p.emit("dve", lambda e, par=par: e.tensor_max(
                gm[:, par:par + 1], gm[:, par:par + 1], gm[:, 2 + par:3 + par]),
                reads=[f"gmax#{par}"], writes=[f"gmax#{par}"])
        for ic0 in range(0, NC_, 4):
            bank = par * 2 + ic0 // 4
            bk = f"ps{bank}"
            for j in range(4):
                ic = ic0 + j
                ps = p.PS[:, bank * 512 + j * 128: bank * 512 + (j + 1) * 128]
                mk.transpose128(bk, ps, xna[:, ic * 128:(ic + 1) * 128],
                                [xnk], identf, wover=(j > 0))
            srcap = bass.AP(p.PS, bank * 512, [[4096, 128], [1, 512]])
            dstap = bass.AP(xnT.t, ic0 * S + i * 128,
                            [[NC_ * S, 128], [S, 4], [1, 128]])
            mk.evac(dstap, srcap, reads=[bk],
                    writes=[f"xnT#{ic0 + j}" for j in range(4)])

    for i in range(NT):
        xn_A(i)
        if i >= 1:
            xn_CD(i - 1)
        if i >= 2:
            xn_EF(i - 2)
    xn_CD(NT - 1)
    xn_EF(NT - 2)
    xn_EF(NT - 1)
    p.emit("dve", lambda e: e.tensor_max(gm[:, 0:1], gm[:, 0:1], gm[:, 1:2]),
           reads=["gmax#0", "gmax#1"], writes=["gmax"])
    p.freebuf(xn)
    p.emit("dve", lambda e: e.memset(gm[:, 4:6], 0.0),
           reads=["gmax"], writes=["gmax"])
    mk.part_reduce(gm[:, 0:1], "gmax", gm[0:1, 4:5], "gmax", ALU.max)
    ccbuf = p.alloc("ccbuf", [1, 16], F32)
    p.emit("dve", lambda e: e.memset(ccbuf.ap, 0.0), writes=["ccbuf"])
    p.emit("dve", lambda e: e.tensor_copy(ccbuf[0:1, 0:1], gm[0:1, 4:5]),
           reads=["gmax"], writes=["ccbuf"])
    if n_cores > 1:
        cc_in = nc.dram_tensor("cc_in_l0f", [1, 16], F32, kind="Internal")
        cc_out = nc.dram_tensor("cc_out_l0f", [1, 16], F32, kind="Internal")
        p.emit("pool", lambda e: e.dma_start(out=cc_in[:, :], in_=ccbuf.ap),
               reads=["ccbuf"], writes=["ccdram"], ch="qpool")
        p.emit("pool", lambda e: e.collective_compute(
            "AllReduce", ALU.max, replica_groups=[list(range(n_cores))],
            ins=[cc_in.ap().opt()], outs=[cc_out.ap().opt()]),
            reads=["ccdram"], writes=["ccdram2"], ch="cc")
        p.emit("sp", lambda e: e.dma_start(out=ccbuf[0:1, 0:16], in_=cc_out[:, :]),
               reads=["ccdram2"], writes=["ccbuf"], ch="qsp")
    p.emit("dve", lambda e: e.reciprocal(ccbuf[0:1, 1:2], ccbuf[0:1, 0:1]),
           reads=["ccbuf"], writes=["ccbuf"])
    p.emit("dve", lambda e: e.tensor_scalar_mul(ccbuf[0:1, 2:3], ccbuf[0:1, 1:2],
                                                128.0), reads=["ccbuf"], writes=["ccbuf"])
    mk.bcast_scalar(ccbuf[0:1, 2:3], "ccbuf", mk.biasb[:, 0:1], "biasb")
    CLIP = 128.0 - 1e-5
    for ic in range(NC_):
        sl = xnT.t[:, ic * S:(ic + 1) * S]
        p.emit("dve", lambda e, sl=sl: e.tensor_scalar(
            sl, sl, mk.biasb[:, 0:1], CLIP, ALU.mult, ALU.min),
            reads=[f"xnT#{ic}", "biasb"], writes=[f"xnT#{ic}"])
        p.emit("dve", lambda e, sl=sl: e.tensor_scalar_max(sl, sl, -CLIP),
            reads=[f"xnT#{ic}"], writes=[f"xnT#{ic}"])
    wq = p.alloc("l0f_wq", [128, NC_ * D], BF16)
    wstage = p.alloc("l0f_wst", [128, D], F32)
    wsum = p.alloc("wsum", [128, 4], F32)
    for ic in range(NC_):
        p.emit("sp", lambda e, ic=ic: e.dma_start(
            out=wstage.ap,
            in_=io["L0_ffn_weight"][ic * 128:(ic + 1) * 128, :]),
            reads=["l0f_wst"], writes=["l0f_wst"], ch="qsp")
        if ic == 0:
            p.emit("dve", lambda e: e.tensor_reduce(
                wsum[:, 0:1], wstage.ap, axis=AX.X,
                op=ALU.add, apply_absolute_value=True),
                reads=["l0f_wst"], writes=["wsum"])
        else:
            p.emit("dve", lambda e: e.tensor_reduce(
                wsum[:, 1:2], wstage.ap, axis=AX.X,
                op=ALU.add, apply_absolute_value=True),
                reads=["l0f_wst"], writes=["wsum"])
            p.emit("dve", lambda e: e.tensor_add(wsum[:, 0:1], wsum[:, 0:1],
                                                 wsum[:, 1:2]),
                   reads=["wsum"], writes=["wsum"])
    mk.part_reduce(wsum[:, 0:1], "wsum", ccbuf[0:1, 4:5], "ccbuf", ALU.add)
    p.emit("dve", lambda e: e.tensor_scalar_mul(ccbuf[0:1, 5:6], ccbuf[0:1, 4:5],
                                                1.0 / (D * D)),
           reads=["ccbuf"], writes=["ccbuf"])
    p.emit("dve", lambda e: e.tensor_scalar(ccbuf[0:1, 6:7], ccbuf[0:1, 5:6],
                                            1e-5, 0.5, ALU.add, ALU.mult),
           reads=["ccbuf"], writes=["ccbuf"])
    p.emit("dve", lambda e: e.tensor_mul(ccbuf[0:1, 7:8], ccbuf[0:1, 5:6],
                                         ccbuf[0:1, 0:1]),
           reads=["ccbuf"], writes=["ccbuf"])
    p.emit("dve", lambda e: e.tensor_scalar_mul(ccbuf[0:1, 8:9], ccbuf[0:1, 7:8],
                                                1.0 / 128.0),
           reads=["ccbuf"], writes=["ccbuf"])
    mk.bcast_scalar(ccbuf[0:1, 6:7], "ccbuf", mk.biasb[:, 1:2], "biasb")
    mk.bcast_scalar(ccbuf[0:1, 8:9], "ccbuf", mk.biasb[:, 2:3], "biasb")
    tmp = p.alloc("l0f_tmp", [128, D], F32)
    for ic in range(NC_):
        qsl = wq.t[:, ic * D:(ic + 1) * D]
        p.emit("sp", lambda e, ic=ic: e.dma_start(
            out=wstage.ap,
            in_=io["L0_ffn_weight"][ic * 128:(ic + 1) * 128, :]),
            reads=["l0f_wst"], writes=["l0f_wst"], ch="qsp")
        p.emit("dve", lambda e, qsl=qsl: e.tensor_scalar(
            qsl, wstage.ap, mk.biasb[:, 1:2], None, ALU.is_gt),
            reads=["l0f_wst", "biasb"], writes=[f"l0f_wq#{ic}"])
        p.emit("dve", lambda e: e.tensor_scalar_mul(tmp.ap, wstage.ap, -1.0),
               reads=["l0f_wst"], writes=["l0f_tmp"])
        p.emit("dve", lambda e: e.tensor_scalar(
            tmp.ap, tmp.ap, mk.biasb[:, 1:2], None, ALU.is_gt),
            reads=["l0f_tmp", "biasb"], writes=["l0f_tmp"])
        p.emit("dve", lambda e, qsl=qsl: e.tensor_sub(qsl, qsl, tmp.ap),
            reads=[f"l0f_wq#{ic}", "l0f_tmp"], writes=[f"l0f_wq#{ic}"])
    p.freebuf(wstage); p.freebuf(tmp)
    out_proj_residual(mk, xnT, wq, "xnT", "l0f_wq", None, 1e-5,
                      scale_ap=mk.biasb[:, 2:3], scale_key="biasb")
    p.freebuf(xnT); p.freebuf(wq); p.freebuf(gtab); p.freebuf(btab)
    p.freebuf(gm); p.freebuf(ccbuf); p.freebuf(wsum)


def block_l1_attn(mk):
    p = mk.p
    io = mk.io
    H = 16
    KSPAN = 17 * 128
    mk.const("l1a_cos"); mk.const("l1a_sinsw")
    XT = p.alloc("XT", [128, NC_ * S], BF16)
    mk.full_transpose_cast(
        lambda ic, t: mk.X[:, t * D + ic * 128: t * D + ic * 128 + 128],
        lambda ic, t: f"X#{t}", XT, lambda ic, t: f"XT#{ic}")
    mk.spill_x("l1a")

    cosb = mk.const("l1a_cos")
    sinswb = mk.const("l1a_sinsw")
    QT = p.alloc("QTa", [128, NC_ * S], BF16)
    KT = p.alloc("KTa", [128, NC_ * KSPAN], BF16)
    VW = H * 65
    NKT = 17
    rA, rB, rC = mk.rA, mk.rB, mk.rC
    p.emit("dve", lambda e: e.memset(KT.ap, 0.0),
           writes=[f"KTa#{ic}" for ic in range(NC_)])

    def load_cat_weight(wname, sem_dram, geo_dram, scale):
        wt = p.alloc(wname, [128, NC_ * D], BF16)
        stg = p.alloc(wname + "_stg", [128, 1024], F32)
        skey = wname + "_stg"
        for ic in range(NC_):
            p.emit("sp", lambda e, ic=ic: e.dma_start(
                out=stg[:, 0:512], in_=sem_dram[ic * 128:(ic + 1) * 128, :]),
                reads=[skey], writes=[skey], ch="qsp")
            p.emit("sp", lambda e, ic=ic: e.dma_start(
                out=stg[:, 512:1024], in_=geo_dram[ic * 128:(ic + 1) * 128, :]),
                reads=[skey], writes=[skey], ch="qsp")
            for part in range(2):
                srcap = bass.AP(stg.t, part * 512, [[1024, 128], [32, 16], [1, 32]])
                dstap = bass.AP(wt.t, ic * D + part * 32,
                                [[NC_ * D, 128], [64, 16], [1, 32]])
                if scale is None:
                    p.emit("dve", lambda e, srcap=srcap, dstap=dstap:
                           e.tensor_copy(dstap, srcap),
                           reads=[skey], writes=[f"{wname}#{ic}"])
                else:
                    p.emit("dve", lambda e, srcap=srcap, dstap=dstap, scale=scale:
                           e.tensor_scalar_mul(dstap, srcap, scale),
                           reads=[skey], writes=[f"{wname}#{ic}"])
        p.freebuf(stg)
        return wt

    def project_chunk(wt, wname, oc, b0=0):
        for tb in range(4):
            members = []
            for ic in range(NC_):
                lhs = wt.t[:, ic * D + oc * 128: ic * D + oc * 128 + 128]
                rhs = XT.t[:, ic * S + tb * 512: ic * S + tb * 512 + 512]
                members.append((lambda e, lhs=lhs, rhs=rhs, b=b0 + tb, ic=ic:
                                e.matmul(p.PS[:, b * 512:(b + 1) * 512], lhs, rhs,
                                         start=(ic == 0), stop=(ic == NC_ - 1)),
                                [f"{wname}#{ic}", f"XT#{ic}"]))
            mk.mm_group(f"ps{b0 + tb}", members)

    def rope_evac(dst, dname, oc, dcol, cos, sinsw, b0=0):
        pall = p.PS[:, b0 * 512: b0 * 512 + 2048]
        pk = [f"ps{b0 + j}" for j in range(4)]
        p.emit("dve", lambda e, pall=pall: e.tensor_mul(rA.ap, pall, cos.ap),
               reads=pk + [cos.name], writes=["ropeA"])
        p.emit("dve", lambda e, pall=pall: e.tensor_mul(rB.ap, pall, sinsw.ap),
               reads=pk + [sinsw.name], writes=["ropeB"])
        mask = [(i + 16) % 32 for i in range(32)]
        p.emit("dve", lambda e, mask=mask: e.stream_shuffle(rC.ap, rB.ap, mask),
               reads=["ropeB"], writes=["ropeC"])
        p.emit("dve", lambda e: e.tensor_add(dst.t[:, dcol:dcol + S], rA.ap, rC.ap),
               reads=["ropeA", "ropeC"], writes=[f"{dname}#{oc}"])

    sc = 1.0 / np.sqrt(32.0)
    Wq = load_cat_weight("WqcT", io["L1_attn_wq_sem"], io["L1_attn_wq_geo"], sc)
    for oc in range(NC_):
        project_chunk(Wq, "WqcT", oc, (oc % 2) * 4)
        if oc > 0:
            rope_evac(QT, "QTa", oc - 1, (oc - 1) * S, cosb, sinswb,
                      ((oc - 1) % 2) * 4)
    rope_evac(QT, "QTa", NC_ - 1, (NC_ - 1) * S, cosb, sinswb, ((NC_ - 1) % 2) * 4)
    p.freebuf(Wq)
    Wk = load_cat_weight("WkcT", io["L1_attn_wk_sem"], io["L1_attn_wk_geo"], None)
    for oc in range(NC_):
        project_chunk(Wk, "WkcT", oc, (oc % 2) * 4)
        if oc > 0:
            rope_evac(KT, "KTa", oc - 1, (oc - 1) * KSPAN, cosb, sinswb,
                      ((oc - 1) % 2) * 4)
    rope_evac(KT, "KTa", NC_ - 1, (NC_ - 1) * KSPAN, cosb, sinswb,
              ((NC_ - 1) % 2) * 4)
    p.freebuf(Wk)
    mk.free_const("l1a_cos"); mk.free_const("l1a_sinsw")
    head_stats(mk, QT, "QTa", 0)
    for ic in range(NC_):
        for part, nm in ((0, "L1_attn_k_sem_null"), (1, "L1_attn_k_geo_null")):
            for hh in range(2):
                h = 2 * ic + hh
                dst = KT.t[:, ic * KSPAN + 2048: ic * KSPAN + 2049]
                dstap = bass.AP(KT.t, (hh * 64 + part * 32) * (NC_ * KSPAN)
                                + ic * KSPAN + 2048,
                                [[NC_ * KSPAN, 32], [1, 1]])
                srcap = io[nm][h * 32:(h + 1) * 32]
                p.emit("pool", lambda e, dstap=dstap, srcap=srcap:
                       e.dma_start(out=dstap, in_=srcap.unsqueeze(1)),
                       reads=[f"KTa#{ic}"], writes=[f"KTa#{ic}"], ch="qpool")

    VpA = p.alloc("VpaA", [128, 8 * VW], BF16)
    VpB = p.alloc("VpaB", [128, 9 * VW], BF16)

    def vp_fn(kt):
        if kt < 8:
            return VpA, kt * VW, f"VpaA#{kt}"
        return VpB, (kt - 8) * VW, f"VpaB#{kt - 8}"
    for vb, vn, ntk in ((VpA, "VpaA", 8), (VpB, "VpaB", 9)):
        p.emit("dve", lambda e, vb=vb: e.memset(vb.ap, 0.0),
               writes=[f"{vn}#{kt}" for kt in range(ntk)])
        ones_ap = bass.AP(vb.t, 64, [[ntk * VW, 128], [VW, ntk], [65, H]])
        p.emit("dve", lambda e, ones_ap=ones_ap: e.memset(ones_ap, 1.0),
               writes=[f"{vn}#{kt}" for kt in range(ntk)])
    vchunk = p.alloc("vchunk", [128, S], F32)
    identf = mk.const("identf")
    WvT = mk.load_weight_N("WvTa", io["L1_attn_wv"])

    def v_evac_a(oc, b0):
        pall = p.PS[:, b0 * 512: b0 * 512 + 2048]
        pk = [f"ps{b0 + j}" for j in range(4)]
        p.emit("dve", lambda e, pall=pall: e.tensor_copy(vchunk.ap, pall),
               reads=pk, writes=["vchunk"])
        h0 = 2 * oc
        for g4 in range(4):
            bank = b0 + g4
            bk = f"ps{bank}"
            for j in range(4):
                kt = g4 * 4 + j
                ps = p.PS[:, bank * 512 + j * 128: bank * 512 + (j + 1) * 128]
                mk.transpose128(bk, ps, vchunk[:, kt * 128:(kt + 1) * 128],
                                ["vchunk"], identf, wover=(j > 0))
            vb, vcol0, _ = vp_fn(g4 * 4)
            nwid = 8 if g4 < 2 else 9
            srcap = bass.AP(p.PS, bank * 512,
                            [[4096, 128], [128, 4], [64, 2], [1, 64]])
            dstap = bass.AP(vb.t, vcol0 + h0 * 65,
                            [[nwid * VW, 128], [VW, 4], [65, 2], [1, 64]])
            p.emit("dve", lambda e, srcap=srcap, dstap=dstap:
                   e.tensor_copy(dstap, srcap),
                   reads=[bk], writes=[vp_fn(g4 * 4 + j)[2] for j in range(4)])

    for oc in range(NC_):
        project_chunk(WvT, "WvTa", oc, (oc % 2) * 4)
        if oc > 0:
            v_evac_a(oc - 1, ((oc - 1) % 2) * 4)
    v_evac_a(NC_ - 1, ((NC_ - 1) % 2) * 4)
    p.freebuf(WvT); p.freebuf(vchunk); p.freebuf(XT)
    vsrc = io["L1_attn_v_null"]
    vdst = bass.AP(VpB.t, 8 * VW, [[9 * VW, 1], [65, 16], [1, 64]])
    p.emit("pool", lambda e, vdst=vdst, vsrc=vsrc: e.dma_start(
        out=vdst, in_=vsrc.rearrange("(h d) -> h d", d=64).unsqueeze(0)),
        reads=["VpaB#8"], writes=["VpaB#8"], ch="qpool")

    head_stats(mk, KT, "KTa", 8, span=KSPAN)
    finish_bounds(mk, H)

    OnT = p.alloc("OnTa", [128, NC_ * S], BF16)
    mk.PTb = p.alloc("PTb", [128, 3072], BF16)
    kt_lists = []
    for qb in range(4):
        lst = [(kt, ("band" if kt * 128 - qb * 512 >= 0 else None),
                (kt * 128 - qb * 512 if kt * 128 - qb * 512 >= 0 else None))
               for kt in range(4 * (qb + 1))]
        lst.append((16, None, None))
        kt_lists.append(lst)
    attn_loop(mk, QT, KT, vp_fn, OnT, H, S, kt_lists, null_pad_tile=16,
              kspan=KSPAN, names=("QTa", "KTa", "Vpa", "OnTa"))
    p.freebuf(QT); p.freebuf(KT); p.freebuf(VpA); p.freebuf(VpB)
    p.freebuf(mk.PTb)
    del mk.PTb

    WoT = mk.load_weight_N("WoTa", io["L1_attn_wo"])
    mk.realloc_x()
    out_proj_residual(mk, OnT, WoT, "OnTa", "WoTa", "l1a", post_eps=1e-5)
    p.freebuf(OnT); p.freebuf(WoT)


def l1f_shiftbuf(mk, name):
    p = mk.p
    X = mk.X
    sb = p.alloc(name, [128, NT * 512], F32)
    p.emit("dve", lambda e, sb=sb: e.memset(sb.t[0:1, 0:512], 0.0),
           writes=[f"{name}#0"])
    for i in range(NT):
        eng, ch = p.hch()
        p.emit(eng, lambda e, i=i, sb=sb, X=X: e.dma_start(
            out=sb.t[1:128, i * 512:(i + 1) * 512],
            in_=X[0:127, i * D + 512:(i + 1) * D]),
            reads=[f"X#{i}"], writes=[f"{name}#{i}"], ch=ch)
        if i > 0:
            p.emit("sp", lambda e, i=i, sb=sb, X=X: e.dma_start(
                out=sb.t[0:1, i * 512:(i + 1) * 512],
                in_=X[127:128, (i - 1) * D + 512:i * D]),
                reads=[f"X#{i-1}"], writes=[f"{name}#{i}"], ch="qsp")
    return sb


def l1f_sub_attn(mk, l, n_sub):
    p = mk.p
    io = mk.io
    H, HD = 8, 64
    NCQ = 4
    wqkv = mk.load_weight_N(f"wqkv{l}", io["L1_ffn_wqkv"][l], O=1536)
    qs8 = p.alloc("qs8", [128, 512], F32)
    ks8 = p.alloc("ks8", [128, 512], F32)
    for tab, nm in ((qs8, "L1_ffn_q_scale"), (ks8, "L1_ffn_k_scale")):
        srcap = bass.AP(io[nm].tensor, l * 64, [[0, 128], [0, 8], [1, 64]])
        p.emit("sp", lambda e, tab=tab, srcap=srcap: e.dma_start(
            out=tab.ap, in_=srcap), writes=[tab.name], ch="qsp")
    sb = l1f_shiftbuf(mk, f"shb{l}")
    hT = p.alloc(f"hT{l}", [128, NC_ * S], BF16)
    def src_ap(ic, t):
        if ic < 4:
            return mk.X[:, t * D + ic * 128: t * D + ic * 128 + 128]
        return sb.t[:, t * 512 + (ic - 4) * 128: t * 512 + (ic - 4) * 128 + 128]
    def src_key(ic, t):
        return f"X#{t}" if ic < 4 else f"shb{l}#{t}"
    mk.full_transpose_cast(src_ap, src_key, hT, lambda ic, t: f"hT{l}#{ic}")
    p.freebuf(sb)
    if l == 0:
        mk.spill_x("l1f_in")
    else:
        mk.spill_x(f"l1f_s{l}a")

    xtabs = {}
    for nm in ("xq_c", "xq_s", "xk_c", "xk_s"):
        key = "dram_" + nm
        if key not in mk.cbuf:
            arr = mk.consts_np[nm].astype(np.float32)
            mk.cbuf[key] = mk.nc.inline_tensor(np.ascontiguousarray(arr),
                                               name=f"c{mk.uid()}_{nm}")
        xtabs[nm] = mk.cbuf[key]
    TBL = p.alloc("xtbl", [128, 2 * 4 * 512], F32)
    QTf = p.alloc(f"QTf{l}", [128, NCQ * S], BF16)
    KTf = p.alloc(f"KTf{l}", [128, NCQ * S], BF16)
    VW = H * 65
    Vp = p.alloc(f"Vpf{l}", [128, NT * VW], BF16)
    ones_ap = bass.AP(Vp.t, 64, [[NT * VW, 128], [VW, NT], [65, H]])
    p.emit("dve", lambda e: e.memset(ones_ap, 1.0),
           writes=[f"Vpf{l}#{kt}" for kt in range(NT)])
    nq = p.alloc("nq", [128, NT * 8], F32)
    nk = p.alloc("nk", [128, NT * 8], F32)
    sc1 = p.alloc("sc1", [128, 1024], F32)
    ss8 = p.alloc("ss8", [128, 48], F32)
    qh2 = p.alloc("qh2", [128, 1024], F32)
    qro = p.alloc("qro", [128, 1024], F32)
    identf = mk.const("identf")

    def emit_proj(i):
        b0 = (i % 2) * 3
        for c3 in range(3):
            members = []
            for ic in range(NC_):
                lhs = hT.t[:, ic * S + i * 128: ic * S + i * 128 + 128]
                rhs = wqkv.t[:, ic * 1536 + c3 * 512: ic * 1536 + (c3 + 1) * 512]
                members.append((lambda e, lhs=lhs, rhs=rhs, b=b0 + c3, ic=ic:
                                e.matmul(p.PS[:, b * 512:(b + 1) * 512], lhs, rhs,
                                         start=(ic == 0), stop=(ic == NC_ - 1)),
                                [f"hT{l}#{ic}", f"wqkv{l}#{ic}"]))
            mk.mm_group(f"ps{b0 + c3}", members)
        ts0 = (i % 2) * 2048
        for ti, nm in enumerate(("xq_c", "xq_s", "xk_c", "xk_s")):
            h64 = xtabs[nm]
            srcap = bass.AP(h64, i * 64, [[NT * 64, 128], [0, 8], [1, 64]])
            eng, ch = p.hch()
            p.emit(eng, lambda e, ti=ti, srcap=srcap, ts0=ts0: e.dma_start(
                out=TBL.t[:, ts0 + ti * 512: ts0 + (ti + 1) * 512], in_=srcap),
                reads=[f"xtbl#{i % 2}"], writes=[f"xtbl#{i % 2}"], ch=ch)

    def emit_process(i):
        b0 = (i % 2) * 3
        ts0 = (i % 2) * 2048
        chains = []
        for qk in range(2):
            ops = []
            bank = b0 + qk
            bk = f"ps{bank}"
            psl = p.PS[:, bank * 512:(bank + 1) * 512]
            s0 = qk * 512
            n0 = qk * 24
            sck, ssk, qhk, qrk = (f"sc1#{qk}", f"ss8#{qk}", f"qh2#{qk}",
                                  f"qro#{qk}")
            sca = sc1.t[:, s0:s0 + 512]
            qha = qh2.t[:, s0:s0 + 512]
            qra = qro.t[:, s0:s0 + 512]
            ops.append(lambda psl=psl, sca=sca, bk=bk, sck=sck: p.emit(
                "act", lambda e, psl=psl, sca=sca: e.activation(
                    sca, psl, AF.Square), reads=[bk], writes=[sck]))
            ops.append(lambda sca=sca, n0=n0, sck=sck, ssk=ssk: p.emit(
                "dve", lambda e, sca=sca, n0=n0: e.tensor_reduce(
                    ss8[:, n0:n0 + 8], sca.rearrange("p (h d) -> p h d", d=64),
                    axis=AX.X, op=ALU.add), reads=[sck], writes=[ssk]))
            ops.append(lambda n0=n0, ssk=ssk: p.emit(
                "act", lambda e, n0=n0: e.activation(
                    ss8[:, n0 + 8:n0 + 16], ss8[:, n0:n0 + 8], AF.Sqrt),
                reads=[ssk], writes=[ssk]))
            ops.append(lambda n0=n0, ssk=ssk: p.emit(
                "dve", lambda e, n0=n0: e.tensor_scalar_max(
                    ss8[:, n0 + 8:n0 + 16], ss8[:, n0 + 8:n0 + 16], 1e-12),
                reads=[ssk], writes=[ssk]))
            ops.append(lambda n0=n0, ssk=ssk: p.emit(
                "dve", lambda e, n0=n0: e.reciprocal(
                    ss8[:, n0 + 16:n0 + 24], ss8[:, n0 + 8:n0 + 16]),
                reads=[ssk], writes=[ssk]))
            for h in range(H):
                ops.append(lambda psl=psl, qha=qha, h=h, n0=n0, bk=bk,
                           ssk=ssk, qhk=qhk: p.emit(
                    "dve", lambda e, psl=psl, qha=qha, h=h, n0=n0: e.tensor_scalar(
                        qha[:, h * 64:(h + 1) * 64], psl[:, h * 64:(h + 1) * 64],
                        ss8[:, n0 + 16 + h:n0 + 17 + h], None, ALU.mult),
                    reads=[bk, ssk], writes=[qhk]))
            stab = qs8 if qk == 0 else ks8
            ops.append(lambda stab=stab, qha=qha, qhk=qhk: p.emit(
                "dve", lambda e, stab=stab, qha=qha: e.tensor_mul(
                    qha, qha, stab.ap), reads=[qhk, stab.name], writes=[qhk]))
            ops.append(lambda sca=sca, qha=qha, sck=sck, qhk=qhk: p.emit(
                "act", lambda e, sca=sca, qha=qha: e.activation(
                    sca, qha, AF.Square), reads=[qhk], writes=[sck]))
            nbuf, nname = (nq, "nq") if qk == 0 else (nk, "nk")
            ops.append(lambda nbuf=nbuf, sca=sca, sck=sck, nname=nname: p.emit(
                "dve", lambda e, nbuf=nbuf, sca=sca, i=i: e.tensor_reduce(
                    nbuf.t[:, i * 8:(i + 1) * 8],
                    sca.rearrange("p (h d) -> p h d", d=64),
                    axis=AX.X, op=ALU.add), reads=[sck], writes=[nname]))
            ct = TBL.t[:, ts0 + (qk * 2) * 512: ts0 + (qk * 2 + 1) * 512]
            st_ = TBL.t[:, ts0 + (qk * 2 + 1) * 512: ts0 + (qk * 2 + 2) * 512]
            swap_ap = bass.AP(qh2.t, s0 + 32,
                              [[1024, 128], [64, 8], [-32, 2], [1, 32]])
            ops.append(lambda ct=ct, qra=qra, qha=qha, qhk=qhk, qrk=qrk: p.emit(
                "dve", lambda e, ct=ct, qra=qra, qha=qha: e.tensor_mul(
                    qra, qha, ct), reads=[qhk, f"xtbl#{i % 2}"], writes=[qrk]))
            ops.append(lambda st_=st_, swap_ap=swap_ap, sca=sca, qhk=qhk,
                       sck=sck: p.emit(
                "dve", lambda e, st_=st_, swap_ap=swap_ap, sca=sca: e.tensor_tensor(
                    sca, swap_ap, st_, ALU.mult),
                reads=[qhk, f"xtbl#{i % 2}"], writes=[sck]))
            ops.append(lambda qra=qra, sca=sca, qrk=qrk, sck=sck: p.emit(
                "dve", lambda e, qra=qra, sca=sca: e.tensor_add(qra, qra, sca),
                reads=[qrk, sck], writes=[qrk]))
            TF = QTf if qk == 0 else KTf
            tfn = f"QTf{l}" if qk == 0 else f"KTf{l}"
            bank2 = 6 + qk
            bk2 = f"ps{bank2}"
            for c in range(4):
                ps2 = p.PS[:, bank2 * 512 + c * 128: bank2 * 512 + (c + 1) * 128]
                ops.append(lambda ps2=ps2, qra=qra, c=c, bk2=bk2, qrk=qrk:
                           mk.transpose128(bk2, ps2, qra[:, c * 128:(c + 1) * 128],
                                           [qrk], identf, wover=(c > 0)))
            srcap = bass.AP(p.PS, bank2 * 512, [[4096, 128], [1, 512]])
            dstap = bass.AP(TF.t, i * 128, [[NCQ * S, 128], [S, 4], [1, 128]])
            ops.append(lambda srcap=srcap, dstap=dstap, bk2=bk2, tfn=tfn:
                       mk.evac(dstap, srcap, reads=[bk2],
                               writes=[f"{tfn}#{c}" for c in range(4)]))
            chains.append(ops)
        for a, b in zip(chains[0], chains[1]):
            a()
            b()
        vb = b0 + 2
        vsrc = p.PS[:, vb * 512:(vb + 1) * 512].rearrange("p (h d) -> p h d", d=64)
        vdst = Vp.t[:, i * VW: i * VW + VW].rearrange("p (h d) -> p h d", d=65)[:, :, 0:64]
        p.emit("dve", lambda e, vsrc=vsrc, vdst=vdst: e.tensor_copy(vdst, vsrc),
               reads=[f"ps{vb}"], writes=[f"Vpf{l}#{i}"])

    for i in range(NT):
        emit_proj(i)
        if i > 0:
            emit_process(i - 1)
    emit_process(NT - 1)
    p.freebuf(hT); p.freebuf(wqkv); p.freebuf(TBL)
    p.freebuf(sc1); p.freebuf(qh2); p.freebuf(qro)
    p.freebuf(qs8); p.freebuf(ks8)

    for w in range(4):
        for qk in range(2):
            nbuf, nname = (nq, "nq") if qk == 0 else (nk, "nk")
            t0_ = 4 * w if qk == 0 else max(0, 4 * w - 4)
            t1_ = 4 * w + 4
            p.emit("dve", lambda e, nbuf=nbuf, t0_=t0_: e.tensor_copy(
                ss8[:, 0:8], nbuf.t[:, t0_ * 8:(t0_ + 1) * 8]),
                reads=[nname], writes=["ss8#0"])
            for t_ in range(t0_ + 1, t1_):
                p.emit("dve", lambda e, nbuf=nbuf, t_=t_: e.tensor_max(
                    ss8[:, 0:8], ss8[:, 0:8], nbuf.t[:, t_ * 8:(t_ + 1) * 8]),
                    reads=["ss8#0", nname], writes=["ss8#0"])
            mk.transpose128("ps7", p.PS[0:8, 7 * 512:7 * 512 + 128],
                            ss8[:, 0:8], ["ss8#0"], identf)
            p.emit("dve", lambda e: e.tensor_reduce(
                mk.bb[0:8, 0:1] if False else mk.bb[0:8, 16 + 0:17],
                p.PS[0:8, 7 * 512:7 * 512 + 128], axis=AX.X, op=ALU.max),
                reads=["ps7"], writes=["bb"])
            p.emit("sp", lambda e, w=w, qk=qk: e.dma_start(
                out=mk.brow[0:1, 64 + qk * 8: 64 + (qk + 1) * 8],
                in_=mk.bb[0:8, 16:17]),
                reads=["bb"], writes=["brow"], ch="qsp")
        p.emit("dve", lambda e, w=w: e.tensor_tensor(
            mk.brow[0:1, w * 16: w * 16 + 8], mk.brow[0:1, 64:72],
            mk.brow[0:1, 72:80], ALU.mult), reads=["brow"], writes=["brow"])
        p.emit("act", lambda e, w=w: e.activation(
            mk.brow[0:1, w * 16: w * 16 + 8], mk.brow[0:1, w * 16: w * 16 + 8],
            AF.Sqrt), reads=["brow"], writes=["brow"])
        p.emit("dve", lambda e, w=w: e.tensor_scalar_mul(
            mk.brow[0:1, w * 16: w * 16 + 8], mk.brow[0:1, w * 16: w * 16 + 8],
            8.0), reads=["brow"], writes=["brow"])
    p.freebuf(nq); p.freebuf(nk); p.freebuf(ss8)

    OnT = p.alloc(f"OnTf{l}", [128, NCQ * S], BF16)
    mk.PTb = p.alloc("PTb", [128, 3072], BF16)
    kt_lists = []
    for w in range(4):
        lst = []
        if w > 0:
            for kt in range(4 * (w - 1), 4 * w):
                lst.append((kt, "band_prev", kt * 128 - 512 * (w - 1)))
        for kt in range(4 * w, 4 * w + 4):
            lst.append((kt, "band", kt * 128 - 512 * w))
        kt_lists.append(lst)
    attn_loop(mk, QTf, KTf,
              lambda kt: (Vp, kt * VW, f"Vpf{l}#{kt}"), OnT, H, S, kt_lists,
              names=(f"QTf{l}", f"KTf{l}", f"Vpf{l}", f"OnTf{l}"),
              bias_col=lambda h, qb: qb * 16 + h)
    p.freebuf(QTf); p.freebuf(KTf); p.freebuf(Vp); p.freebuf(mk.PTb)
    del mk.PTb

    WoT = mk.load_weight_N(f"wof{l}", io["L1_ffn_wo"][l], O=D)
    mk.realloc_x()
    out_proj_residual(mk, OnT, WoT, f"OnTf{l}", f"wof{l}",
                      "l1f_in" if l == 0 else f"l1f_s{l}a",
                      post_eps=None, nchunks=NCQ)
    p.freebuf(OnT); p.freebuf(WoT)


def l1f_sub_ffn(mk, l, n_sub):
    p = mk.p
    io = mk.io
    HID = 2730
    NU = 22
    sb = l1f_shiftbuf(mk, f"sh2b{l}")
    gtab = p.alloc("l1f_g", [128, D], F32)
    btab = p.alloc("l1f_b", [128, D], F32)
    for tab, nm in ((gtab, "L1_ffn_ln_g"), (btab, "L1_ffn_ln_b")):
        srcap = bass.AP(io[nm].tensor, l * D, [[0, 128], [1, D]])
        p.emit("sp", lambda e, tab=tab, srcap=srcap: e.dma_start(
            out=tab.ap, in_=srcap), writes=[tab.name], ch="qsp")
    lnhT = p.alloc(f"lnhT{l}", [128, NC_ * S], BF16)
    hln = p.alloc("hln", [128, 2 * D], F32)
    st = mk.stat
    epsb = mk.epsbuf(1e-5)
    identf = mk.const("identf")
    for i in range(NT):
        par = i % 2
        b0 = par * 16
        sk = f"stat#{par}"
        hk = f"hln#{par}"
        hla = hln.t[:, par * D:(par + 1) * D]
        x1ap = mk.X[:, i * D: i * D + 512]
        x2ap = sb.t[:, i * 512:(i + 1) * 512]
        xk = f"X#{i}"
        sk2 = f"sh2b{l}#{i}"
        p.emit("dve", lambda e, x1ap=x1ap, b0=b0: e.bn_stats(
            st[:, b0:b0 + 6], x1ap), reads=[xk], writes=[sk])
        p.emit("dve", lambda e, x2ap=x2ap, b0=b0: e.bn_stats(
            st[:, b0 + 6:b0 + 12], x2ap), reads=[sk2], writes=[sk])
        p.emit("dve", lambda e, b0=b0: e.bn_aggr(st[:, b0 + 12:b0 + 14],
                                                 st[:, b0:b0 + 12]),
               reads=[sk], writes=[sk])
        p.emit("act", lambda e, b0=b0: e.activation(
            st[:, b0 + 14:b0 + 15], st[:, b0 + 13:b0 + 14], AF.Sqrt,
            bias=epsb.ap), reads=[sk, epsb.name], writes=[sk])
        p.emit("dve", lambda e, b0=b0: e.reciprocal(st[:, b0 + 15:b0 + 16],
                                                    st[:, b0 + 14:b0 + 15]),
               reads=[sk], writes=[sk])
        p.emit("dve", lambda e, b0=b0: e.scalar_tensor_tensor(
            st[:, b0 + 12:b0 + 13], st[:, b0 + 12:b0 + 13], -1.0,
            st[:, b0 + 15:b0 + 16], ALU.mult, ALU.mult),
            reads=[sk], writes=[sk])
        p.emit("act", lambda e, x1ap=x1ap, hla=hla, b0=b0: e.activation(
            hla[:, 0:512], x1ap, AF.Identity, bias=st[:, b0 + 12:b0 + 13],
            scale=st[:, b0 + 15:b0 + 16]), reads=[xk, sk], writes=[hk])
        p.emit("act", lambda e, x2ap=x2ap, hla=hla, b0=b0: e.activation(
            hla[:, 512:1024], x2ap, AF.Identity, bias=st[:, b0 + 12:b0 + 13],
            scale=st[:, b0 + 15:b0 + 16]), reads=[sk2, sk], writes=[hk])
        p.emit("dve", lambda e, hla=hla: e.tensor_mul(hla, hla, gtab.ap),
               reads=[hk, gtab.name], writes=[hk])
        p.emit("dve", lambda e, hla=hla: e.tensor_add(hla, hla, btab.ap),
               reads=[hk, btab.name], writes=[hk])
        for ic0 in range(0, NC_, 4):
            bank = par * 2 + ic0 // 4
            bk = f"ps{bank}"
            for j in range(4):
                ic = ic0 + j
                ps = p.PS[:, bank * 512 + j * 128: bank * 512 + (j + 1) * 128]
                mk.transpose128(bk, ps, hla[:, ic * 128:(ic + 1) * 128],
                                [hk], identf, wover=(j > 0))
            srcap = bass.AP(p.PS, bank * 512, [[4096, 128], [1, 512]])
            dstap = bass.AP(lnhT.t, ic0 * S + i * 128,
                            [[NC_ * S, 128], [S, 4], [1, 128]])
            mk.evac(dstap, srcap, reads=[bk],
                    writes=[f"lnhT{l}#{ic0 + j}" for j in range(4)])
    p.freebuf(sb); p.freebuf(hln); p.freebuf(gtab); p.freebuf(btab)
    mk.spill_x(f"l1f_s{l}f")

    gatedT = p.alloc(f"gT{l}", [128, NU * 1024], BF16)
    w2b = p.alloc(f"w2b{l}", [128, NU * D], BF16)
    b2tab = p.alloc("l1f_b2", [128, D], F32)
    srcap = bass.AP(io["L1_ffn_b2"].tensor, l * D, [[0, 128], [1, D]])
    p.emit("sp", lambda e, srcap=srcap: e.dma_start(out=b2tab.ap, in_=srcap),
           writes=[b2tab.name], ch="qsp")
    w2st = p.alloc("w2st", [128, 2 * D], F32)
    for uc in range(NU):
        u0 = uc * 128
        usz = min(128, HID - u0)
        par = uc % 2
        skey = f"w2st#{par}"
        sta = w2st.t[0:usz, par * D:(par + 1) * D]
        eng, ch = p.hch()
        p.emit(eng, lambda e, u0=u0, usz=usz, sta=sta: e.dma_start(
            out=sta, in_=io["L1_ffn_w2"][l, u0:u0 + usz, :]),
            reads=[skey], writes=[skey], ch=ch)
        mk.evac(w2b.t[0:usz, uc * D:(uc + 1) * D], sta,
                reads=[skey], writes=[f"w2b{l}#{uc}"])
    p.freebuf(w2st)

    w1st = p.alloc("w1st", [128, 2 * NC_ * 128], F32)
    w1a = p.alloc("w1a", [128, 2 * NC_ * 128], BF16)
    w1g = p.alloc("w1g", [128, 2 * NC_ * 128], BF16)
    b1c = p.alloc("b1c", [128, 4], F32)
    gl = p.alloc("glscr", [128, 2048], F32)
    W1 = io["L1_ffn_w1"]
    B1 = io["L1_ffn_b1"]
    for half in range(2):
        tbase = half * 8

        def emit_h1(uc, tbase=tbase):
            u0 = uc * 128
            usz = min(128, HID - u0)
            par = uc % 2
            ws0 = par * 1024
            for side, coff in ((0, 0), (1, HID)):
                wbuf = (w1a, w1g)[side]
                wname = ("w1a", "w1g")[side] + f"#{par}"
                srcap = bass.AP(W1.tensor, l * (D * 5460) + coff + u0,
                                [[5460, 128], [128 * 5460, NC_], [1, usz]])
                eng, ch = p.hch()
                p.emit(eng, lambda e, srcap=srcap, usz=usz, ws0=ws0: e.dma_start(
                    out=bass.AP(w1st.t, ws0, [[2048, 128], [128, NC_], [1, usz]]),
                    in_=srcap), reads=[f"w1st#{par}"], writes=[f"w1st#{par}"],
                    ch=ch)
                p.emit("dve", lambda e, wbuf=wbuf, ws0=ws0: e.tensor_copy(
                    wbuf.t[:, ws0:ws0 + 1024], w1st.t[:, ws0:ws0 + 1024]),
                    reads=[f"w1st#{par}"], writes=[wname])
            p.emit("sp", lambda e, u0=u0, usz=usz, par=par: e.dma_start(
                out=b1c[0:usz, 2 * par:2 * par + 1],
                in_=B1[l, u0:u0 + usz].unsqueeze(1)),
                reads=[f"b1c#{par}"], writes=[f"b1c#{par}"], ch="qsp")
            p.emit("sp", lambda e, u0=u0, usz=usz, par=par: e.dma_start(
                out=b1c[0:usz, 2 * par + 1:2 * par + 2],
                in_=B1[l, HID + u0:HID + u0 + usz].unsqueeze(1)),
                reads=[f"b1c#{par}"], writes=[f"b1c#{par}"], ch="qsp")
            pb0 = par * 4
            for side in (0, 1):
                wbuf = (w1a, w1g)[side]
                wname = ("w1a", "w1g")[side] + f"#{par}"
                bb0 = pb0 + side * 2
                for tb in range(2):
                    members = []
                    for ic in range(NC_):
                        lhs = wbuf.t[:, ws0 + ic * 128: ws0 + ic * 128 + usz]
                        rhs = lnhT.t[:, ic * S + tbase * 128 + tb * 512:
                                     ic * S + tbase * 128 + (tb + 1) * 512]
                        members.append((lambda e, lhs=lhs, rhs=rhs, b=bb0 + tb,
                                        ic=ic, usz=usz:
                                        e.matmul(p.PS[0:usz, b * 512:(b + 1) * 512],
                                                 lhs, rhs, start=(ic == 0),
                                                 stop=(ic == NC_ - 1)),
                                        [wname, f"lnhT{l}#{ic}"]))
                    mk.mm_group(f"ps{bb0 + tb}", members)

        def emit_gate(uc):
            u0 = uc * 128
            usz = min(128, HID - u0)
            par = uc % 2
            pb0 = par * 4
            p.emit("act", lambda e, usz=usz, pb0=pb0, par=par: e.activation(
                gl[0:usz, par * 1024:(par + 1) * 1024],
                p.PS[0:usz, (pb0 + 2) * 512:(pb0 + 4) * 512], AF.Gelu,
                bias=b1c[0:usz, 2 * par + 1:2 * par + 2]),
                reads=[f"ps{pb0 + 2}", f"ps{pb0 + 3}", f"b1c#{par}"],
                writes=[f"glscr#{par}"])
            dstap = gatedT.t[0:usz, uc * 1024:(uc + 1) * 1024]
            p.emit("dve", lambda e, dstap=dstap, usz=usz, pb0=pb0, par=par:
                   e.scalar_tensor_tensor(
                       dstap, p.PS[0:usz, pb0 * 512:(pb0 + 2) * 512],
                       b1c[0:usz, 2 * par:2 * par + 1],
                       gl[0:usz, par * 1024:(par + 1) * 1024],
                       ALU.add, ALU.mult),
                   reads=[f"ps{pb0}", f"ps{pb0 + 1}", f"b1c#{par}",
                          f"glscr#{par}"], writes=[f"gT{l}#{uc}"])

        for uc in range(NU):
            emit_h1(uc)
            if uc > 0:
                emit_gate(uc - 1)
        emit_gate(NU - 1)
        out_proj_residual(
            mk, gatedT, w2b, f"gT{l}", f"w2b{l}", f"l1f_s{l}f",
            post_eps=None, bias_tab=b2tab,
            nchunks=NU, chunk_k=lambda ic: min(128, HID - ic * 128),
            tiles=range(tbase, tbase + 8), ont_span=1024,
            dst_spill=f"l1f_x{l}fd")
    p.freebuf(gatedT); p.freebuf(w1st); p.freebuf(w1a); p.freebuf(w1g)
    p.freebuf(b1c); p.freebuf(gl); p.freebuf(w2b); p.freebuf(b2tab)
    p.freebuf(lnhT)
    last = (l == n_sub - 1)
    mk.realloc_x()
    ln_pend = None
    for i in range(NT):
        mk.unspill_x_tile(f"l1f_x{l}fd", i)
        if last:
            xb = mk.xtra if i % 2 == 0 else mk.xtra_b
            xbk = "xtra" if i % 2 == 0 else "xtra_b"
            sp = mk.spill["l1f_in"]
            eng, ch = p.hch()
            p.emit(eng, lambda e, i=i, sp=sp, xb=xb: e.dma_start(
                out=xb.ap, in_=sp[i * 128:(i + 1) * 128, :]),
                reads=["sp_l1f_in#%d" % i], writes=[xbk], ch=ch)
            xap = mk.X[:, i * D:(i + 1) * D]
            p.emit("dve", lambda e, xap=xap, xb=xb: e.tensor_add(xap, xap, xb.ap),
                   reads=[f"X#{i}", xbk], writes=[f"X#{i}"])
            fa, fb = mk.ln_tile_stages(i, 1e-5)
            fa()
            if ln_pend is not None:
                ln_pend()
            ln_pend = fb
    if ln_pend is not None:
        ln_pend()


def block_l1_ffn(mk):
    p = mk.p
    if mk.rA is not None:
        p.freebuf(mk.rA); p.freebuf(mk.rB); p.freebuf(mk.rC)
        mk.rA = mk.rB = mk.rC = None
    mk.xtra = p.alloc("xtra", [128, D], F32, side="high")
    mk.xtra2 = p.alloc("xtra2", [128, D], F32, side="high")
    mk.xtra_b = p.alloc("xtra_b", [128, D], F32, side="high")
    mk.xtra2_b = p.alloc("xtra2_b", [128, D], F32, side="high")
    for l in range(2):
        l1f_sub_attn(mk, l, 2)
        l1f_sub_ffn(mk, l, 2)


def build_nc(upto="all", n_cores=8):
    nc = bass.Bass("TRN2", target_bir_lowering=False, debug=False)
    names_2d = {
        "L0_attn_Wq": (D, D), "L0_attn_Wk": (D, D), "L0_attn_Wv": (D, D),
        "L0_attn_Wo": (D, D), "L0_ffn_weight": (D, D),
        "L1_attn_wq_sem": (D, 512), "L1_attn_wk_sem": (D, 512),
        "L1_attn_wq_geo": (D, 512), "L1_attn_wk_geo": (D, 512),
        "L1_attn_wv": (D, D), "L1_attn_wo": (D, D),
    }
    io = {}
    io["x"] = nc.dram_tensor("x", [S, D], F32, kind="ExternalInput").ap()
    for n, shp in names_2d.items():
        io[n] = nc.dram_tensor(n, list(shp), F32, kind="ExternalInput").ap()
    for n, shp in {
        "L0_ffn_ln_gamma": (D,), "L0_ffn_ln_beta": (D,),
        "L1_attn_k_sem_null": (512,), "L1_attn_k_geo_null": (512,),
        "L1_attn_v_null": (D,),
    }.items():
        io[n] = nc.dram_tensor(n, list(shp), F32, kind="ExternalInput").ap()
    for n, shp in {
        "L1_ffn_wqkv": (2, D, 1536), "L1_ffn_q_scale": (2, 64),
        "L1_ffn_k_scale": (2, 64), "L1_ffn_wo": (2, 512, D),
        "L1_ffn_ln_g": (2, D), "L1_ffn_ln_b": (2, D),
        "L1_ffn_w1": (2, D, 5460), "L1_ffn_b1": (2, 5460),
        "L1_ffn_w2": (2, 2730, D), "L1_ffn_b2": (2, D),
    }.items():
        io[n] = nc.dram_tensor(n, list(shp), F32, kind="ExternalInput").ap()
    io["out"] = nc.dram_tensor("out", [S, D], F32, kind="ExternalOutput").ap()

    mk = MK(nc, io)
    X0 = mk.X
    for i in range(NT):
        mk.p.emit("sp", lambda e, i=i, X0=X0: e.dma_start(
            out=X0[:, i * D:(i + 1) * D], in_=io["x"][i * 128:(i + 1) * 128, :]),
            writes=[f"X#{i}"], ch="qsp")
    block_l0_attn(mk)
    if upto == "l0a":
        mk.write_out()
        mk.finalize()
        return nc, mk
    block_l0_ffn(mk, n_cores=n_cores)
    if upto == "l0f":
        mk.write_out()
        mk.finalize()
        return nc, mk
    block_l1_attn(mk)
    if upto == "l1a":
        mk.write_out()
        mk.finalize()
        return nc, mk
    block_l1_ffn(mk)
    mk.write_out()
    mk.finalize()
    return nc, mk


def kernel(**inputs):
    n_cores = 8
    nc, mk = build_nc(upto="all", n_cores=n_cores)
    x_full = np.ascontiguousarray(inputs["L0_attn_x"], dtype=np.float32)
    weight_names = [
        "L0_attn_Wq", "L0_attn_Wk", "L0_attn_Wv", "L0_attn_Wo",
        "L0_ffn_weight", "L0_ffn_ln_gamma", "L0_ffn_ln_beta",
        "L1_attn_wq_sem", "L1_attn_wk_sem", "L1_attn_wq_geo", "L1_attn_wk_geo",
        "L1_attn_wv", "L1_attn_wo", "L1_attn_k_sem_null", "L1_attn_k_geo_null",
        "L1_attn_v_null", "L1_ffn_wqkv", "L1_ffn_q_scale", "L1_ffn_k_scale",
        "L1_ffn_wo", "L1_ffn_ln_g", "L1_ffn_ln_b", "L1_ffn_w1", "L1_ffn_b1",
        "L1_ffn_w2", "L1_ffn_b2",
    ]
    wmap = {n: np.ascontiguousarray(inputs[n], dtype=np.float32)
            for n in weight_names}
    in_maps = []
    for c in range(n_cores):
        m = {"x": np.ascontiguousarray(x_full[c])}
        m.update(wmap)
        in_maps.append(m)
    res = run_bass_kernel_spmd(nc, in_maps, core_ids=list(range(n_cores)))
    out = np.stack([np.asarray(r["out"], dtype=np.float32)
                    for r in res.results], axis=0)
    return out


# version 18
# speedup vs baseline: 1.1227x; 1.0846x over previous
import numpy as np
import ml_dtypes
from contextlib import ExitStack

import concourse.bass as bass
import concourse.mybir as mybir
from concourse.bass_utils import run_bass_kernel_spmd

F32 = mybir.dt.float32
BF16 = mybir.dt.bfloat16
AF = mybir.ActivationFunctionType
ALU = mybir.AluOpType
AX = mybir.AxisListType

S = 2048
D = 1024
NT = S // 128
NC_ = D // 128
NEG = -1.0e30

ENGINES = ("pe", "dve", "act", "pool", "sp")
DMA_CH = ("qsp", "qpool", "qact")
DMA_SLOTS = {"qsp": 8, "qpool": 8, "qact": 8}
DMA_SLOT_NAMES = tuple(f"{q}_{i}" for q in DMA_CH for i in range(DMA_SLOTS[q]))
ALL_CH = ENGINES + DMA_SLOT_NAMES + ("cc",)
SBUF_LO = 16512
SBUF_HI = 229312
X_FIXED = 127296


def host_constants():
    c = {}
    c["identf"] = np.eye(128, dtype=np.float32)
    c["identb"] = np.eye(128, dtype=np.float32)
    t = np.arange(S, dtype=np.float64)

    dk = 64
    inv = 1.0 / (10000.0 ** (np.arange(0, dk, 2, dtype=np.float64) / dk))
    ang = np.outer(t, inv)
    cos, sin = np.cos(ang), np.sin(ang)
    cf = np.zeros((64, S)); ssw = np.zeros((64, S))
    cf[0::2] = cos.T; cf[1::2] = cos.T
    ssw[0::2] = sin.T; ssw[1::2] = -sin.T
    c["l0_cos"] = np.tile(cf, (2, 1)).astype(np.float32)
    c["l0_sinsw"] = np.tile(ssw, (2, 1)).astype(np.float32)

    ghd = 32
    ginv = 1.0 / (10000.0 ** (np.arange(0, ghd, 2, dtype=np.float64) / ghd))
    gang = np.outer(t, ginv)
    gcos, gsin = np.cos(gang), np.sin(gang)
    hc = np.zeros((64, S)); hs = np.zeros((64, S))
    hc[0:32] = 1.0
    hc[32:48] = gcos.T; hc[48:64] = gcos.T
    hs[32:48] = gsin.T; hs[48:64] = -gsin.T
    c["l1a_cos"] = np.tile(hc, (2, 1)).astype(np.float32)
    c["l1a_sinsw"] = np.tile(hs, (2, 1)).astype(np.float32)

    dh = 64
    half = np.arange(0, dh, 2, dtype=np.float64)
    xfreq = np.outer(t, 1.0 / (10000.0 ** (half / dh)))
    xf2 = np.concatenate([xfreq, xfreq], axis=1)
    base = (half + 0.4 * dh) / (1.4 * dh)
    power = (t - S // 2) / 256.0
    scl = base[None, :] ** power[:, None]
    scl2 = np.concatenate([scl, scl], axis=1)
    xc, xs_ = np.cos(xf2), np.sin(xf2)
    qc = xc * scl2 * 8.0
    qs_ = np.concatenate([-(xs_[:, :32] * scl), (xs_[:, 32:] * scl)], axis=1) * 8.0
    kc = xc / scl2
    ks_ = np.concatenate([-(xs_[:, :32] / scl), (xs_[:, 32:] / scl)], axis=1)
    for nm, tab in (("xq_c", qc), ("xq_s", qs_), ("xk_c", kc), ("xk_s", ks_)):
        m = np.zeros((128, NT * dh), dtype=np.float32)
        for i in range(NT):
            m[:, i * dh:(i + 1) * dh] = tab[i * 128:(i + 1) * 128]
        c[nm] = m

    j = np.arange(-512, 512)[None, :]
    k = np.arange(128)[:, None]
    c["band"] = np.where(k <= j, 0.0, NEG).astype(np.float32)
    c["band1"] = np.where(k <= j + 1, 0.0, NEG).astype(np.float32)
    c["band_prev"] = np.where(k >= j, 0.0, NEG).astype(np.float32)
    c["padcol"] = np.concatenate(
        [np.zeros((1, 1)), np.full((127, 1), NEG)]).astype(np.float32)
    c["padcol64"] = np.tile(c["padcol"], (1, 64))
    c["blockdiag2"] = np.zeros((128, 128), dtype=np.float32)
    c["blockdiag2"][0:64, 0] = 1.0
    c["blockdiag2"][64:128, 1] = 1.0
    c["ones_1x128"] = np.ones((1, 128), dtype=np.float32)
    c["ones128"] = np.ones((128, 128), dtype=np.float32)
    c["neg_ones_1x128"] = np.full((1, 128), -1.0, dtype=np.float32)
    return c


class Buf:
    __slots__ = ("name", "t", "off", "nbytes", "ghosts")
    def __init__(self, name, t, off, nbytes, ghosts):
        self.name, self.t, self.off, self.nbytes, self.ghosts = name, t, off, nbytes, ghosts
    def __getitem__(self, idx):
        return self.t[idx]
    @property
    def ap(self):
        return self.t[:]


class Prog:
    def __init__(self, nc):
        self.nc = nc
        self.es = ExitStack()
        self.ops = {e: [] for e in ENGINES}
        self.sem = {n: self.es.enter_context(nc.semaphore("sem_" + n)) for n in ALL_CH}
        self.cnt = {n: 0 for n in ALL_CH}
        self.dma_rr = {q: 0 for q in DMA_CH}
        self.clockat = {}
        self.eng_clock = {e: {} for e in ENGINES}
        self.last_w = {}
        self.readers = {}
        self.free = [(SBUF_LO, SBUF_HI)]
        self.ghost = []
        self.high_water = 0
        self.bufs = {}
        self.PS = self.es.enter_context(nc.psum_tensor("PS", [128, 4096], F32))
        self.n_ops = 0
        self.n_waits = 0

    def alloc(self, name, shape, dtype, align=64, side="low", fixed_off=None):
        assert name not in self.bufs, name
        itemsize = 2 if dtype == BF16 else 4
        per_part = int(np.prod(shape[1:])) * itemsize
        per_part = -(-per_part // align) * align
        cands = []
        for i, (a, b) in enumerate(self.free):
            if fixed_off is not None:
                if a <= fixed_off and fixed_off + per_part <= b:
                    cands.append((0, i, fixed_off))
            elif side == "high":
                aa = ((b - per_part) // align) * align
                if aa >= a:
                    cands.append((-b, i, aa))
            else:
                aa = -(-a // align) * align
                if b - aa >= per_part:
                    key = -(b - aa) if per_part >= 24576 else (b - aa)
                    cands.append((key, i, aa))
        cands.sort()
        for _, i, aa in cands[:1]:
            a, b = self.free[i]
            if True:
                del self.free[i]
                if aa > a:
                    self.free.insert(i, (a, aa)); i += 1
                if aa + per_part < b:
                    self.free.insert(i, (aa + per_part, b))
                ghosts = []
                new_ghost = []
                for (s0, e0, kl) in self.ghost:
                    if s0 < aa + per_part and e0 > aa:
                        ghosts.extend(kl)
                        if s0 < aa:
                            new_ghost.append((s0, aa, kl))
                        if e0 > aa + per_part:
                            new_ghost.append((aa + per_part, e0, kl))
                    else:
                        new_ghost.append((s0, e0, kl))
                self.ghost = new_ghost
                t = self.nc.alloc_sbuf_tensor_at(name, list(shape), dtype, offset=aa)
                b_ = Buf(name, t, aa, per_part, ghosts)
                self.bufs[name] = b_
                self.high_water = max(self.high_water, self.used())
                return b_
        raise MemoryError(f"SBUF alloc fail {name} ({per_part}B); free={self.free}")

    def used(self):
        return (SBUF_HI - SBUF_LO) - sum(b - a for a, b in self.free)

    def freebuf(self, buf):
        if isinstance(buf, str):
            buf = self.bufs[buf]
        del self.bufs[buf.name]
        pref = buf.name + "#"
        keys = [k for k in (set(self.last_w) | set(self.readers))
                if k == buf.name or k.startswith(pref)]
        self.ghost.append((buf.off, buf.off + buf.nbytes, keys))
        self.free.append((buf.off, buf.off + buf.nbytes))
        self.free.sort()
        merged = []
        for a, b in self.free:
            if merged and a <= merged[-1][1]:
                merged[-1] = (merged[-1][0], max(merged[-1][1], b))
            else:
                merged.append((a, b))
        self.free = merged

    def _wait(self, eng, pts):
        need = {}
        ec = self.eng_clock[eng]
        for ch, v in pts:
            if ec.get(ch, 0) < v:
                need[ch] = max(need.get(ch, 0), v)
        for ch, v in sorted(need.items()):
            self.ops[eng].append(("wait", ch, v))
            self.n_waits += 1
            ec[ch] = max(ec.get(ch, 0), v)
            for c2, v2 in self.clockat.get((ch, v), {}).items():
                ec[c2] = max(ec.get(c2, 0), v2)

    def emit(self, eng, fn, reads=(), writes=(), inc=True, ch=None, wover=()):
        ch = ch or eng
        slot_gate = None
        if ch in DMA_CH:
            q = ch
            ch = f"{q}_{self.dma_rr[q] % DMA_SLOTS[q]}"
            self.dma_rr[q] += 1
            if self.cnt[ch] > 0:
                slot_gate = (ch, self.cnt[ch])
        step = 16 if ch.startswith("q") else 1
        reads = [r for r in reads if r]
        writes = [w for w in writes if w]
        wover = [w for w in wover if w]
        ghost_extra = []
        for w in writes + wover:
            b = self.bufs.get(w.split("#")[0])
            if b and b.ghosts:
                ghost_extra.extend(g for g in b.ghosts
                                   if g in self.last_w or g in self.readers)
        pts = []
        for kk in reads:
            w = self.last_w.get(kk)
            if w:
                pts.append(w)
        for kk in writes + ghost_extra:
            w = self.last_w.get(kk)
            if w:
                pts.append(w)
            pts.extend(self.readers.get(kk, ()))
        if slot_gate is not None:
            pts.append(slot_gate)
        self._wait(eng, pts)
        pub = self.cnt[ch] + step
        if inc:
            self.cnt[ch] = pub
            self.clockat[(ch, pub)] = dict(self.eng_clock[eng])
        self.ops[eng].append(("op", fn, self.sem[ch] if inc else None, step))
        for kk in writes + wover:
            self.last_w[kk] = (ch, pub)
            self.readers[kk] = []
        for kk in reads:
            self.readers.setdefault(kk, []).append((ch, pub))
        self.n_ops += 1

    def hch(self):
        self.hflip = (getattr(self, "hflip", 0) + 1) % 3
        return (("sp", "qsp"), ("act", "qact"),
                ("pool", "qpool"))[self.hflip]

    def wait_all(self, eng):
        self._wait(eng, [(ch, self.cnt[ch]) for ch in ALL_CH if self.cnt[ch] > 0])

    def finalize(self):
        with self.nc.Block() as block:
            attr = {"pe": "tensor", "dve": "vector", "act": "scalar",
                    "pool": "gpsimd", "sp": "sync"}
            for eng in ENGINES:
                ops = self.ops[eng]
                sems = self.sem
                def body(e, ops=ops, sems=sems):
                    for item in ops:
                        if item[0] == "wait":
                            e.wait_ge(sems[item[1]], item[2])
                        else:
                            _, fn, sem_, step = item
                            ins = fn(e)
                            if sem_ is not None:
                                ins.then_inc(sem_, step)
                getattr(block, attr[eng])(body)


class MK:
    def __init__(self, nc, io):
        self.nc = nc
        self.io = io
        self.p = Prog(nc)
        self.consts_np = host_constants()
        self.cbuf = {}
        self._uid = 0
        self.evac_flip = 0
        p = self.p
        self.eps = {}
        self.X = p.alloc("X", [128, NT * D], F32, fixed_off=X_FIXED)
        self.spill = {}
        self.stat = p.alloc("stat", [128, 32], F32, side="high")
        self.bb = p.alloc("bb", [128, 24], F32, side="high")
        self.brow = p.alloc("brow", [1, 80], F32, side="high")
        self.biasb = p.alloc("biasb", [128, 4], F32, side="high")
        self.biasball = p.alloc("biasball", [128, 128], F32, side="high")
        self.rA = p.alloc("ropeA", [128, S], F32, side="high")
        self.rB = p.alloc("ropeB", [128, S], F32, side="high")
        self.rC = p.alloc("ropeC", [128, S], F32, side="high")
        for nm in ("identf", "ones_1x128", "neg_ones_1x128", "padcol", "padcol64"):
            self.const(nm)
        self.epsbuf(1e-3)
        self.epsbuf(1e-5)
        for nm in ("identb", "blockdiag2", "band", "band_prev"):
            self.const(nm, BF16)
        self.rzb = p.alloc("rzb", [128, 1024], F32, side="high")

    def uid(self):
        self._uid += 1
        return self._uid

    def const(self, name, dtype=F32, data=None):
        key = (name, str(dtype))
        if key in self.cbuf:
            return self.cbuf[key]
        arr = self.consts_np[name] if data is None else data
        arr = arr.astype(ml_dtypes.bfloat16 if dtype == BF16 else np.float32)
        h = self.nc.inline_tensor(np.ascontiguousarray(arr), name=f"c{self.uid()}_{name}")
        b = self.p.alloc(f"C{self.uid()}_{name}", list(arr.shape), dtype, side="high")
        self.p.emit("sp", lambda e, b=b, h=h: e.dma_start(out=b.ap, in_=h.ap()),
                    writes=[b.name], ch="qsp")
        self.cbuf[key] = b
        return b

    def free_const(self, name, dtype=F32):
        key = (name, str(dtype))
        if key in self.cbuf:
            self.p.freebuf(self.cbuf.pop(key))

    def epsbuf(self, val):
        if val in self.eps:
            return self.eps[val]
        b = self.p.alloc(f"eps{self.uid()}", [128, 1], F32, side="high")
        self.p.emit("dve", lambda e, b=b, v=val: e.memset(b.ap, v), writes=[b.name])
        self.eps[val] = b
        return b

    def evac(self, dst_ap, src_ap, reads, writes, scale=None, eng=None):
        if eng is None:
            eng = ("dve", "act")[self.evac_flip % 2]
            self.evac_flip += 1
        if eng == "dve":
            if scale is None:
                self.p.emit("dve", lambda e: e.tensor_copy(dst_ap, src_ap),
                            reads=reads, writes=writes)
            else:
                self.p.emit("dve", lambda e: e.tensor_scalar_mul(dst_ap, src_ap, scale),
                            reads=reads, writes=writes)
        else:
            if scale is None:
                self.p.emit("act", lambda e: e.copy(dst_ap, src_ap),
                            reads=reads, writes=writes)
            else:
                self.p.emit("act", lambda e: e.mul(dst_ap, src_ap, scale),
                            reads=reads, writes=writes)

    def mm_group(self, bank_key, members):
        n = len(members)
        for i, (fn, reads) in enumerate(members):
            first, last = i == 0, i == n - 1
            self.p.emit("pe", fn, reads=reads,
                        writes=[bank_key] if first else (),
                        wover=[bank_key] if (last and not first) else (),
                        inc=last)

    def transpose128(self, bank_key, out_ap, in_ap, reads, ident, wover=False):
        self.p.emit("pe", lambda e: e.transpose(out_ap, in_ap, ident.ap),
                    reads=list(reads) + [ident.name],
                    writes=() if wover else [bank_key],
                    wover=[bank_key] if wover else (), inc=True)

    def full_transpose_cast(self, src_ap_fn, src_key_fn, dst, dst_key_fn, scale=None):
        identf = self.const("identf")
        p = self.p
        g = 0
        for t_ in range(NT):
            for ic0 in range(0, NC_, 4):
                bank = g % 4
                g += 1
                bk = f"ps{bank}"
                for j in range(4):
                    ic = ic0 + j
                    ps = p.PS[:, bank * 512 + j * 128: bank * 512 + (j + 1) * 128]
                    self.transpose128(bk, ps, src_ap_fn(ic, t_),
                                      [src_key_fn(ic, t_)], identf,
                                      wover=(j > 0))
                srcap = bass.AP(p.PS, bank * 512, [[4096, 128], [1, 512]])
                dstap = bass.AP(dst.t, ic0 * S + t_ * 128,
                                [[NC_ * S, 128], [S, 4], [1, 128]])
                self.evac(dstap, srcap, reads=[bk],
                          writes=[dst_key_fn(ic0 + j, t_) for j in range(4)],
                          scale=scale)

    def load_weight_T(self, wname, dram, scale=None):
        p = self.p
        O, I = dram.shape[0], dram.shape[1]
        noc, nic = O // 128, I // 128
        wt = p.alloc(wname, [128, nic * O], BF16)
        stage = p.alloc(wname + "_st", [128, 2 * I], F32)
        identf = self.const("identf")
        for oc in range(noc):
            par = oc % 2
            skey = f"{wname}_st#{par}"
            sta = stage.t[:, par * I:(par + 1) * I]
            eng, ch = p.hch()
            p.emit(eng, lambda e, oc=oc, sta=sta, dram=dram:
                   e.dma_start(out=sta, in_=dram[oc * 128:(oc + 1) * 128, :]),
                   reads=[skey], writes=[skey], ch=ch)
            for ic in range(nic):
                bank = (oc * nic + ic) % 4
                bk = f"ps{bank}"
                ps = p.PS[:, bank * 512: bank * 512 + 128]
                self.transpose128(bk, ps, sta[:, ic * 128:(ic + 1) * 128],
                                  [skey], identf)
                dcol = ic * O + oc * 128
                self.evac(wt.t[:, dcol:dcol + 128], ps,
                          reads=[bk], writes=[f"{wname}#{ic}"], scale=scale)
        p.freebuf(stage)
        return wt

    def load_weight_N(self, wname, dram, scale=None, O=None):
        p = self.p
        if O is None:
            I, O = dram.shape[0], dram.shape[1]
        else:
            I = dram.shape[0]
        nic = I // 128
        wt = p.alloc(wname, [128, nic * O], BF16)
        stage = p.alloc(wname + "_st", [128, 2 * O], F32)
        for ic in range(nic):
            par = ic % 2
            skey = f"{wname}_st#{par}"
            sta = stage.t[:, par * O:(par + 1) * O]
            eng, ch = p.hch()
            p.emit(eng, lambda e, ic=ic, sta=sta, dram=dram:
                   e.dma_start(out=sta, in_=dram[ic * 128:(ic + 1) * 128, :]),
                   reads=[skey], writes=[skey], ch=ch)
            self.evac(wt.t[:, ic * O:(ic + 1) * O], sta,
                      reads=[skey], writes=[f"{wname}#{ic}"], scale=scale)
        p.freebuf(stage)
        return wt

    def spill_x(self, tag, free=True):
        sp = self.nc.dram_tensor(f"spill_{tag}", [S, D], F32, kind="Internal").ap()
        self.spill[tag] = sp
        X = self.X
        for i in range(NT):
            self.p.emit("pool", lambda e, i=i, sp=sp, X=X:
                        e.dma_start(out=sp[i * 128:(i + 1) * 128, :],
                                    in_=X[:, i * D:(i + 1) * D]),
                        reads=[f"X#{i}"], writes=[f"sp_{tag}#{i}"], ch="qpool")
        if free:
            self.p.freebuf(self.X)
            self.X = None

    def realloc_x(self):
        assert self.X is None
        try:
            self.X = self.p.alloc("X", [128, NT * D], F32, fixed_off=X_FIXED)
        except MemoryError:
            self.X = self.p.alloc("X", [128, NT * D], F32)

    def unspill_x_tile(self, tag, i):
        sp = self.spill[tag]
        X = self.X
        eng, ch = self.p.hch()
        self.p.emit(eng, lambda e, i=i, sp=sp, X=X:
                    e.dma_start(out=X[:, i * D:(i + 1) * D],
                                in_=sp[i * 128:(i + 1) * 128, :]),
                    reads=[f"sp_{tag}#{i}"], writes=[f"X#{i}"], ch=ch)

    def ln_tile_stages(self, i, eps):
        p = self.p
        st = self.stat
        b0 = (i % 2) * 16
        sk = f"stat#{i % 2}"
        epsb = self.epsbuf(eps)
        xap = self.X[:, i * D:(i + 1) * D]
        xk = f"X#{i}"

        def fa():
            for ch_ in range(2):
                a = self.X[:, i * D + ch_ * 512: i * D + (ch_ + 1) * 512]
                p.emit("dve", lambda e, a=a, ch_=ch_, b0=b0: e.bn_stats(
                    st[:, b0 + ch_ * 6: b0 + (ch_ + 1) * 6], a),
                    reads=[xk], writes=[sk])
            p.emit("dve", lambda e, b0=b0: e.bn_aggr(st[:, b0 + 12:b0 + 14],
                                                     st[:, b0:b0 + 12]),
                   reads=[sk], writes=[sk])
            p.emit("act", lambda e, b0=b0: e.activation(
                st[:, b0 + 14:b0 + 15], st[:, b0 + 13:b0 + 14], AF.Sqrt,
                bias=epsb.ap), reads=[sk, epsb.name], writes=[sk])

        def fb():
            p.emit("dve", lambda e, b0=b0: e.reciprocal(st[:, b0 + 15:b0 + 16],
                                                        st[:, b0 + 14:b0 + 15]),
                   reads=[sk], writes=[sk])
            p.emit("dve", lambda e, b0=b0: e.scalar_tensor_tensor(
                st[:, b0 + 12:b0 + 13], st[:, b0 + 12:b0 + 13], -1.0,
                st[:, b0 + 15:b0 + 16], ALU.mult, ALU.mult),
                reads=[sk], writes=[sk])
            p.emit("act", lambda e, b0=b0: e.activation(
                xap, xap, AF.Identity, bias=st[:, b0 + 12:b0 + 13],
                scale=st[:, b0 + 15:b0 + 16]),
                reads=[xk, sk], writes=[xk])
        return fa, fb

    def ln_tile(self, i, eps, gamma=None, beta=None):
        p = self.p
        st = self.stat
        b0 = (i % 2) * 16
        sk = f"stat#{i % 2}"
        epsb = self.epsbuf(eps)
        xap = self.X[:, i * D:(i + 1) * D]
        xk = f"X#{i}"
        for ch_ in range(2):
            a = self.X[:, i * D + ch_ * 512: i * D + (ch_ + 1) * 512]
            p.emit("dve", lambda e, a=a, ch_=ch_, b0=b0: e.bn_stats(
                st[:, b0 + ch_ * 6: b0 + (ch_ + 1) * 6], a),
                reads=[xk], writes=[sk])
        p.emit("dve", lambda e, b0=b0: e.bn_aggr(st[:, b0 + 12:b0 + 14],
                                                 st[:, b0:b0 + 12]),
               reads=[sk], writes=[sk])
        p.emit("act", lambda e, b0=b0: e.activation(
            st[:, b0 + 14:b0 + 15], st[:, b0 + 13:b0 + 14], AF.Sqrt,
            bias=epsb.ap), reads=[sk, epsb.name], writes=[sk])
        p.emit("dve", lambda e, b0=b0: e.reciprocal(st[:, b0 + 15:b0 + 16],
                                                    st[:, b0 + 14:b0 + 15]),
               reads=[sk], writes=[sk])
        p.emit("dve", lambda e, b0=b0: e.scalar_tensor_tensor(
            st[:, b0 + 12:b0 + 13], st[:, b0 + 12:b0 + 13], -1.0,
            st[:, b0 + 15:b0 + 16], ALU.mult, ALU.mult),
            reads=[sk], writes=[sk])
        p.emit("act", lambda e, b0=b0: e.activation(
            xap, xap, AF.Identity, bias=st[:, b0 + 12:b0 + 13],
            scale=st[:, b0 + 15:b0 + 16]),
            reads=[xk, sk], writes=[xk])
        if gamma is not None:
            p.emit("dve", lambda e: e.tensor_mul(xap, xap, gamma.ap),
                   reads=[xk, gamma.name], writes=[xk])
            p.emit("dve", lambda e: e.tensor_add(xap, xap, beta.ap),
                   reads=[xk, beta.name], writes=[xk])

    def bcast_scalar(self, src_ap, src_key, dst_ap, dst_key, neg=False):
        p = self.p
        ones = self.const("neg_ones_1x128" if neg else "ones_1x128", F32)
        self.mm_group("ps7", [(lambda e: e.matmul(
            p.PS[:, 7 * 512:7 * 512 + 1], ones.ap, src_ap,
            start=True, stop=True), [ones.name, src_key])])
        p.emit("dve", lambda e: e.tensor_copy(dst_ap, p.PS[:, 7 * 512:7 * 512 + 1]),
               reads=["ps7"], writes=[dst_key])

    def part_reduce(self, src_ap, src_key, dst_ap, dst_key, op, n=128):
        p = self.p
        identf = self.const("identf")
        self.transpose128("ps7", p.PS[0:1, 7 * 512:7 * 512 + n], src_ap,
                          [src_key], identf)
        p.emit("dve", lambda e: e.tensor_reduce(
            dst_ap, p.PS[0:1, 7 * 512:7 * 512 + n], axis=AX.X, op=op),
            reads=["ps7"], writes=[dst_key])

    def write_out(self):
        out = self.io["out"]
        X = self.X
        for i in range(NT):
            self.p.emit("pool", lambda e, i=i, X=X: e.dma_start(
                out=out[i * 128:(i + 1) * 128, :],
                in_=X[:, i * D:(i + 1) * D]),
                reads=[f"X#{i}"], ch="qpool")
        self.p.wait_all("pool")

    def finalize(self):
        self.p.finalize()


def head_stats(mk, TN, tname, col0, nchunks=NC_, span=S):
    p = mk.p
    bdb = mk.const("blockdiag2", BF16)
    sq = p.alloc("sqst", [128, span], BF16)
    tail = span - min(span, 2048)
    for oc in range(nchunks):
        p.emit("act", lambda e, oc=oc: e.activation(
            sq.ap, TN.t[:, oc * span:(oc + 1) * span], AF.Square),
            reads=[f"{tname}#{oc}"], writes=["sqst"])
        nb = min(span, 2048) // 512
        for tb in range(nb):
            bank = 4 + tb
            mk.mm_group(f"ps{bank}", [(lambda e, tb=tb, bank=bank: e.matmul(
                p.PS[:, bank * 512:(bank + 1) * 512], bdb.ap,
                sq[:, tb * 512:(tb + 1) * 512], start=True, stop=True),
                [bdb.name, "sqst"])])
        p.emit("dve", lambda e, nb=nb, oc=oc: e.tensor_reduce(
            mk.bb[0:2, col0 + oc:col0 + oc + 1], p.PS[0:2, 4 * 512:(4 + nb) * 512],
            axis=AX.X, op=ALU.max),
            reads=[f"ps{4 + tb}" for tb in range(nb)], writes=["bb"])
        if tail > 0:
            mk.mm_group("ps4", [(lambda e: e.matmul(
                p.PS[:, 4 * 512:4 * 512 + tail], bdb.ap,
                sq[:, 2048:2048 + tail], start=True, stop=True),
                [bdb.name, "sqst"])])
            p.emit("dve", lambda e: e.tensor_reduce(
                mk.bb[0:2, 22:23], p.PS[0:2, 4 * 512:4 * 512 + tail],
                axis=AX.X, op=ALU.max), reads=["ps4"], writes=["bb"])
            p.emit("dve", lambda e, oc=oc: e.tensor_max(
                mk.bb[0:2, col0 + oc:col0 + oc + 1],
                mk.bb[0:2, col0 + oc:col0 + oc + 1], mk.bb[0:2, 22:23]),
                reads=["bb"], writes=["bb"])
    p.freebuf(sq)


def finish_bounds(mk, nheads):
    p = mk.p
    p.emit("sp", lambda e: e.dma_start(out=mk.brow[0:1, 0:32], in_=mk.bb[0:2, 0:16]),
           reads=["bb"], writes=["brow"], ch="qsp")
    qap = bass.AP(mk.brow.t, 0, [[80, 1], [16, 2], [1, 8]])
    kap = bass.AP(mk.brow.t, 8, [[80, 1], [16, 2], [1, 8]])
    oap = bass.AP(mk.brow.t, 32, [[80, 1], [8, 2], [1, 8]])
    p.emit("dve", lambda e: e.tensor_tensor(oap, qap, kap, ALU.mult),
           reads=["brow"], writes=["brow"])
    p.emit("act", lambda e: e.activation(mk.brow[0:1, 48:64], mk.brow[0:1, 32:48],
                                         AF.Sqrt), reads=["brow"], writes=["brow"])


def attn_loop(mk, QT, KT, vp_fn, OnT, H, span, kt_lists, null_pad_tile=None,
              kspan=None, names=("QT", "KT", "Vp", "OnT"), bias_col=None):
    p = mk.p
    PTb = mk.PTb
    qtn, ktn, vpn, ontn = names
    identb = mk.const("identb", BF16)
    onesb = mk.const("ones128", BF16)
    negonesf = mk.const("neg_ones_1x128", F32)
    nqb = len(kt_lists)
    if bias_col is None:
        bias_col = lambda h, qb: 48 + (h % 2) * 8 + h // 2
    mk.mm_group("ps0", [(lambda e: e.matmul(
        p.PS[:, 0:64], negonesf.ap, mk.brow[0:1, 0:64],
        start=True, stop=True), [negonesf.name, "brow"])])
    p.emit("dve", lambda e: e.tensor_copy(mk.biasball[:, 0:64], p.PS[:, 0:64]),
           reads=["ps0"], writes=["biasball"])
    if null_pad_tile is not None:
        padc = mk.const("padcol64")
        p.emit("dve", lambda e: e.tensor_tensor(
            mk.biasball[:, 64:128], mk.biasball[:, 0:64], padc.ap, ALU.add),
            reads=["biasball", padc.name], writes=["biasball"])

    bc = [0]
    gpt = [0]
    gz = [0]
    pending_z = []
    ksp = kspan or span

    def emit_S_pair(h0, h1, qb, kt, bandname, off, b0_, b1_):
        has_mask = bandname is not None
        for h, b in ((h0, b0_), (h1, b1_)):
            half, oc = h % 2, h // 2
            lhs = KT.t[half * 64:half * 64 + 64,
                       oc * ksp + kt * 128: oc * ksp + kt * 128 + 128]
            rhs = QT.t[half * 64:half * 64 + 64,
                       oc * S + qb * 512: oc * S + qb * 512 + 512]
            p.emit("pe", lambda e, lhs=lhs, rhs=rhs, b=b, hm=has_mask:
                   e.matmul(p.PS[:, b * 512:(b + 1) * 512], lhs, rhs,
                            start=True, stop=not hm),
                   reads=[f"{ktn}#{oc}", f"{qtn}#{oc}"],
                   writes=[f"ps{b}"], inc=not has_mask)
        if has_mask:
            bnd = mk.const(bandname, BF16)
            if bandname == "band_prev":
                c_lo, c_hi = off, 512
                msl = bnd.t[:, 512: 512 + (512 - off)]
            else:
                c_lo, c_hi = 0, min(512, off + 128)
                msl = bnd.t[:, 512 - off: 512 - off + (c_hi - c_lo)]
            for b in (b0_, b1_):
                p.emit("pe", lambda e, msl=msl, b=b, c_lo=c_lo, c_hi=c_hi:
                       e.matmul(p.PS[:, b * 512 + c_lo: b * 512 + c_hi],
                                identb.ap, msl, start=False, stop=True),
                       reads=[identb.name, bnd.name],
                       wover=[f"ps{b}"], inc=True)

    def emit_exp1(h, kt, qb, b):
        c0 = bias_col(h, qb)
        pslot = gpt[0] % 6
        gpt[0] += 1
        ptk = f"PTb#{pslot}"
        pad = (null_pad_tile is not None and kt == null_pad_tile)
        bsl = (mk.biasball[:, 64 + c0:64 + c0 + 1] if pad
               else mk.biasball[:, c0:c0 + 1])
        p.emit("act", lambda e, bsl=bsl, b=b, pslot=pslot:
               e.activation(PTb.t[:, pslot * 512:(pslot + 1) * 512],
                            p.PS[:, b * 512:(b + 1) * 512],
                            AF.Exp, bias=bsl),
               reads=[f"ps{b}", "biasball"], writes=[ptk])
        return pslot

    def emit_pv1(h, kt, ob, pslot, st_, last):
        ptk = f"PTb#{pslot}"
        vb, vcol0, vkey = vp_fn(kt)
        lhs = vb.t[:, vcol0 + h * 65: vcol0 + h * 65 + 65]
        rhs = PTb.t[:, pslot * 512:(pslot + 1) * 512]
        p.emit("pe", lambda e, lhs=lhs, rhs=rhs, st_=st_, last=last, ob=ob:
               e.matmul(p.PS[0:65, ob * 512:ob * 512 + 512], lhs, rhs,
                        start=st_, stop=last),
               reads=[vkey, ptk],
               writes=[f"ps{ob}"] if st_ else (),
               wover=[f"ps{ob}"] if (last and not st_) else (),
               inc=last)

    zstage = mk.nc.dram_tensor(f"zstage_{mk.uid()}", [2, 512], F32,
                               kind="Internal").ap()

    def make_z2(dsts, ocks, obs):
        for hx in range(2):
            ob = obs[hx]
            stg = mk.rzb.t[64:65, hx * 512:(hx + 1) * 512]
            p.emit("dve", lambda e, ob=ob, stg=stg: e.reciprocal(
                stg, p.PS[64:65, ob * 512:(ob + 1) * 512]),
                reads=[f"ps{ob}"], writes=[f"zrow#{hx}"])
            p.emit("sp", lambda e, hx=hx, stg=stg: e.dma_start(
                out=zstage[hx:hx + 1, :], in_=stg),
                reads=[f"zrow#{hx}"], writes=[f"zst#{hx}"], ch="qsp")
            srcap = bass.AP(zstage.tensor, hx * 512, [[0, 64], [1, 512]])
            dstap = bass.AP(mk.rzb.t, hx * 512, [[1024, 64], [1, 512]])
            p.emit("sp", lambda e, srcap=srcap, dstap=dstap: e.dma_start(
                out=dstap, in_=srcap),
                reads=[f"zst#{hx}"], writes=[f"rzb#{hx}"], ch="qsp")

        def go():
            for hx in range(2):
                ob, dst, ock = obs[hx], dsts[hx], ocks[hx]
                rza = mk.rzb.t[0:64, hx * 512:(hx + 1) * 512]
                p.emit("dve", lambda e, rza=rza, dst=dst, ob=ob: e.tensor_tensor(
                    dst, p.PS[0:64, ob * 512:(ob + 1) * 512], rza, ALU.mult),
                    reads=[f"ps{ob}", f"rzb#{hx}"], writes=[ock])
        return go

    items = [(hp, qb) for hp in range(H // 2) for qb in range(nqb)]
    for item_i, (hp, qb) in enumerate(items):
        h0, h1 = 2 * hp, 2 * hp + 1
        oc = hp
        obs = (4, 5) if item_i % 2 == 0 else (6, 7)
        kt_list = kt_lists[qb]
        n = len(kt_list)
        sbank = []
        pslots = []

        def do_exp(ki):
            pkt = kt_list[ki][0]
            pslots.append((emit_exp1(h0, pkt, qb, sbank[ki][0]),
                           emit_exp1(h1, pkt, qb, sbank[ki][1])))

        def do_pv(ki):
            pkt = kt_list[ki][0]
            for hx, (h, ob) in enumerate(((h0, obs[0]), (h1, obs[1]))):
                emit_pv1(h, pkt, ob, pslots[ki][hx],
                         st_=(ki == 0), last=(ki == n - 1))

        assert n >= 3
        for ki, (kt, bandname, off) in enumerate(kt_list):
            if ki >= 3:
                do_pv(ki - 3)
            if ki >= 1:
                do_exp(ki - 1)
            b0_ = bc[0] % 4
            b1_ = (bc[0] + 1) % 4
            bc[0] += 2
            sbank.append((b0_, b1_))
            emit_S_pair(h0, h1, qb, kt, bandname, off, b0_, b1_)
            if ki == 2 and pending_z:
                for z in pending_z:
                    z()
                pending_z.clear()
        do_exp(n - 1)
        for ki in range(max(0, n - 3), n):
            do_pv(ki)
        dsts, ocks = [], []
        for h in (h0, h1):
            half = h % 2
            dsts.append(OnT.t[half * 64:half * 64 + 64,
                              oc * S + qb * 512: oc * S + qb * 512 + 512])
            ocks.append(f"{ontn}#{oc}")
        pending_z.append(make_z2(dsts, ocks, obs))
    for z in pending_z:
        z()


def out_proj_residual(mk, OnT, WoT, ont_name, wot_name, spill_tag, post_eps,
                      gamma=None, beta=None, extra_spill=None, bias_tab=None,
                      nchunks=NC_, scale_ap=None, scale_key=None, chunk_k=None,
                      tiles=None, ont_span=S, dst_spill=None):
    p = mk.p
    X = mk.X
    ln_pend = [None]
    if dst_spill is not None and dst_spill not in mk.spill:
        mk.spill[dst_spill] = mk.nc.dram_tensor(
            f"spill_{dst_spill}", [S, D], F32, kind="Internal").ap()
    if dst_spill is None and spill_tag is not None:
        for i in (tiles if tiles is not None else range(NT)):
            mk.unspill_x_tile(spill_tag, i)
    for i in (tiles if tiles is not None else range(NT)):
        if dst_spill is not None:
            xb = mk.xtra2 if i % 2 == 0 else mk.xtra2_b
            xbk = "xtra2" if i % 2 == 0 else "xtra2_b"
            sp0 = mk.spill[spill_tag]
            eng, ch = p.hch()
            p.emit(eng, lambda e, i=i, sp0=sp0, xb=xb: e.dma_start(
                out=xb.ap, in_=sp0[i * 128:(i + 1) * 128, :]),
                reads=[f"sp_{spill_tag}#{i}"], writes=[xbk], ch=ch)
        if extra_spill is not None:
            sp = mk.spill[extra_spill]
            p.emit("sp", lambda e, i=i, sp=sp: e.dma_start(
                out=mk.xtra.ap, in_=sp[i * 128:(i + 1) * 128, :]),
                reads=[f"sp_{extra_spill}#{i}"], writes=["xtra"], ch="qsp")
        tpar = (i % 2) * 2
        for oh in range(2):
            bank = 4 + tpar + oh
            members = []
            for ic in range(nchunks):
                kk = 128 if chunk_k is None else chunk_k(ic)
                i_loc = i - (tiles.start if tiles is not None else 0)
                lhs = OnT.t[0:kk, ic * ont_span + i_loc * 128:
                            ic * ont_span + i_loc * 128 + 128]
                rhs = WoT.t[0:kk, ic * D + oh * 512: ic * D + oh * 512 + 512]
                members.append((lambda e, lhs=lhs, rhs=rhs, bank=bank, ic=ic,
                                nchunks=nchunks:
                                e.matmul(p.PS[:, bank * 512:(bank + 1) * 512],
                                         lhs, rhs, start=(ic == 0),
                                         stop=(ic == nchunks - 1)),
                                [f"{ont_name}#{ic}", f"{wot_name}#{ic}"]))
            mk.mm_group(f"ps{bank}", members)
        for oh in range(2):
            bank = 4 + tpar + oh
            if dst_spill is not None:
                xap = xb[:, oh * 512:(oh + 1) * 512]
                xkey = xbk
            else:
                xap = X[:, i * D + oh * 512: i * D + (oh + 1) * 512]
                xkey = f"X#{i}"
            if scale_ap is None:
                p.emit("dve", lambda e, xap=xap, bank=bank: e.scalar_tensor_tensor(
                    xap, p.PS[:, bank * 512:(bank + 1) * 512], 1.0, xap,
                    ALU.mult, ALU.add),
                    reads=[f"ps{bank}", xkey], writes=[xkey])
            else:
                p.emit("dve", lambda e, xap=xap, bank=bank: e.scalar_tensor_tensor(
                    xap, p.PS[:, bank * 512:(bank + 1) * 512], scale_ap, xap,
                    ALU.mult, ALU.add),
                    reads=[f"ps{bank}", xkey, scale_key], writes=[xkey])
            if bias_tab is not None:
                p.emit("dve", lambda e, xap=xap, oh=oh: e.tensor_add(
                    xap, xap, bias_tab.t[:, oh * 512:(oh + 1) * 512]),
                    reads=[xkey, bias_tab.name], writes=[xkey])
            if extra_spill is not None:
                p.emit("dve", lambda e, xap=xap, oh=oh: e.tensor_add(
                    xap, xap, mk.xtra[:, oh * 512:(oh + 1) * 512]),
                    reads=[xkey, "xtra"], writes=[xkey])
        if dst_spill is not None:
            spd = mk.spill[dst_spill]
            p.emit("pool", lambda e, i=i, spd=spd, xb=xb: e.dma_start(
                out=spd[i * 128:(i + 1) * 128, :], in_=xb.ap),
                reads=[xbk], writes=[f"sp_{dst_spill}#{i}"], ch="qpool")
        if post_eps is not None:
            assert gamma is None
            fa, fb = mk.ln_tile_stages(i, post_eps)
            fa()
            if ln_pend[0] is not None:
                ln_pend[0]()
            ln_pend[0] = fb
    if ln_pend[0] is not None:
        ln_pend[0]()


def block_l0_attn(mk):
    p = mk.p
    io = mk.io
    H = 16
    mk.const("l0_cos"); mk.const("l0_sinsw")
    WqT = mk.load_weight_T("WqT", io["L0_attn_Wq"], scale=0.125)
    XT = p.alloc("XT", [128, NC_ * S], BF16)
    mk.full_transpose_cast(
        lambda ic, t: mk.X[:, t * D + ic * 128: t * D + ic * 128 + 128],
        lambda ic, t: f"X#{t}", XT, lambda ic, t: f"XT#{ic}")
    mk.spill_x("l0a")

    cosb = mk.const("l0_cos")
    sinswb = mk.const("l0_sinsw")
    QT = p.alloc("QT", [128, NC_ * S], BF16)
    KT = p.alloc("KT", [128, NC_ * S], BF16)
    rA, rB, rC = mk.rA, mk.rB, mk.rC

    def project_chunk(wt, wname, oc, b0=0):
        for tb in range(4):
            members = []
            for ic in range(NC_):
                lhs = wt.t[:, ic * D + oc * 128: ic * D + oc * 128 + 128]
                rhs = XT.t[:, ic * S + tb * 512: ic * S + tb * 512 + 512]
                members.append((lambda e, lhs=lhs, rhs=rhs, b=b0 + tb, ic=ic:
                                e.matmul(p.PS[:, b * 512:(b + 1) * 512], lhs, rhs,
                                         start=(ic == 0), stop=(ic == NC_ - 1)),
                                [f"{wname}#{ic}", f"XT#{ic}"]))
            mk.mm_group(f"ps{b0 + tb}", members)

    def rope_evac(dst, dname, oc, cos, sinsw, swap, b0=0):
        pall = p.PS[:, b0 * 512: b0 * 512 + 2048]
        pk = [f"ps{b0 + j}" for j in range(4)]
        p.emit("dve", lambda e, pall=pall: e.tensor_mul(rA.ap, pall, cos.ap),
               reads=pk + [cos.name], writes=["ropeA"])
        p.emit("dve", lambda e, pall=pall: e.tensor_mul(rB.ap, pall, sinsw.ap),
               reads=pk + [sinsw.name], writes=["ropeB"])
        if swap == "pair":
            mask = [(i + 1 if i % 2 == 0 else i - 1) for i in range(32)]
        else:
            mask = [(i + 16) % 32 for i in range(32)]
        p.emit("dve", lambda e, mask=mask: e.stream_shuffle(rC.ap, rB.ap, mask),
               reads=["ropeB"], writes=["ropeC"])
        dcol = oc * S
        p.emit("dve", lambda e: e.tensor_add(dst.t[:, dcol:dcol + S], rA.ap, rC.ap),
               reads=["ropeA", "ropeC"], writes=[f"{dname}#{oc}"])

    for oc in range(NC_):
        project_chunk(WqT, "WqT", oc, (oc % 2) * 4)
        if oc > 0:
            rope_evac(QT, "QT", oc - 1, cosb, sinswb, "pair", ((oc - 1) % 2) * 4)
    rope_evac(QT, "QT", NC_ - 1, cosb, sinswb, "pair", ((NC_ - 1) % 2) * 4)
    p.freebuf(WqT)
    WkT = mk.load_weight_T("WkT", io["L0_attn_Wk"])
    for oc in range(NC_):
        project_chunk(WkT, "WkT", oc, (oc % 2) * 4)
        if oc > 0:
            rope_evac(KT, "KT", oc - 1, cosb, sinswb, "pair", ((oc - 1) % 2) * 4)
    rope_evac(KT, "KT", NC_ - 1, cosb, sinswb, "pair", ((NC_ - 1) % 2) * 4)
    p.freebuf(WkT)
    mk.free_const("l0_cos"); mk.free_const("l0_sinsw")
    head_stats(mk, QT, "QT", 0)
    head_stats(mk, KT, "KT", 8)

    VW = H * 65
    VpA = p.alloc("VpA", [128, 8 * VW], BF16)
    VpB = p.alloc("VpB", [128, 8 * VW], BF16)

    def vp_fn(kt):
        if kt < 8:
            return VpA, kt * VW, f"VpA#{kt}"
        return VpB, (kt - 8) * VW, f"VpB#{kt - 8}"
    for vb, vn in ((VpA, "VpA"), (VpB, "VpB")):
        ones_ap = bass.AP(vb.t, 64, [[8 * VW, 128], [VW, 8], [65, H]])
        p.emit("dve", lambda e, ones_ap=ones_ap: e.memset(ones_ap, 1.0),
               writes=[f"{vn}#{kt}" for kt in range(8)])
    vchunk = p.alloc("vchunk", [128, S], F32)
    identf = mk.const("identf")
    WvT = mk.load_weight_T("WvT", io["L0_attn_Wv"])

    def v_evac(oc, b0):
        pall = p.PS[:, b0 * 512: b0 * 512 + 2048]
        pk = [f"ps{b0 + j}" for j in range(4)]
        p.emit("dve", lambda e, pall=pall: e.tensor_copy(vchunk.ap, pall),
               reads=pk, writes=["vchunk"])
        h0 = 2 * oc
        for g4 in range(4):
            bank = b0 + g4
            bk = f"ps{bank}"
            for j in range(4):
                kt = g4 * 4 + j
                ps = p.PS[:, bank * 512 + j * 128: bank * 512 + (j + 1) * 128]
                mk.transpose128(bk, ps, vchunk[:, kt * 128:(kt + 1) * 128],
                                ["vchunk"], identf, wover=(j > 0))
            vb, vcol0, _ = vp_fn(g4 * 4)
            srcap = bass.AP(p.PS, bank * 512,
                            [[4096, 128], [128, 4], [64, 2], [1, 64]])
            dstap = bass.AP(vb.t, vcol0 + h0 * 65,
                            [[8 * VW, 128], [VW, 4], [65, 2], [1, 64]])
            p.emit("dve", lambda e, srcap=srcap, dstap=dstap:
                   e.tensor_copy(dstap, srcap),
                   reads=[bk], writes=[vp_fn(g4 * 4 + j)[2] for j in range(4)])

    for oc in range(NC_):
        project_chunk(WvT, "WvT", oc, (oc % 2) * 4)
        if oc > 0:
            v_evac(oc - 1, ((oc - 1) % 2) * 4)
    v_evac(NC_ - 1, ((NC_ - 1) % 2) * 4)
    p.freebuf(WvT); p.freebuf(vchunk); p.freebuf(XT)
    finish_bounds(mk, H)

    OnT = p.alloc("OnT", [128, NC_ * S], BF16)
    mk.PTb = p.alloc("PTb", [128, 3072], BF16)
    kt_lists = [[(kt, ("band" if kt * 128 >= qb * 512 else None),
                  (kt * 128 - qb * 512 if kt * 128 >= qb * 512 else None))
                 for kt in range(4 * (qb + 1))] for qb in range(4)]
    attn_loop(mk, QT, KT, vp_fn, OnT, H, S, kt_lists)
    p.freebuf(QT); p.freebuf(KT); p.freebuf(VpA); p.freebuf(VpB)
    p.freebuf(mk.PTb)
    del mk.PTb

    WoT = mk.load_weight_T("WoT", io["L0_attn_Wo"])
    mk.realloc_x()
    out_proj_residual(mk, OnT, WoT, "OnT", "WoT", "l0a", post_eps=1e-5)
    p.freebuf(OnT); p.freebuf(WoT)


def block_l0_ffn(mk, n_cores=8):
    p = mk.p
    io = mk.io
    nc = mk.nc
    gtab = p.alloc("l0f_g", [128, D], F32)
    btab = p.alloc("l0f_b", [128, D], F32)
    for tab, nm in ((gtab, "L0_ffn_ln_gamma"), (btab, "L0_ffn_ln_beta")):
        srcap = bass.AP(io[nm].tensor, 0, [[0, 128], [1, D]])
        p.emit("sp", lambda e, tab=tab, srcap=srcap: e.dma_start(out=tab.ap, in_=srcap),
               writes=[tab.name], ch="qsp")
    xnT = p.alloc("xnT", [128, NC_ * S], BF16)
    xn = p.alloc("xn_scr", [128, 2 * D], F32)
    gm = p.alloc("gmax", [128, 8], F32)
    st = mk.stat
    epsb = mk.epsbuf(1e-3)
    identf = mk.const("identf")
    def xn_A(i):
        par = i % 2
        b0 = par * 16
        sk = f"stat#{par}"
        xap = mk.X[:, i * D:(i + 1) * D]
        xk = f"X#{i}"
        for ch_ in range(2):
            p.emit("dve", lambda e, xap=xap, ch_=ch_, b0=b0: e.bn_stats(
                st[:, b0 + ch_ * 6: b0 + (ch_ + 1) * 6],
                xap[:, ch_ * 512:(ch_ + 1) * 512]),
                reads=[xk], writes=[sk])
        p.emit("dve", lambda e, b0=b0: e.bn_aggr(st[:, b0 + 12:b0 + 14],
                                                 st[:, b0:b0 + 12]),
               reads=[sk], writes=[sk])
        p.emit("act", lambda e, b0=b0: e.activation(
            st[:, b0 + 14:b0 + 15], st[:, b0 + 13:b0 + 14], AF.Sqrt,
            bias=epsb.ap), reads=[sk, epsb.name], writes=[sk])

    def xn_CD(i):
        par = i % 2
        b0 = par * 16
        sk = f"stat#{par}"
        xnk = f"xn_scr#{par}"
        xna = xn.t[:, par * D:(par + 1) * D]
        xap = mk.X[:, i * D:(i + 1) * D]
        xk = f"X#{i}"
        p.emit("dve", lambda e, b0=b0: e.reciprocal(st[:, b0 + 15:b0 + 16],
                                                    st[:, b0 + 14:b0 + 15]),
               reads=[sk], writes=[sk])
        p.emit("dve", lambda e, b0=b0: e.scalar_tensor_tensor(
            st[:, b0 + 12:b0 + 13], st[:, b0 + 12:b0 + 13], -1.0,
            st[:, b0 + 15:b0 + 16], ALU.mult, ALU.mult),
            reads=[sk], writes=[sk])
        p.emit("act", lambda e, xap=xap, xna=xna, b0=b0: e.activation(
            xna, xap, AF.Identity, bias=st[:, b0 + 12:b0 + 13],
            scale=st[:, b0 + 15:b0 + 16]),
            reads=[xk, sk], writes=[xnk])

    def xn_EF(i):
        par = i % 2
        xnk = f"xn_scr#{par}"
        xna = xn.t[:, par * D:(par + 1) * D]
        p.emit("dve", lambda e, xna=xna: e.tensor_mul(xna, xna, gtab.ap),
               reads=[xnk, gtab.name], writes=[xnk])
        p.emit("dve", lambda e, xna=xna: e.tensor_add(xna, xna, btab.ap),
               reads=[xnk, btab.name], writes=[xnk])
        if i < 2:
            p.emit("dve", lambda e, xna=xna, par=par: e.tensor_reduce(
                gm[:, par:par + 1], xna, axis=AX.X, op=ALU.max,
                apply_absolute_value=True), reads=[xnk], writes=[f"gmax#{par}"])
        else:
            p.emit("dve", lambda e, xna=xna, par=par: e.tensor_reduce(
                gm[:, 2 + par:3 + par], xna, axis=AX.X, op=ALU.max,
                apply_absolute_value=True), reads=[xnk], writes=[f"gmax#{par}"])
            p.emit("dve", lambda e, par=par: e.tensor_max(
                gm[:, par:par + 1], gm[:, par:par + 1], gm[:, 2 + par:3 + par]),
                reads=[f"gmax#{par}"], writes=[f"gmax#{par}"])
        for ic0 in range(0, NC_, 4):
            bank = par * 2 + ic0 // 4
            bk = f"ps{bank}"
            for j in range(4):
                ic = ic0 + j
                ps = p.PS[:, bank * 512 + j * 128: bank * 512 + (j + 1) * 128]
                mk.transpose128(bk, ps, xna[:, ic * 128:(ic + 1) * 128],
                                [xnk], identf, wover=(j > 0))
            srcap = bass.AP(p.PS, bank * 512, [[4096, 128], [1, 512]])
            dstap = bass.AP(xnT.t, ic0 * S + i * 128,
                            [[NC_ * S, 128], [S, 4], [1, 128]])
            mk.evac(dstap, srcap, reads=[bk],
                    writes=[f"xnT#{ic0 + j}" for j in range(4)])

    for i in range(NT):
        xn_A(i)
        if i >= 1:
            xn_CD(i - 1)
        if i >= 2:
            xn_EF(i - 2)
    xn_CD(NT - 1)
    xn_EF(NT - 2)
    xn_EF(NT - 1)
    p.emit("dve", lambda e: e.tensor_max(gm[:, 0:1], gm[:, 0:1], gm[:, 1:2]),
           reads=["gmax#0", "gmax#1"], writes=["gmax"])
    p.freebuf(xn)
    p.emit("dve", lambda e: e.memset(gm[:, 4:6], 0.0),
           reads=["gmax"], writes=["gmax"])
    mk.part_reduce(gm[:, 0:1], "gmax", gm[0:1, 4:5], "gmax", ALU.max)
    ccbuf = p.alloc("ccbuf", [1, 16], F32)
    p.emit("dve", lambda e: e.memset(ccbuf.ap, 0.0), writes=["ccbuf"])
    p.emit("dve", lambda e: e.tensor_copy(ccbuf[0:1, 0:1], gm[0:1, 4:5]),
           reads=["gmax"], writes=["ccbuf"])
    if n_cores > 1:
        cc_in = nc.dram_tensor("cc_in_l0f", [1, 16], F32, kind="Internal")
        cc_out = nc.dram_tensor("cc_out_l0f", [1, 16], F32, kind="Internal")
        p.emit("pool", lambda e: e.dma_start(out=cc_in[:, :], in_=ccbuf.ap),
               reads=["ccbuf"], writes=["ccdram"], ch="qpool")
        p.emit("pool", lambda e: e.collective_compute(
            "AllReduce", ALU.max, replica_groups=[list(range(n_cores))],
            ins=[cc_in.ap().opt()], outs=[cc_out.ap().opt()]),
            reads=["ccdram"], writes=["ccdram2"], ch="cc")
        p.emit("sp", lambda e: e.dma_start(out=ccbuf[0:1, 0:16], in_=cc_out[:, :]),
               reads=["ccdram2"], writes=["ccbuf"], ch="qsp")
    p.emit("dve", lambda e: e.reciprocal(ccbuf[0:1, 1:2], ccbuf[0:1, 0:1]),
           reads=["ccbuf"], writes=["ccbuf"])
    p.emit("dve", lambda e: e.tensor_scalar_mul(ccbuf[0:1, 2:3], ccbuf[0:1, 1:2],
                                                128.0), reads=["ccbuf"], writes=["ccbuf"])
    mk.bcast_scalar(ccbuf[0:1, 2:3], "ccbuf", mk.biasb[:, 0:1], "biasb")
    CLIP = 128.0 - 1e-5
    for ic in range(NC_):
        sl = xnT.t[:, ic * S:(ic + 1) * S]
        p.emit("dve", lambda e, sl=sl: e.tensor_scalar(
            sl, sl, mk.biasb[:, 0:1], CLIP, ALU.mult, ALU.min),
            reads=[f"xnT#{ic}", "biasb"], writes=[f"xnT#{ic}"])
        p.emit("dve", lambda e, sl=sl: e.tensor_scalar_max(sl, sl, -CLIP),
            reads=[f"xnT#{ic}"], writes=[f"xnT#{ic}"])
    wq = p.alloc("l0f_wq", [128, NC_ * D], BF16)
    wstage = p.alloc("l0f_wst", [128, D], F32)
    wsum = p.alloc("wsum", [128, 4], F32)
    for ic in range(NC_):
        p.emit("sp", lambda e, ic=ic: e.dma_start(
            out=wstage.ap,
            in_=io["L0_ffn_weight"][ic * 128:(ic + 1) * 128, :]),
            reads=["l0f_wst"], writes=["l0f_wst"], ch="qsp")
        if ic == 0:
            p.emit("dve", lambda e: e.tensor_reduce(
                wsum[:, 0:1], wstage.ap, axis=AX.X,
                op=ALU.add, apply_absolute_value=True),
                reads=["l0f_wst"], writes=["wsum"])
        else:
            p.emit("dve", lambda e: e.tensor_reduce(
                wsum[:, 1:2], wstage.ap, axis=AX.X,
                op=ALU.add, apply_absolute_value=True),
                reads=["l0f_wst"], writes=["wsum"])
            p.emit("dve", lambda e: e.tensor_add(wsum[:, 0:1], wsum[:, 0:1],
                                                 wsum[:, 1:2]),
                   reads=["wsum"], writes=["wsum"])
    mk.part_reduce(wsum[:, 0:1], "wsum", ccbuf[0:1, 4:5], "ccbuf", ALU.add)
    p.emit("dve", lambda e: e.tensor_scalar_mul(ccbuf[0:1, 5:6], ccbuf[0:1, 4:5],
                                                1.0 / (D * D)),
           reads=["ccbuf"], writes=["ccbuf"])
    p.emit("dve", lambda e: e.tensor_scalar(ccbuf[0:1, 6:7], ccbuf[0:1, 5:6],
                                            1e-5, 0.5, ALU.add, ALU.mult),
           reads=["ccbuf"], writes=["ccbuf"])
    p.emit("dve", lambda e: e.tensor_mul(ccbuf[0:1, 7:8], ccbuf[0:1, 5:6],
                                         ccbuf[0:1, 0:1]),
           reads=["ccbuf"], writes=["ccbuf"])
    p.emit("dve", lambda e: e.tensor_scalar_mul(ccbuf[0:1, 8:9], ccbuf[0:1, 7:8],
                                                1.0 / 128.0),
           reads=["ccbuf"], writes=["ccbuf"])
    mk.bcast_scalar(ccbuf[0:1, 6:7], "ccbuf", mk.biasb[:, 1:2], "biasb")
    mk.bcast_scalar(ccbuf[0:1, 8:9], "ccbuf", mk.biasb[:, 2:3], "biasb")
    tmp = p.alloc("l0f_tmp", [128, D], F32)
    for ic in range(NC_):
        qsl = wq.t[:, ic * D:(ic + 1) * D]
        p.emit("sp", lambda e, ic=ic: e.dma_start(
            out=wstage.ap,
            in_=io["L0_ffn_weight"][ic * 128:(ic + 1) * 128, :]),
            reads=["l0f_wst"], writes=["l0f_wst"], ch="qsp")
        p.emit("dve", lambda e, qsl=qsl: e.tensor_scalar(
            qsl, wstage.ap, mk.biasb[:, 1:2], None, ALU.is_gt),
            reads=["l0f_wst", "biasb"], writes=[f"l0f_wq#{ic}"])
        p.emit("dve", lambda e: e.tensor_scalar_mul(tmp.ap, wstage.ap, -1.0),
               reads=["l0f_wst"], writes=["l0f_tmp"])
        p.emit("dve", lambda e: e.tensor_scalar(
            tmp.ap, tmp.ap, mk.biasb[:, 1:2], None, ALU.is_gt),
            reads=["l0f_tmp", "biasb"], writes=["l0f_tmp"])
        p.emit("dve", lambda e, qsl=qsl: e.tensor_sub(qsl, qsl, tmp.ap),
            reads=[f"l0f_wq#{ic}", "l0f_tmp"], writes=[f"l0f_wq#{ic}"])
    p.freebuf(wstage); p.freebuf(tmp)
    out_proj_residual(mk, xnT, wq, "xnT", "l0f_wq", None, 1e-5,
                      scale_ap=mk.biasb[:, 2:3], scale_key="biasb")
    p.freebuf(xnT); p.freebuf(wq); p.freebuf(gtab); p.freebuf(btab)
    p.freebuf(gm); p.freebuf(ccbuf); p.freebuf(wsum)


def block_l1_attn(mk):
    p = mk.p
    io = mk.io
    H = 16
    KSPAN = 17 * 128
    mk.const("l1a_cos"); mk.const("l1a_sinsw")
    XT = p.alloc("XT", [128, NC_ * S], BF16)
    mk.full_transpose_cast(
        lambda ic, t: mk.X[:, t * D + ic * 128: t * D + ic * 128 + 128],
        lambda ic, t: f"X#{t}", XT, lambda ic, t: f"XT#{ic}")
    mk.spill_x("l1a")

    cosb = mk.const("l1a_cos")
    sinswb = mk.const("l1a_sinsw")
    QT = p.alloc("QTa", [128, NC_ * S], BF16)
    KT = p.alloc("KTa", [128, NC_ * KSPAN], BF16)
    VW = H * 65
    NKT = 17
    rA, rB, rC = mk.rA, mk.rB, mk.rC
    p.emit("dve", lambda e: e.memset(KT.ap, 0.0),
           writes=[f"KTa#{ic}" for ic in range(NC_)])

    def load_cat_weight(wname, sem_dram, geo_dram, scale):
        wt = p.alloc(wname, [128, NC_ * D], BF16)
        stg = p.alloc(wname + "_stg", [128, 1024], F32)
        skey = wname + "_stg"
        for ic in range(NC_):
            p.emit("sp", lambda e, ic=ic: e.dma_start(
                out=stg[:, 0:512], in_=sem_dram[ic * 128:(ic + 1) * 128, :]),
                reads=[skey], writes=[skey], ch="qsp")
            p.emit("sp", lambda e, ic=ic: e.dma_start(
                out=stg[:, 512:1024], in_=geo_dram[ic * 128:(ic + 1) * 128, :]),
                reads=[skey], writes=[skey], ch="qsp")
            for part in range(2):
                srcap = bass.AP(stg.t, part * 512, [[1024, 128], [32, 16], [1, 32]])
                dstap = bass.AP(wt.t, ic * D + part * 32,
                                [[NC_ * D, 128], [64, 16], [1, 32]])
                if scale is None:
                    p.emit("dve", lambda e, srcap=srcap, dstap=dstap:
                           e.tensor_copy(dstap, srcap),
                           reads=[skey], writes=[f"{wname}#{ic}"])
                else:
                    p.emit("dve", lambda e, srcap=srcap, dstap=dstap, scale=scale:
                           e.tensor_scalar_mul(dstap, srcap, scale),
                           reads=[skey], writes=[f"{wname}#{ic}"])
        p.freebuf(stg)
        return wt

    def project_chunk(wt, wname, oc, b0=0):
        for tb in range(4):
            members = []
            for ic in range(NC_):
                lhs = wt.t[:, ic * D + oc * 128: ic * D + oc * 128 + 128]
                rhs = XT.t[:, ic * S + tb * 512: ic * S + tb * 512 + 512]
                members.append((lambda e, lhs=lhs, rhs=rhs, b=b0 + tb, ic=ic:
                                e.matmul(p.PS[:, b * 512:(b + 1) * 512], lhs, rhs,
                                         start=(ic == 0), stop=(ic == NC_ - 1)),
                                [f"{wname}#{ic}", f"XT#{ic}"]))
            mk.mm_group(f"ps{b0 + tb}", members)

    def rope_evac(dst, dname, oc, dcol, cos, sinsw, b0=0):
        pall = p.PS[:, b0 * 512: b0 * 512 + 2048]
        pk = [f"ps{b0 + j}" for j in range(4)]
        p.emit("dve", lambda e, pall=pall: e.tensor_mul(rA.ap, pall, cos.ap),
               reads=pk + [cos.name], writes=["ropeA"])
        p.emit("dve", lambda e, pall=pall: e.tensor_mul(rB.ap, pall, sinsw.ap),
               reads=pk + [sinsw.name], writes=["ropeB"])
        mask = [(i + 16) % 32 for i in range(32)]
        p.emit("dve", lambda e, mask=mask: e.stream_shuffle(rC.ap, rB.ap, mask),
               reads=["ropeB"], writes=["ropeC"])
        p.emit("dve", lambda e: e.tensor_add(dst.t[:, dcol:dcol + S], rA.ap, rC.ap),
               reads=["ropeA", "ropeC"], writes=[f"{dname}#{oc}"])

    sc = 1.0 / np.sqrt(32.0)
    Wq = load_cat_weight("WqcT", io["L1_attn_wq_sem"], io["L1_attn_wq_geo"], sc)
    for oc in range(NC_):
        project_chunk(Wq, "WqcT", oc, (oc % 2) * 4)
        if oc > 0:
            rope_evac(QT, "QTa", oc - 1, (oc - 1) * S, cosb, sinswb,
                      ((oc - 1) % 2) * 4)
    rope_evac(QT, "QTa", NC_ - 1, (NC_ - 1) * S, cosb, sinswb, ((NC_ - 1) % 2) * 4)
    p.freebuf(Wq)
    Wk = load_cat_weight("WkcT", io["L1_attn_wk_sem"], io["L1_attn_wk_geo"], None)
    for oc in range(NC_):
        project_chunk(Wk, "WkcT", oc, (oc % 2) * 4)
        if oc > 0:
            rope_evac(KT, "KTa", oc - 1, (oc - 1) * KSPAN, cosb, sinswb,
                      ((oc - 1) % 2) * 4)
    rope_evac(KT, "KTa", NC_ - 1, (NC_ - 1) * KSPAN, cosb, sinswb,
              ((NC_ - 1) % 2) * 4)
    p.freebuf(Wk)
    mk.free_const("l1a_cos"); mk.free_const("l1a_sinsw")
    head_stats(mk, QT, "QTa", 0)
    for ic in range(NC_):
        for part, nm in ((0, "L1_attn_k_sem_null"), (1, "L1_attn_k_geo_null")):
            for hh in range(2):
                h = 2 * ic + hh
                dst = KT.t[:, ic * KSPAN + 2048: ic * KSPAN + 2049]
                dstap = bass.AP(KT.t, (hh * 64 + part * 32) * (NC_ * KSPAN)
                                + ic * KSPAN + 2048,
                                [[NC_ * KSPAN, 32], [1, 1]])
                srcap = io[nm][h * 32:(h + 1) * 32]
                p.emit("pool", lambda e, dstap=dstap, srcap=srcap:
                       e.dma_start(out=dstap, in_=srcap.unsqueeze(1)),
                       reads=[f"KTa#{ic}"], writes=[f"KTa#{ic}"], ch="qpool")

    VpA = p.alloc("VpaA", [128, 8 * VW], BF16)
    VpB = p.alloc("VpaB", [128, 9 * VW], BF16)

    def vp_fn(kt):
        if kt < 8:
            return VpA, kt * VW, f"VpaA#{kt}"
        return VpB, (kt - 8) * VW, f"VpaB#{kt - 8}"
    for vb, vn, ntk in ((VpA, "VpaA", 8), (VpB, "VpaB", 9)):
        p.emit("dve", lambda e, vb=vb: e.memset(vb.ap, 0.0),
               writes=[f"{vn}#{kt}" for kt in range(ntk)])
        ones_ap = bass.AP(vb.t, 64, [[ntk * VW, 128], [VW, ntk], [65, H]])
        p.emit("dve", lambda e, ones_ap=ones_ap: e.memset(ones_ap, 1.0),
               writes=[f"{vn}#{kt}" for kt in range(ntk)])
    vchunk = p.alloc("vchunk", [128, S], F32)
    identf = mk.const("identf")
    WvT = mk.load_weight_N("WvTa", io["L1_attn_wv"])

    def v_evac_a(oc, b0):
        pall = p.PS[:, b0 * 512: b0 * 512 + 2048]
        pk = [f"ps{b0 + j}" for j in range(4)]
        p.emit("dve", lambda e, pall=pall: e.tensor_copy(vchunk.ap, pall),
               reads=pk, writes=["vchunk"])
        h0 = 2 * oc
        for g4 in range(4):
            bank = b0 + g4
            bk = f"ps{bank}"
            for j in range(4):
                kt = g4 * 4 + j
                ps = p.PS[:, bank * 512 + j * 128: bank * 512 + (j + 1) * 128]
                mk.transpose128(bk, ps, vchunk[:, kt * 128:(kt + 1) * 128],
                                ["vchunk"], identf, wover=(j > 0))
            vb, vcol0, _ = vp_fn(g4 * 4)
            nwid = 8 if g4 < 2 else 9
            srcap = bass.AP(p.PS, bank * 512,
                            [[4096, 128], [128, 4], [64, 2], [1, 64]])
            dstap = bass.AP(vb.t, vcol0 + h0 * 65,
                            [[nwid * VW, 128], [VW, 4], [65, 2], [1, 64]])
            p.emit("dve", lambda e, srcap=srcap, dstap=dstap:
                   e.tensor_copy(dstap, srcap),
                   reads=[bk], writes=[vp_fn(g4 * 4 + j)[2] for j in range(4)])

    for oc in range(NC_):
        project_chunk(WvT, "WvTa", oc, (oc % 2) * 4)
        if oc > 0:
            v_evac_a(oc - 1, ((oc - 1) % 2) * 4)
    v_evac_a(NC_ - 1, ((NC_ - 1) % 2) * 4)
    p.freebuf(WvT); p.freebuf(vchunk); p.freebuf(XT)
    vsrc = io["L1_attn_v_null"]
    vdst = bass.AP(VpB.t, 8 * VW, [[9 * VW, 1], [65, 16], [1, 64]])
    p.emit("pool", lambda e, vdst=vdst, vsrc=vsrc: e.dma_start(
        out=vdst, in_=vsrc.rearrange("(h d) -> h d", d=64).unsqueeze(0)),
        reads=["VpaB#8"], writes=["VpaB#8"], ch="qpool")

    head_stats(mk, KT, "KTa", 8, span=KSPAN)
    finish_bounds(mk, H)

    OnT = p.alloc("OnTa", [128, NC_ * S], BF16)
    mk.PTb = p.alloc("PTb", [128, 3072], BF16)
    kt_lists = []
    for qb in range(4):
        lst = [(kt, ("band" if kt * 128 - qb * 512 >= 0 else None),
                (kt * 128 - qb * 512 if kt * 128 - qb * 512 >= 0 else None))
               for kt in range(4 * (qb + 1))]
        lst.append((16, None, None))
        kt_lists.append(lst)
    attn_loop(mk, QT, KT, vp_fn, OnT, H, S, kt_lists, null_pad_tile=16,
              kspan=KSPAN, names=("QTa", "KTa", "Vpa", "OnTa"))
    p.freebuf(QT); p.freebuf(KT); p.freebuf(VpA); p.freebuf(VpB)
    p.freebuf(mk.PTb)
    del mk.PTb

    WoT = mk.load_weight_N("WoTa", io["L1_attn_wo"])
    mk.realloc_x()
    out_proj_residual(mk, OnT, WoT, "OnTa", "WoTa", "l1a", post_eps=1e-5)
    p.freebuf(OnT); p.freebuf(WoT)


def l1f_shiftbuf(mk, name, spill_tag):
    p = mk.p
    sp = mk.spill[spill_tag]
    sb = p.alloc(name, [128, NT * 512], F32)
    p.emit("dve", lambda e, sb=sb: e.memset(sb.t[0:1, 0:512], 0.0),
           writes=[f"{name}#0"])
    for i in range(NT):
        eng, ch = p.hch()
        if i == 0:
            dst = sb.t[1:128, 0:512]
            srcs = sp[0:127, 512:1024]
            rds = [f"sp_{spill_tag}#0"]
        else:
            dst = sb.t[0:128, i * 512:(i + 1) * 512]
            srcs = sp[i * 128 - 1: i * 128 + 127, 512:1024]
            rds = [f"sp_{spill_tag}#{i-1}", f"sp_{spill_tag}#{i}"]
        p.emit(eng, lambda e, dst=dst, srcs=srcs: e.dma_start(
            out=dst, in_=srcs), reads=rds, writes=[f"{name}#{i}"], ch=ch)
    return sb


def l1f_sub_attn(mk, l, n_sub):
    p = mk.p
    io = mk.io
    H, HD = 8, 64
    NCQ = 4
    wqkv = mk.load_weight_N(f"wqkv{l}", io["L1_ffn_wqkv"][l], O=1536)
    qs8 = p.alloc("qs8", [128, 512], F32)
    ks8 = p.alloc("ks8", [128, 512], F32)
    for tab, nm in ((qs8, "L1_ffn_q_scale"), (ks8, "L1_ffn_k_scale")):
        srcap = bass.AP(io[nm].tensor, l * 64, [[0, 128], [0, 8], [1, 64]])
        p.emit("sp", lambda e, tab=tab, srcap=srcap: e.dma_start(
            out=tab.ap, in_=srcap), writes=[tab.name], ch="qsp")
    atag = "l1f_in" if l == 0 else f"l1f_s{l}a"
    mk.spill_x(atag, free=False)
    sb = l1f_shiftbuf(mk, f"shb{l}", atag)
    hT = p.alloc(f"hT{l}", [128, NC_ * S], BF16)
    def src_ap(ic, t):
        if ic < 4:
            return mk.X[:, t * D + ic * 128: t * D + ic * 128 + 128]
        return sb.t[:, t * 512 + (ic - 4) * 128: t * 512 + (ic - 4) * 128 + 128]
    def src_key(ic, t):
        return f"X#{t}" if ic < 4 else f"shb{l}#{t}"
    mk.full_transpose_cast(src_ap, src_key, hT, lambda ic, t: f"hT{l}#{ic}")
    p.freebuf(sb)
    p.freebuf(mk.X)
    mk.X = None

    xtabs = {}
    for nm in ("xq_c", "xq_s", "xk_c", "xk_s"):
        key = "dram_" + nm
        if key not in mk.cbuf:
            arr = mk.consts_np[nm].astype(np.float32)
            mk.cbuf[key] = mk.nc.inline_tensor(np.ascontiguousarray(arr),
                                               name=f"c{mk.uid()}_{nm}")
        xtabs[nm] = mk.cbuf[key]
    TBL = p.alloc("xtbl", [128, 2 * 4 * 512], F32)
    QTf = p.alloc(f"QTf{l}", [128, NCQ * S], BF16)
    KTf = p.alloc(f"KTf{l}", [128, NCQ * S], BF16)
    VW = H * 65
    Vp = p.alloc(f"Vpf{l}", [128, NT * VW], BF16)
    ones_ap = bass.AP(Vp.t, 64, [[NT * VW, 128], [VW, NT], [65, H]])
    p.emit("dve", lambda e: e.memset(ones_ap, 1.0),
           writes=[f"Vpf{l}#{kt}" for kt in range(NT)])
    nq = p.alloc("nq", [128, NT * 8], F32)
    nk = p.alloc("nk", [128, NT * 8], F32)
    sc1 = p.alloc("sc1", [128, 1024], F32)
    ss8 = p.alloc("ss8", [128, 48], F32)
    qh2 = p.alloc("qh2", [128, 1024], F32)
    qro = p.alloc("qro", [128, 1024], F32)
    identf = mk.const("identf")

    def emit_proj(i):
        b0 = (i % 2) * 3
        for c3 in range(3):
            members = []
            for ic in range(NC_):
                lhs = hT.t[:, ic * S + i * 128: ic * S + i * 128 + 128]
                rhs = wqkv.t[:, ic * 1536 + c3 * 512: ic * 1536 + (c3 + 1) * 512]
                members.append((lambda e, lhs=lhs, rhs=rhs, b=b0 + c3, ic=ic:
                                e.matmul(p.PS[:, b * 512:(b + 1) * 512], lhs, rhs,
                                         start=(ic == 0), stop=(ic == NC_ - 1)),
                                [f"hT{l}#{ic}", f"wqkv{l}#{ic}"]))
            mk.mm_group(f"ps{b0 + c3}", members)
        ts0 = (i % 2) * 2048
        for ti, nm in enumerate(("xq_c", "xq_s", "xk_c", "xk_s")):
            h64 = xtabs[nm]
            srcap = bass.AP(h64, i * 64, [[NT * 64, 128], [0, 8], [1, 64]])
            eng, ch = p.hch()
            p.emit(eng, lambda e, ti=ti, srcap=srcap, ts0=ts0: e.dma_start(
                out=TBL.t[:, ts0 + ti * 512: ts0 + (ti + 1) * 512], in_=srcap),
                reads=[f"xtbl#{i % 2}"], writes=[f"xtbl#{i % 2}"], ch=ch)

    def emit_process(i):
        b0 = (i % 2) * 3
        ts0 = (i % 2) * 2048
        chains = []
        for qk in range(2):
            ops = []
            bank = b0 + qk
            bk = f"ps{bank}"
            psl = p.PS[:, bank * 512:(bank + 1) * 512]
            s0 = qk * 512
            n0 = qk * 24
            sck, ssk, qhk, qrk = (f"sc1#{qk}", f"ss8#{qk}", f"qh2#{qk}",
                                  f"qro#{qk}")
            sca = sc1.t[:, s0:s0 + 512]
            qha = qh2.t[:, s0:s0 + 512]
            qra = qro.t[:, s0:s0 + 512]
            ops.append(lambda psl=psl, sca=sca, bk=bk, sck=sck: p.emit(
                "act", lambda e, psl=psl, sca=sca: e.activation(
                    sca, psl, AF.Square), reads=[bk], writes=[sck]))
            ops.append(lambda sca=sca, n0=n0, sck=sck, ssk=ssk: p.emit(
                "dve", lambda e, sca=sca, n0=n0: e.tensor_reduce(
                    ss8[:, n0:n0 + 8], sca.rearrange("p (h d) -> p h d", d=64),
                    axis=AX.X, op=ALU.add), reads=[sck], writes=[ssk]))
            ops.append(lambda n0=n0, ssk=ssk: p.emit(
                "act", lambda e, n0=n0: e.activation(
                    ss8[:, n0 + 8:n0 + 16], ss8[:, n0:n0 + 8], AF.Sqrt),
                reads=[ssk], writes=[ssk]))
            ops.append(lambda n0=n0, ssk=ssk: p.emit(
                "dve", lambda e, n0=n0: e.tensor_scalar_max(
                    ss8[:, n0 + 8:n0 + 16], ss8[:, n0 + 8:n0 + 16], 1e-12),
                reads=[ssk], writes=[ssk]))
            ops.append(lambda n0=n0, ssk=ssk: p.emit(
                "dve", lambda e, n0=n0: e.reciprocal(
                    ss8[:, n0 + 16:n0 + 24], ss8[:, n0 + 8:n0 + 16]),
                reads=[ssk], writes=[ssk]))
            for h in range(H):
                ops.append(lambda psl=psl, qha=qha, h=h, n0=n0, bk=bk,
                           ssk=ssk, qhk=qhk: p.emit(
                    "dve", lambda e, psl=psl, qha=qha, h=h, n0=n0: e.tensor_scalar(
                        qha[:, h * 64:(h + 1) * 64], psl[:, h * 64:(h + 1) * 64],
                        ss8[:, n0 + 16 + h:n0 + 17 + h], None, ALU.mult),
                    reads=[bk, ssk], writes=[qhk]))
            stab = qs8 if qk == 0 else ks8
            ops.append(lambda stab=stab, qha=qha, qhk=qhk: p.emit(
                "dve", lambda e, stab=stab, qha=qha: e.tensor_mul(
                    qha, qha, stab.ap), reads=[qhk, stab.name], writes=[qhk]))
            ops.append(lambda sca=sca, qha=qha, sck=sck, qhk=qhk: p.emit(
                "act", lambda e, sca=sca, qha=qha: e.activation(
                    sca, qha, AF.Square), reads=[qhk], writes=[sck]))
            nbuf, nname = (nq, "nq") if qk == 0 else (nk, "nk")
            ops.append(lambda nbuf=nbuf, sca=sca, sck=sck, nname=nname: p.emit(
                "dve", lambda e, nbuf=nbuf, sca=sca, i=i: e.tensor_reduce(
                    nbuf.t[:, i * 8:(i + 1) * 8],
                    sca.rearrange("p (h d) -> p h d", d=64),
                    axis=AX.X, op=ALU.add), reads=[sck], writes=[nname]))
            ct = TBL.t[:, ts0 + (qk * 2) * 512: ts0 + (qk * 2 + 1) * 512]
            st_ = TBL.t[:, ts0 + (qk * 2 + 1) * 512: ts0 + (qk * 2 + 2) * 512]
            swap_ap = bass.AP(qh2.t, s0 + 32,
                              [[1024, 128], [64, 8], [-32, 2], [1, 32]])
            ops.append(lambda ct=ct, qra=qra, qha=qha, qhk=qhk, qrk=qrk: p.emit(
                "dve", lambda e, ct=ct, qra=qra, qha=qha: e.tensor_mul(
                    qra, qha, ct), reads=[qhk, f"xtbl#{i % 2}"], writes=[qrk]))
            ops.append(lambda st_=st_, swap_ap=swap_ap, sca=sca, qhk=qhk,
                       sck=sck: p.emit(
                "dve", lambda e, st_=st_, swap_ap=swap_ap, sca=sca: e.tensor_tensor(
                    sca, swap_ap, st_, ALU.mult),
                reads=[qhk, f"xtbl#{i % 2}"], writes=[sck]))
            ops.append(lambda qra=qra, sca=sca, qrk=qrk, sck=sck: p.emit(
                "dve", lambda e, qra=qra, sca=sca: e.tensor_add(qra, qra, sca),
                reads=[qrk, sck], writes=[qrk]))
            TF = QTf if qk == 0 else KTf
            tfn = f"QTf{l}" if qk == 0 else f"KTf{l}"
            bank2 = 6 + qk
            bk2 = f"ps{bank2}"
            for c in range(4):
                ps2 = p.PS[:, bank2 * 512 + c * 128: bank2 * 512 + (c + 1) * 128]
                ops.append(lambda ps2=ps2, qra=qra, c=c, bk2=bk2, qrk=qrk:
                           mk.transpose128(bk2, ps2, qra[:, c * 128:(c + 1) * 128],
                                           [qrk], identf, wover=(c > 0)))
            srcap = bass.AP(p.PS, bank2 * 512, [[4096, 128], [1, 512]])
            dstap = bass.AP(TF.t, i * 128, [[NCQ * S, 128], [S, 4], [1, 128]])
            ops.append(lambda srcap=srcap, dstap=dstap, bk2=bk2, tfn=tfn:
                       mk.evac(dstap, srcap, reads=[bk2],
                               writes=[f"{tfn}#{c}" for c in range(4)]))
            chains.append(ops)
        for a, b in zip(chains[0], chains[1]):
            a()
            b()
        vb = b0 + 2
        vsrc = p.PS[:, vb * 512:(vb + 1) * 512].rearrange("p (h d) -> p h d", d=64)
        vdst = Vp.t[:, i * VW: i * VW + VW].rearrange("p (h d) -> p h d", d=65)[:, :, 0:64]
        p.emit("dve", lambda e, vsrc=vsrc, vdst=vdst: e.tensor_copy(vdst, vsrc),
               reads=[f"ps{vb}"], writes=[f"Vpf{l}#{i}"])

    for i in range(NT):
        emit_proj(i)
        if i > 0:
            emit_process(i - 1)
    emit_process(NT - 1)
    p.freebuf(hT); p.freebuf(wqkv); p.freebuf(TBL)
    p.freebuf(sc1); p.freebuf(qh2); p.freebuf(qro)
    p.freebuf(qs8); p.freebuf(ks8)

    for w in range(4):
        for qk in range(2):
            nbuf, nname = (nq, "nq") if qk == 0 else (nk, "nk")
            t0_ = 4 * w if qk == 0 else max(0, 4 * w - 4)
            t1_ = 4 * w + 4
            p.emit("dve", lambda e, nbuf=nbuf, t0_=t0_: e.tensor_copy(
                ss8[:, 0:8], nbuf.t[:, t0_ * 8:(t0_ + 1) * 8]),
                reads=[nname], writes=["ss8#0"])
            for t_ in range(t0_ + 1, t1_):
                p.emit("dve", lambda e, nbuf=nbuf, t_=t_: e.tensor_max(
                    ss8[:, 0:8], ss8[:, 0:8], nbuf.t[:, t_ * 8:(t_ + 1) * 8]),
                    reads=["ss8#0", nname], writes=["ss8#0"])
            mk.transpose128("ps7", p.PS[0:8, 7 * 512:7 * 512 + 128],
                            ss8[:, 0:8], ["ss8#0"], identf)
            p.emit("dve", lambda e: e.tensor_reduce(
                mk.bb[0:8, 0:1] if False else mk.bb[0:8, 16 + 0:17],
                p.PS[0:8, 7 * 512:7 * 512 + 128], axis=AX.X, op=ALU.max),
                reads=["ps7"], writes=["bb"])
            p.emit("sp", lambda e, w=w, qk=qk: e.dma_start(
                out=mk.brow[0:1, 64 + qk * 8: 64 + (qk + 1) * 8],
                in_=mk.bb[0:8, 16:17]),
                reads=["bb"], writes=["brow"], ch="qsp")
        p.emit("dve", lambda e, w=w: e.tensor_tensor(
            mk.brow[0:1, w * 16: w * 16 + 8], mk.brow[0:1, 64:72],
            mk.brow[0:1, 72:80], ALU.mult), reads=["brow"], writes=["brow"])
        p.emit("act", lambda e, w=w: e.activation(
            mk.brow[0:1, w * 16: w * 16 + 8], mk.brow[0:1, w * 16: w * 16 + 8],
            AF.Sqrt), reads=["brow"], writes=["brow"])
        p.emit("dve", lambda e, w=w: e.tensor_scalar_mul(
            mk.brow[0:1, w * 16: w * 16 + 8], mk.brow[0:1, w * 16: w * 16 + 8],
            8.0), reads=["brow"], writes=["brow"])
    p.freebuf(nq); p.freebuf(nk); p.freebuf(ss8)

    OnT = p.alloc(f"OnTf{l}", [128, NCQ * S], BF16)
    mk.PTb = p.alloc("PTb", [128, 3072], BF16)
    kt_lists = []
    for w in range(4):
        lst = []
        if w > 0:
            for kt in range(4 * (w - 1), 4 * w):
                lst.append((kt, "band_prev", kt * 128 - 512 * (w - 1)))
        for kt in range(4 * w, 4 * w + 4):
            lst.append((kt, "band", kt * 128 - 512 * w))
        kt_lists.append(lst)
    attn_loop(mk, QTf, KTf,
              lambda kt: (Vp, kt * VW, f"Vpf{l}#{kt}"), OnT, H, S, kt_lists,
              names=(f"QTf{l}", f"KTf{l}", f"Vpf{l}", f"OnTf{l}"),
              bias_col=lambda h, qb: qb * 16 + h)
    p.freebuf(QTf); p.freebuf(KTf); p.freebuf(Vp); p.freebuf(mk.PTb)
    del mk.PTb

    WoT = mk.load_weight_N(f"wof{l}", io["L1_ffn_wo"][l], O=D)
    mk.realloc_x()
    out_proj_residual(mk, OnT, WoT, f"OnTf{l}", f"wof{l}",
                      "l1f_in" if l == 0 else f"l1f_s{l}a",
                      post_eps=None, nchunks=NCQ)
    p.freebuf(OnT); p.freebuf(WoT)


def l1f_sub_ffn(mk, l, n_sub):
    p = mk.p
    io = mk.io
    HID = 2730
    NU = 22
    mk.spill_x(f"l1f_s{l}f", free=False)
    sb = l1f_shiftbuf(mk, f"sh2b{l}", f"l1f_s{l}f")
    gtab = p.alloc("l1f_g", [128, D], F32)
    btab = p.alloc("l1f_b", [128, D], F32)
    for tab, nm in ((gtab, "L1_ffn_ln_g"), (btab, "L1_ffn_ln_b")):
        srcap = bass.AP(io[nm].tensor, l * D, [[0, 128], [1, D]])
        p.emit("sp", lambda e, tab=tab, srcap=srcap: e.dma_start(
            out=tab.ap, in_=srcap), writes=[tab.name], ch="qsp")
    lnhT = p.alloc(f"lnhT{l}", [128, NC_ * S], BF16)
    hln = p.alloc("hln", [128, 2 * D], F32)
    st = mk.stat
    epsb = mk.epsbuf(1e-5)
    identf = mk.const("identf")
    for i in range(NT):
        par = i % 2
        b0 = par * 16
        sk = f"stat#{par}"
        hk = f"hln#{par}"
        hla = hln.t[:, par * D:(par + 1) * D]
        x1ap = mk.X[:, i * D: i * D + 512]
        x2ap = sb.t[:, i * 512:(i + 1) * 512]
        xk = f"X#{i}"
        sk2 = f"sh2b{l}#{i}"
        p.emit("dve", lambda e, x1ap=x1ap, b0=b0: e.bn_stats(
            st[:, b0:b0 + 6], x1ap), reads=[xk], writes=[sk])
        p.emit("dve", lambda e, x2ap=x2ap, b0=b0: e.bn_stats(
            st[:, b0 + 6:b0 + 12], x2ap), reads=[sk2], writes=[sk])
        p.emit("dve", lambda e, b0=b0: e.bn_aggr(st[:, b0 + 12:b0 + 14],
                                                 st[:, b0:b0 + 12]),
               reads=[sk], writes=[sk])
        p.emit("act", lambda e, b0=b0: e.activation(
            st[:, b0 + 14:b0 + 15], st[:, b0 + 13:b0 + 14], AF.Sqrt,
            bias=epsb.ap), reads=[sk, epsb.name], writes=[sk])
        p.emit("dve", lambda e, b0=b0: e.reciprocal(st[:, b0 + 15:b0 + 16],
                                                    st[:, b0 + 14:b0 + 15]),
               reads=[sk], writes=[sk])
        p.emit("dve", lambda e, b0=b0: e.scalar_tensor_tensor(
            st[:, b0 + 12:b0 + 13], st[:, b0 + 12:b0 + 13], -1.0,
            st[:, b0 + 15:b0 + 16], ALU.mult, ALU.mult),
            reads=[sk], writes=[sk])
        p.emit("act", lambda e, x1ap=x1ap, hla=hla, b0=b0: e.activation(
            hla[:, 0:512], x1ap, AF.Identity, bias=st[:, b0 + 12:b0 + 13],
            scale=st[:, b0 + 15:b0 + 16]), reads=[xk, sk], writes=[hk])
        p.emit("act", lambda e, x2ap=x2ap, hla=hla, b0=b0: e.activation(
            hla[:, 512:1024], x2ap, AF.Identity, bias=st[:, b0 + 12:b0 + 13],
            scale=st[:, b0 + 15:b0 + 16]), reads=[sk2, sk], writes=[hk])
        p.emit("dve", lambda e, hla=hla: e.tensor_mul(hla, hla, gtab.ap),
               reads=[hk, gtab.name], writes=[hk])
        p.emit("dve", lambda e, hla=hla: e.tensor_add(hla, hla, btab.ap),
               reads=[hk, btab.name], writes=[hk])
        for ic0 in range(0, NC_, 4):
            bank = par * 2 + ic0 // 4
            bk = f"ps{bank}"
            for j in range(4):
                ic = ic0 + j
                ps = p.PS[:, bank * 512 + j * 128: bank * 512 + (j + 1) * 128]
                mk.transpose128(bk, ps, hla[:, ic * 128:(ic + 1) * 128],
                                [hk], identf, wover=(j > 0))
            srcap = bass.AP(p.PS, bank * 512, [[4096, 128], [1, 512]])
            dstap = bass.AP(lnhT.t, ic0 * S + i * 128,
                            [[NC_ * S, 128], [S, 4], [1, 128]])
            mk.evac(dstap, srcap, reads=[bk],
                    writes=[f"lnhT{l}#{ic0 + j}" for j in range(4)])
    p.freebuf(sb); p.freebuf(hln); p.freebuf(gtab); p.freebuf(btab)
    p.freebuf(mk.X)
    mk.X = None

    gatedT = p.alloc(f"gT{l}", [128, NU * 1024], BF16)
    w2b = p.alloc(f"w2b{l}", [128, NU * D], BF16)
    b2tab = p.alloc("l1f_b2", [128, D], F32)
    srcap = bass.AP(io["L1_ffn_b2"].tensor, l * D, [[0, 128], [1, D]])
    p.emit("sp", lambda e, srcap=srcap: e.dma_start(out=b2tab.ap, in_=srcap),
           writes=[b2tab.name], ch="qsp")
    w2st = p.alloc("w2st", [128, 2 * D], F32)
    for uc in range(NU):
        u0 = uc * 128
        usz = min(128, HID - u0)
        par = uc % 2
        skey = f"w2st#{par}"
        sta = w2st.t[0:usz, par * D:(par + 1) * D]
        eng, ch = p.hch()
        p.emit(eng, lambda e, u0=u0, usz=usz, sta=sta: e.dma_start(
            out=sta, in_=io["L1_ffn_w2"][l, u0:u0 + usz, :]),
            reads=[skey], writes=[skey], ch=ch)
        mk.evac(w2b.t[0:usz, uc * D:(uc + 1) * D], sta,
                reads=[skey], writes=[f"w2b{l}#{uc}"])
    p.freebuf(w2st)

    w1st = p.alloc("w1st", [128, 2 * NC_ * 128], F32)
    w1a = p.alloc("w1a", [128, 2 * NC_ * 128], BF16)
    w1g = p.alloc("w1g", [128, 2 * NC_ * 128], BF16)
    b1c = p.alloc("b1c", [128, 4], F32)
    gl = p.alloc("glscr", [128, 2048], F32)
    W1 = io["L1_ffn_w1"]
    B1 = io["L1_ffn_b1"]
    for half in range(2):
        tbase = half * 8

        def emit_h1(uc, tbase=tbase):
            u0 = uc * 128
            usz = min(128, HID - u0)
            par = uc % 2
            ws0 = par * 1024
            for side, coff in ((0, 0), (1, HID)):
                wbuf = (w1a, w1g)[side]
                wname = ("w1a", "w1g")[side] + f"#{par}"
                srcap = bass.AP(W1.tensor, l * (D * 5460) + coff + u0,
                                [[5460, 128], [128 * 5460, NC_], [1, usz]])
                eng, ch = p.hch()
                p.emit(eng, lambda e, srcap=srcap, usz=usz, ws0=ws0: e.dma_start(
                    out=bass.AP(w1st.t, ws0, [[2048, 128], [128, NC_], [1, usz]]),
                    in_=srcap), reads=[f"w1st#{par}"], writes=[f"w1st#{par}"],
                    ch=ch)
                p.emit("dve", lambda e, wbuf=wbuf, ws0=ws0: e.tensor_copy(
                    wbuf.t[:, ws0:ws0 + 1024], w1st.t[:, ws0:ws0 + 1024]),
                    reads=[f"w1st#{par}"], writes=[wname])
            p.emit("sp", lambda e, u0=u0, usz=usz, par=par: e.dma_start(
                out=b1c[0:usz, 2 * par:2 * par + 1],
                in_=B1[l, u0:u0 + usz].unsqueeze(1)),
                reads=[f"b1c#{par}"], writes=[f"b1c#{par}"], ch="qsp")
            p.emit("sp", lambda e, u0=u0, usz=usz, par=par: e.dma_start(
                out=b1c[0:usz, 2 * par + 1:2 * par + 2],
                in_=B1[l, HID + u0:HID + u0 + usz].unsqueeze(1)),
                reads=[f"b1c#{par}"], writes=[f"b1c#{par}"], ch="qsp")
            pb0 = par * 4
            for side in (0, 1):
                wbuf = (w1a, w1g)[side]
                wname = ("w1a", "w1g")[side] + f"#{par}"
                bb0 = pb0 + side * 2
                for tb in range(2):
                    members = []
                    for ic in range(NC_):
                        lhs = wbuf.t[:, ws0 + ic * 128: ws0 + ic * 128 + usz]
                        rhs = lnhT.t[:, ic * S + tbase * 128 + tb * 512:
                                     ic * S + tbase * 128 + (tb + 1) * 512]
                        members.append((lambda e, lhs=lhs, rhs=rhs, b=bb0 + tb,
                                        ic=ic, usz=usz:
                                        e.matmul(p.PS[0:usz, b * 512:(b + 1) * 512],
                                                 lhs, rhs, start=(ic == 0),
                                                 stop=(ic == NC_ - 1)),
                                        [wname, f"lnhT{l}#{ic}"]))
                    mk.mm_group(f"ps{bb0 + tb}", members)

        def emit_gate(uc):
            u0 = uc * 128
            usz = min(128, HID - u0)
            par = uc % 2
            pb0 = par * 4
            p.emit("act", lambda e, usz=usz, pb0=pb0, par=par: e.activation(
                gl[0:usz, par * 1024:(par + 1) * 1024],
                p.PS[0:usz, (pb0 + 2) * 512:(pb0 + 4) * 512], AF.Gelu,
                bias=b1c[0:usz, 2 * par + 1:2 * par + 2]),
                reads=[f"ps{pb0 + 2}", f"ps{pb0 + 3}", f"b1c#{par}"],
                writes=[f"glscr#{par}"])
            dstap = gatedT.t[0:usz, uc * 1024:(uc + 1) * 1024]
            p.emit("dve", lambda e, dstap=dstap, usz=usz, pb0=pb0, par=par:
                   e.scalar_tensor_tensor(
                       dstap, p.PS[0:usz, pb0 * 512:(pb0 + 2) * 512],
                       b1c[0:usz, 2 * par:2 * par + 1],
                       gl[0:usz, par * 1024:(par + 1) * 1024],
                       ALU.add, ALU.mult),
                   reads=[f"ps{pb0}", f"ps{pb0 + 1}", f"b1c#{par}",
                          f"glscr#{par}"], writes=[f"gT{l}#{uc}"])

        for uc in range(NU):
            emit_h1(uc)
            if uc > 0:
                emit_gate(uc - 1)
        emit_gate(NU - 1)
        out_proj_residual(
            mk, gatedT, w2b, f"gT{l}", f"w2b{l}", f"l1f_s{l}f",
            post_eps=None, bias_tab=b2tab,
            nchunks=NU, chunk_k=lambda ic: min(128, HID - ic * 128),
            tiles=range(tbase, tbase + 8), ont_span=1024,
            dst_spill=f"l1f_x{l}fd")
    p.freebuf(gatedT); p.freebuf(w1st); p.freebuf(w1a); p.freebuf(w1g)
    p.freebuf(b1c); p.freebuf(gl); p.freebuf(w2b); p.freebuf(b2tab)
    p.freebuf(lnhT)
    last = (l == n_sub - 1)
    mk.realloc_x()
    ln_pend = None
    for i in range(NT):
        mk.unspill_x_tile(f"l1f_x{l}fd", i)
        if last:
            xb = mk.xtra if i % 2 == 0 else mk.xtra_b
            xbk = "xtra" if i % 2 == 0 else "xtra_b"
            sp = mk.spill["l1f_in"]
            eng, ch = p.hch()
            p.emit(eng, lambda e, i=i, sp=sp, xb=xb: e.dma_start(
                out=xb.ap, in_=sp[i * 128:(i + 1) * 128, :]),
                reads=["sp_l1f_in#%d" % i], writes=[xbk], ch=ch)
            xap = mk.X[:, i * D:(i + 1) * D]
            p.emit("dve", lambda e, xap=xap, xb=xb: e.tensor_add(xap, xap, xb.ap),
                   reads=[f"X#{i}", xbk], writes=[f"X#{i}"])
            fa, fb = mk.ln_tile_stages(i, 1e-5)
            fa()
            if ln_pend is not None:
                ln_pend()
            ln_pend = fb
    if ln_pend is not None:
        ln_pend()


def block_l1_ffn(mk):
    p = mk.p
    if mk.rA is not None:
        p.freebuf(mk.rA); p.freebuf(mk.rB); p.freebuf(mk.rC)
        mk.rA = mk.rB = mk.rC = None
    mk.xtra = p.alloc("xtra", [128, D], F32, side="high")
    mk.xtra2 = p.alloc("xtra2", [128, D], F32, side="high")
    mk.xtra_b = p.alloc("xtra_b", [128, D], F32, side="high")
    mk.xtra2_b = p.alloc("xtra2_b", [128, D], F32, side="high")
    for l in range(2):
        l1f_sub_attn(mk, l, 2)
        l1f_sub_ffn(mk, l, 2)


def build_nc(upto="all", n_cores=8):
    nc = bass.Bass("TRN2", target_bir_lowering=False, debug=False)
    names_2d = {
        "L0_attn_Wq": (D, D), "L0_attn_Wk": (D, D), "L0_attn_Wv": (D, D),
        "L0_attn_Wo": (D, D), "L0_ffn_weight": (D, D),
        "L1_attn_wq_sem": (D, 512), "L1_attn_wk_sem": (D, 512),
        "L1_attn_wq_geo": (D, 512), "L1_attn_wk_geo": (D, 512),
        "L1_attn_wv": (D, D), "L1_attn_wo": (D, D),
    }
    io = {}
    io["x"] = nc.dram_tensor("x", [S, D], F32, kind="ExternalInput").ap()
    for n, shp in names_2d.items():
        io[n] = nc.dram_tensor(n, list(shp), F32, kind="ExternalInput").ap()
    for n, shp in {
        "L0_ffn_ln_gamma": (D,), "L0_ffn_ln_beta": (D,),
        "L1_attn_k_sem_null": (512,), "L1_attn_k_geo_null": (512,),
        "L1_attn_v_null": (D,),
    }.items():
        io[n] = nc.dram_tensor(n, list(shp), F32, kind="ExternalInput").ap()
    for n, shp in {
        "L1_ffn_wqkv": (2, D, 1536), "L1_ffn_q_scale": (2, 64),
        "L1_ffn_k_scale": (2, 64), "L1_ffn_wo": (2, 512, D),
        "L1_ffn_ln_g": (2, D), "L1_ffn_ln_b": (2, D),
        "L1_ffn_w1": (2, D, 5460), "L1_ffn_b1": (2, 5460),
        "L1_ffn_w2": (2, 2730, D), "L1_ffn_b2": (2, D),
    }.items():
        io[n] = nc.dram_tensor(n, list(shp), F32, kind="ExternalInput").ap()
    io["out"] = nc.dram_tensor("out", [S, D], F32, kind="ExternalOutput").ap()

    mk = MK(nc, io)
    X0 = mk.X
    for i in range(NT):
        mk.p.emit("sp", lambda e, i=i, X0=X0: e.dma_start(
            out=X0[:, i * D:(i + 1) * D], in_=io["x"][i * 128:(i + 1) * 128, :]),
            writes=[f"X#{i}"], ch="qsp")
    block_l0_attn(mk)
    if upto == "l0a":
        mk.write_out()
        mk.finalize()
        return nc, mk
    block_l0_ffn(mk, n_cores=n_cores)
    if upto == "l0f":
        mk.write_out()
        mk.finalize()
        return nc, mk
    block_l1_attn(mk)
    if upto == "l1a":
        mk.write_out()
        mk.finalize()
        return nc, mk
    block_l1_ffn(mk)
    mk.write_out()
    mk.finalize()
    return nc, mk


def kernel(**inputs):
    n_cores = 8
    nc, mk = build_nc(upto="all", n_cores=n_cores)
    x_full = np.ascontiguousarray(inputs["L0_attn_x"], dtype=np.float32)
    weight_names = [
        "L0_attn_Wq", "L0_attn_Wk", "L0_attn_Wv", "L0_attn_Wo",
        "L0_ffn_weight", "L0_ffn_ln_gamma", "L0_ffn_ln_beta",
        "L1_attn_wq_sem", "L1_attn_wk_sem", "L1_attn_wq_geo", "L1_attn_wk_geo",
        "L1_attn_wv", "L1_attn_wo", "L1_attn_k_sem_null", "L1_attn_k_geo_null",
        "L1_attn_v_null", "L1_ffn_wqkv", "L1_ffn_q_scale", "L1_ffn_k_scale",
        "L1_ffn_wo", "L1_ffn_ln_g", "L1_ffn_ln_b", "L1_ffn_w1", "L1_ffn_b1",
        "L1_ffn_w2", "L1_ffn_b2",
    ]
    wmap = {n: np.ascontiguousarray(inputs[n], dtype=np.float32)
            for n in weight_names}
    in_maps = []
    for c in range(n_cores):
        m = {"x": np.ascontiguousarray(x_full[c])}
        m.update(wmap)
        in_maps.append(m)
    res = run_bass_kernel_spmd(nc, in_maps, core_ids=list(range(n_cores)))
    out = np.stack([np.asarray(r["out"], dtype=np.float32)
                    for r in res.results], axis=0)
    return out
